# version 01
# baseline (speedup 1.0000x reference)
import numpy as np
import ml_dtypes
from contextlib import ExitStack
import concourse.bass as bass
import concourse.mybir as mybir
from concourse.bass_utils import run_bass_kernel_spmd

F32 = mybir.dt.float32
BF16 = mybir.dt.bfloat16
I32 = mybir.dt.int32
AF = mybir.ActivationFunctionType
ALU = mybir.AluOpType
AX = mybir.AxisListType.X
BF = ml_dtypes.bfloat16

NCORES = 8
T = 2048
D = 1024

ENGINES = ("sync", "gpsimd", "tensor", "vector", "scalar")


class Emitter:

    def __init__(self, nc, ctx):
        self.nc = nc
        self.ctx = ctx
        self.ops = {e: [] for e in ENGINES}
        self.sem = {e: ctx.enter_context(nc.semaphore("S" + e)) for e in ENGINES}
        self.cnt = {e: 0 for e in ENGINES}
        self.lastw = {e: {} for e in ENGINES}
        self.floor_ = {e: 0 for e in ENGINES}
        self.dsems = {}

    def dsem(self, name):
        key = "D:" + name
        if key not in self.dsems:
            self.dsems[key] = self.ctx.enter_context(self.nc.semaphore(key))
            self.cnt[key] = 0
            self.lastw[key] = {}
        return key

    def tok(self, e):
        return (e, self.cnt[e])

    def floor(self):
        for k in list(self.cnt.keys()):
            self.floor_[k] = self.cnt[k]

    def _sem_of(self, key):
        return self.dsems[key] if key.startswith("D:") else self.sem[key]

    def op(self, e, fn, deps=(), inc=1, nofloor=False, semkey=None):
        alld = [d for d in deps if d is not None]
        if not nofloor:
            for de, fv in self.floor_.items():
                if fv > 0:
                    alld.append((de, fv))
        agg = {}
        for (de, dv) in alld:
            if dv <= 0:
                continue
            if de == e and e == "tensor":
                continue
            agg[de] = max(agg.get(de, 0), dv)
        waits = []
        lw = self.lastw[e]
        for de, dv in agg.items():
            if lw.get(de, 0) < dv:
                lw[de] = dv
                waits.append((self._sem_of(de), dv))
        key = semkey if semkey is not None else e
        self.cnt[key] += inc
        self.ops[e].append((waits, fn, self._sem_of(key), inc))
        return (key, self.cnt[key])

    def seq(self, e, fns, deps=()):
        t = None
        for i, fn in enumerate(fns):
            t = self.op(e, fn, deps=(list(deps) if i == 0 else [t]))
        return t

    def dma(self, q, out, in_, deps=(), nofloor=True, sem="misc"):
        return self.op(q, (lambda o, i: (lambda eng: eng.dma_start(out=o, in_=i)))(out, in_),
                       deps=deps, inc=16, nofloor=nofloor, semkey=self.dsem(sem))

    def run(self, blk):
        def mk(e):
            def body(eng):
                for (waits, fn, mysem, inc) in self.ops[e]:
                    for (s, v) in waits:
                        eng.wait_ge(s, v)
                    ins = fn(eng)
                    if inc > 0:
                        ins.then_inc(mysem, inc)
            return body
        blk.sync(mk("sync"))
        blk.gpsimd(mk("gpsimd"))
        blk.tensor(mk("tensor"))
        blk.vector(mk("vector"))
        blk.scalar(mk("scalar"))


class Banks:

    def __init__(self, em, pts):
        self.em = em
        self.pt = pts
        self.w = [None] * 8
        self.r = [{} for _ in range(8)]

    def ap(self, b, c0=0, c1=512):
        return self.pt[b // 2][:, (b % 2) * 512 + c0:(b % 2) * 512 + c1]

    def pair(self, i):
        return self.pt[i][:]

    def wdeps(self, banks):
        d = []
        for b in banks:
            d += list(self.r[b].items())
            if self.w[b] is not None:
                d.append(self.w[b])
        return d

    def rdeps(self, banks):
        return [self.w[b] for b in banks if self.w[b] is not None]

    def setw(self, banks, tokv):
        for b in banks:
            self.w[b] = tokv
            self.r[b] = {}

    def setr(self, banks, tokv):
        for b in banks:
            self.r[b][tokv[0]] = max(self.r[b].get(tokv[0], 0), tokv[1])


def mm_group(em, bk, banks, mms, deps=()):
    alld = list(deps) + bk.wdeps(banks)

    def fn(eng, mms=mms):
        ins = None
        for (o, l, r, st, sp) in mms:
            ins = eng.matmul(o, l, r, start=st, stop=sp)
        return ins
    t = em.op("tensor", fn, deps=alld, inc=1)
    bk.setw(banks, t)
    return t


def tr_group(em, bk, banks, trs, deps=()):
    alld = list(deps) + bk.wdeps(banks)

    def fn(eng, trs=trs):
        ins = None
        for (o, i, idn) in trs:
            ins = eng.transpose(o, i, idn)
        return ins
    t = em.op("tensor", fn, deps=alld, inc=1)
    bk.setw(banks, t)
    return t


def ev(em, e, bk, banks, fn, deps=()):
    t = em.op(e, fn, deps=list(deps) + bk.rdeps(banks), inc=1)
    bk.setr(banks, t)
    return t


def ldw_fm(em, sb_ap, dram_ap, deps=(), sem="wmisc"):
    K_, M_ = dram_ap.shape
    c = K_ // 128
    src = dram_ap.rearrange("(c p) m -> p c m", p=128)
    dst = sb_ap.rearrange("p (c m) -> p c m", c=c)
    return em.dma("gpsimd", dst, src, deps=deps, sem=sem)



def emit_ln(em, bk, B, x_dram, want_rxn):
    XF2, SQ = B["XF2"], B["SQ"]
    pt = bk.pt
    xf2_cons = [[], []]
    sq_cons = None
    for g in range(8):
        slot = g % 2
        xa = XF2[:, slot * T:(slot + 1) * T]
        sqa = SQ[:, 0:T]
        tl = em.dma("sync", xa, x_dram[g * 128:(g + 1) * 128, :], deps=xf2_cons[slot], sem="xf%d" % slot)
        tq = em.op("scalar", (lambda sqa, xa: lambda s: s.activation(sqa, xa, AF.Square))(sqa, xa),
                   deps=[tl, sq_cons])
        t1 = mm_group(em, bk, [0, 1, 2, 3],
                      [(bk.ap(c)[0:1, :], B["onescolf"], xa[:, c * 512:(c + 1) * 512], g == 0, g == 7)
                       for c in range(4)], deps=[tl])
        t2 = mm_group(em, bk, [4, 5, 6, 7],
                      [(bk.ap(4 + c)[0:1, :], B["onescolf"], sqa[:, c * 512:(c + 1) * 512], g == 0, g == 7)
                       for c in range(4)], deps=[tq])
        xf2_cons[slot] = [t1, tq]
        sq_cons = t2
    MR, VR, AR = B["MROW"], B["VROW"], B["AROW"]
    tmean = ev(em, "vector", bk, [0, 1, 2, 3], lambda v: (
        v.tensor_single_scalar(MR[:, 0:1024], pt[0][0:1, :], 1.0 / 1024.0, ALU.mult),
        v.tensor_single_scalar(MR[:, 1024:2048], pt[1][0:1, :], 1.0 / 1024.0, ALU.mult))[-1])
    tmsq = ev(em, "vector", bk, [4, 5, 6, 7], lambda v: (
        v.tensor_single_scalar(VR[:, 0:1024], pt[2][0:1, :], 1.0 / 1024.0, ALU.mult),
        v.tensor_single_scalar(VR[:, 1024:2048], pt[3][0:1, :], 1.0 / 1024.0, ALU.mult))[-1])
    tvar = em.seq("vector", [
        lambda v: v.tensor_tensor(AR, MR, MR, ALU.mult),
        lambda v: v.tensor_tensor(VR, VR, AR, ALU.subtract)], deps=[tmean, tmsq])
    talpha = em.seq("scalar", [
        lambda s: s.activation(AR, VR, AF.Ln, bias=B["CEPS"][0:1, 0:1]),
        lambda s: s.activation(AR, AR, AF.Exp, scale=-0.5)], deps=[tvar])
    tbeta = em.op("vector", lambda v: v.scalar_tensor_tensor(MR, MR, -1.0, AR, ALU.mult, ALU.mult),
                  deps=[talpha])
    trxn = None
    if want_rxn:
        t1 = em.seq("scalar", [
            lambda s: s.activation(VR, VR, AF.Ln),
            lambda s: s.activation(VR, VR, AF.Exp, scale=0.5)], deps=[tvar, talpha])
        trxn = em.seq("vector", [
            lambda v: v.tensor_tensor(VR, VR, AR, ALU.mult),
            lambda v: v.tensor_scalar(VR, VR, 32.0, 1e-12, ALU.mult, ALU.max),
            lambda v: v.reciprocal(VR, VR)], deps=[t1])
    ta = mm_group(em, bk, [0, 1, 2, 3],
                  [(bk.ap(c), B["ones1f"][0:1, :], AR[:, c * 512:(c + 1) * 512], True, True) for c in range(4)],
                  deps=[talpha])
    tb = mm_group(em, bk, [4, 5, 6, 7],
                  [(bk.ap(4 + c), B["ones1f"][0:1, :], MR[:, c * 512:(c + 1) * 512], True, True) for c in range(4)],
                  deps=[tbeta])
    tba = ev(em, "vector", bk, [0, 1, 2, 3], lambda v: (
        v.tensor_copy(B["BCA"][:, 0:1024], pt[0][:]),
        v.tensor_copy(B["BCA"][:, 1024:2048], pt[1][:]))[-1])
    tbb = ev(em, "vector", bk, [4, 5, 6, 7], lambda v: (
        v.tensor_copy(B["BCB"][:, 0:1024], pt[2][:]),
        v.tensor_copy(B["BCB"][:, 1024:2048], pt[3][:]))[-1])
    for g in range(8):
        slot = g % 2
        xa = XF2[:, slot * T:(slot + 1) * T]
        tl = em.dma("sync", xa, x_dram[g * 128:(g + 1) * 128, :],
                    deps=(xf2_cons[slot] or []) + [sq_cons], sem="xf%d" % slot)
        tv = em.seq("vector", [
            (lambda xa: lambda v: v.tensor_tensor(xa, xa, B["BCA"], ALU.mult))(xa),
            (lambda xa, g: lambda v: v.tensor_tensor(B["XNB"][:, g * T:(g + 1) * T], xa, B["BCB"], ALU.add))(xa, g)],
            deps=[tl, tba, tbb])
        xf2_cons[slot] = [tv]
    em.floor()
    return trxn



def emit_l0_routing(em, bk, B, W, rxn_tok):
    pt = bk.pt
    R6 = B["gsc1"][0:1, 8:16]
    RXN = B["VROW"]
    LG, LTM, GTM, GFM = B["LG"], B["LTM"], B["GTM"], B["GFM"]
    tcn = ldw_fm(em, B["CNB"], W["l0_centT"], sem="wcn")
    trt = ldw_fm(em, B["RTB"], W["l0_routeT"], sem="wrt")
    tprev = None
    for g in range(8):
        tq = em.op("scalar", (lambda g: lambda s: s.activation(B["SQ8"], B["CNB"][:, g * 8:(g + 1) * 8], AF.Square))(g),
                   deps=[tcn, tprev])
        tprev = mm_group(em, bk, [0], [(bk.ap(0, 0, 8)[0:1, :], B["onescolf"], B["SQ8"], g == 0, g == 7)],
                         deps=[tq])
    trn = ev(em, "vector", bk, [0], lambda v: v.tensor_copy(R6, pt[0][0:1, 0:8]))
    trn = em.seq("scalar", [
        lambda s: s.activation(R6, R6, AF.Ln),
        lambda s: s.activation(R6, R6, AF.Exp, scale=-0.5)], deps=[trn])
    trn = em.op("vector", lambda v: v.tensor_scalar_min(R6, R6, 1e12), deps=[trn])
    tbc = mm_group(em, bk, [0], [(bk.ap(0, 0, 8), B["ones1f"][0:1, :], R6, True, True)], deps=[trn])
    tnrm = None
    for g in range(8):
        tnrm = ev(em, "vector", bk, [0], (lambda g: lambda v: v.tensor_tensor(
            B["CNB"][:, g * 8:(g + 1) * 8], B["CNB"][:, g * 8:(g + 1) * 8], bk.ap(0, 0, 8), ALU.mult))(g),
            deps=[tcn])
    for c in range(4):
        mm_group(em, bk, [c], [(bk.ap(c)[0:8, :], B["CNB"][:, g * 8:(g + 1) * 8],
                                B["XNB"][:, g * T + c * 512:g * T + (c + 1) * 512], g == 0, g == 7)
                               for g in range(8)], deps=[tnrm])
        mm_group(em, bk, [4 + c], [(bk.ap(4 + c)[0:8, :], B["RTB"][:, g * 8:(g + 1) * 8],
                                    B["XNB"][:, g * T + c * 512:g * T + (c + 1) * 512], g == 0, g == 7)
                                   for g in range(8)], deps=[trt])
    tcos = ev(em, "vector", bk, [0, 1, 2, 3], lambda v: (
        v.tensor_copy(LG[0:8, 0:1024], pt[0][0:8, :]),
        v.tensor_copy(LG[0:8, 1024:2048], pt[1][0:8, :]))[-1])
    trb = mm_group(em, bk, [0, 1, 2, 3],
                   [(bk.ap(c)[0:8, :], B["ones1f"][0:1, 0:8], RXN[0:1, c * 512:(c + 1) * 512], True, True)
                    for c in range(4)], deps=[rxn_tok, tcos])
    t1 = ev(em, "vector", bk, [0, 1, 2, 3], lambda v: (
        v.tensor_tensor(LG[0:8, 0:1024], LG[0:8, 0:1024], pt[0][0:8, :], ALU.mult),
        v.tensor_tensor(LG[0:8, 1024:2048], LG[0:8, 1024:2048], pt[1][0:8, :], ALU.mult))[-1], deps=[tcos])
    t2 = ev(em, "vector", bk, [4, 5, 6, 7], lambda v: (
        v.tensor_tensor(LG[0:8, 0:1024], LG[0:8, 0:1024], pt[2][0:8, :], ALU.add),
        v.tensor_tensor(LG[0:8, 1024:2048], LG[0:8, 1024:2048], pt[3][0:8, :], ALU.add))[-1], deps=[t1])
    for t in range(16):
        b = t % 4
        tr_group(em, bk, [b], [(bk.ap(b, 0, 8), LG[0:8, t * 128:(t + 1) * 128], B["idf"][0:8, 0:8])], deps=[t2])
        ev(em, "vector", bk, [b], (lambda t, b: lambda v: v.tensor_copy(LTM[:, t * 8:(t + 1) * 8], bk.ap(b, 0, 8)))(t, b))
    em.floor()
    C1, C2, C3 = B["gcol1"], B["gcol2"], B["gcol3"]
    ta = None
    for t in range(16):
        lt = LTM[:, t * 8:(t + 1) * 8]
        gt = GTM[:, t * 8:(t + 1) * 8]
        sc1 = B["gsc1"][:, t * 8:(t + 1) * 8]
        tv = em.seq("vector", [
            (lambda lt: lambda v: v.reduce_max(C1, lt, AX))(lt),
            (lambda lt, sc1: lambda v: v.tensor_scalar(sc1, lt, C1, None, ALU.is_ge))(lt, sc1),
            (lambda lt, sc1: lambda v: v.scalar_tensor_tensor(sc1, sc1, -1e30, lt, ALU.mult, ALU.add))(lt, sc1),
            (lambda sc1: lambda v: v.reduce_max(C2, sc1, AX))(sc1),
            lambda v: v.tensor_scalar(C1, C1, -1.0, None, ALU.mult)], deps=[ta])
        ta = em.op("scalar", (lambda lt, sc1: lambda s: s.activation(sc1, lt, AF.Exp, bias=C1))(lt, sc1), deps=[tv])
        em.seq("vector", [
            (lambda lt, gt: lambda v: v.tensor_scalar(gt, lt, C2, None, ALU.is_ge))(lt, gt),
            (lambda gt, sc1: lambda v: v.tensor_tensor(sc1, sc1, gt, ALU.mult))(gt, sc1),
            (lambda sc1: lambda v: v.reduce_sum(C3, sc1, AX))(sc1),
            lambda v: v.reciprocal(C3, C3),
            (lambda gt, sc1: lambda v: v.tensor_scalar(gt, sc1, C3, None, ALU.mult))(gt, sc1)], deps=[ta])
    em.floor()
    for t in range(16):
        b = t % 4
        pb = bk.pair(b // 2).bitcast(BF16)
        out = pb[0:128, (b % 2) * 1024:(b % 2) * 1024 + 8][0:8, :] if False else None
        oap = bk.pt[b // 2].bitcast(BF16)[0:8, (b % 2) * 1024:(b % 2) * 1024 + 128]
        tr_group(em, bk, [b], [(oap, GTM[:, t * 8:(t + 1) * 8], B["idb"])])
        ev(em, "vector", bk, [b], (lambda t, oap: lambda v: v.tensor_copy(
            GFM[0:8, t * 128:(t + 1) * 128], oap))(t, oap))
    em.floor()
    for e in range(8):
        em.dma("sync", B["GROW"][e][0:1, :], GFM[e:e + 1, :], nofloor=False, sem="grow")
    em.floor()


def emit_l0_attn(em, bk, B, W, dump=None):
    pt = bk.pt
    XNB, S, T1B, BIG = B["XNB"], B["S"], B["T1B"], B["BIG"]
    QT = BIG[:, 0:2048]
    KT = BIG[:, 2048:4096]
    ATT = [BIG[:, 4096:6144], BIG[:, 6144:8192]]
    VTM = BIG[:, 12288:12288 + 16 * 130]
    VTM4 = VTM.rearrange("p (t c w) -> p t c w", c=2, w=65)
    YN = S.bitcast(BF16)[:, 0:16384]
    twv = ldw_fm(em, B["WVT"], W["l0_wvT"], sem="wvt")
    for m2 in range(2):
        for c in range(4):
            b = (m2 * 4 + c) % 4
            mm_group(em, bk, [b], [(bk.ap(b), B["WVT"][:, g * 256 + m2 * 128:g * 256 + m2 * 128 + 128],
                                    XNB[:, g * T + c * 512:g * T + (c + 1) * 512], g == 0, g == 7)
                                   for g in range(8)], deps=[twv])
            ev(em, "vector", bk, [b], (lambda m2, c, b: lambda v: v.tensor_copy(
                T1B[:, m2 * 2048 + c * 512:m2 * 2048 + (c + 1) * 512], bk.ap(b)))(m2, c, b))
    em.floor()
    if dump:
        dump("xnbA", B["XNB"])
    tvones = em.op("vector", lambda v: v.memset(VTM4[:, :, :, 64:65], 1.0))
    wus_cons = [None, None]
    tdpk = None
    for c in range(8):
        slot = c % 2
        wq = B["WUS"][:, slot * 768:slot * 768 + 256]
        wk = B["WUS"][:, slot * 768 + 256:slot * 768 + 512]
        wv = B["WUS"][:, slot * 768 + 512:slot * 768 + 768]
        dep = wus_cons[slot] if wus_cons[slot] is not None else em.tok("tensor")
        tq = em.dma("gpsimd", wq.rearrange("p (k m) -> p k m", k=2),
                    W["l0_wuT"][:, 128 * c:128 * (c + 1)].rearrange("(k p) m -> p k m", p=128), deps=[dep],
                    sem="wus%d" % slot)
        tk = em.dma("gpsimd", wk.rearrange("p (k m) -> p k m", k=2),
                    W["l0_wuT"][:, 1024 + 128 * c:1024 + 128 * (c + 1)].rearrange("(k p) m -> p k m", p=128), deps=[dep],
                    sem="wus%d" % slot)
        tv_ = em.dma("gpsimd", wv.rearrange("p (k m) -> p k m", k=2),
                     W["l0_wuT"][:, 2048 + 128 * c:2048 + 128 * (c + 1)].rearrange("(k p) m -> p k m", p=128), deps=[dep],
                     sem="wus%d" % slot)
        for (wqk, dst, tl) in ((wq, QT, tv_), (wk, KT, tv_)):
            for q in range(4):
                b = q % 2
                mm_group(em, bk, [b], [(bk.ap(b), wqk[:, kc * 128:(kc + 1) * 128],
                                        T1B[:, kc * 2048 + q * 512:kc * 2048 + (q + 1) * 512], kc == 0, kc == 1)
                                       for kc in range(2)], deps=[tl])
                ev(em, "vector", bk, [b], (lambda dst, q, b: lambda v: v.tensor_copy(
                    dst[:, q * 512:(q + 1) * 512], bk.ap(b)))(dst, q, b))
        for tt in range(16):
            b = 2 + tt % 2
            mm_group(em, bk, [b], [(bk.ap(b, 0, 128), T1B[:, kc * 2048 + tt * 128:kc * 2048 + (tt + 1) * 128],
                                    wv[:, kc * 128:(kc + 1) * 128], kc == 0, kc == 1)
                                   for kc in range(2)], deps=[tv_, tvones])
            ev(em, "vector", bk, [b], (lambda tt, b: lambda v: v.tensor_copy(
                VTM4[:, tt, :, 0:64], bk.ap(b, 0, 128).rearrange("p (c w) -> p c w", c=2)))(tt, b))
        wus_cons[slot] = em.tok("tensor")
        em.floor()
        for hp in range(2):
            h = 2 * c + hp
            att_cons = [None, None]
            for kt in range(16):
                aslot = kt % 2
                texp = []
                for half in range(2):
                    mm_group(em, bk, [0, 1],
                             [(bk.ap(qq), KT[64 * hp:64 * hp + 64, kt * 128:(kt + 1) * 128],
                               QT[64 * hp:64 * hp + 64, (half * 2 + qq) * 512:(half * 2 + qq + 1) * 512], True, True)
                              for qq in range(2)])
                    texp.append(ev(em, "scalar", bk, [0, 1], (lambda aslot, half: lambda s: s.activation(
                        ATT[aslot][:, half * 1024:(half + 1) * 1024], pt[0][:], AF.Exp, scale=0.125))(aslot, half),
                        deps=[att_cons[aslot]]))
                mm_group(em, bk, [4, 5, 6, 7],
                         [(bk.ap(4 + qq)[0:65, :], VTM[:, kt * 130 + hp * 65:kt * 130 + hp * 65 + 65],
                           ATT[aslot][:, qq * 512:(qq + 1) * 512], kt == 0, kt == 15) for qq in range(4)],
                         deps=texp)
                att_cons[aslot] = em.tok("tensor")
            tdr = ev(em, "vector", bk, [4, 5, 6, 7], lambda v: (
                v.tensor_copy(B["RROW"][0:1, 0:1024], pt[2][64:65, :]),
                v.tensor_copy(B["RROW"][0:1, 1024:2048], pt[3][64:65, :]))[-1], deps=[tdpk])
            tdpk = em.dma("sync", B["DPACK"][h:h + 1, :], B["RROW"][0:1, :], deps=[tdr], nofloor=False, sem="dpk")
            ev(em, "vector", bk, [4, 5, 6, 7], (lambda c, hp: lambda v: (
                v.tensor_copy(YN[64 * hp:64 * hp + 64, c * 2048:c * 2048 + 1024], pt[2][0:64, :]),
                v.tensor_copy(YN[64 * hp:64 * hp + 64, c * 2048 + 1024:c * 2048 + 2048], pt[3][0:64, :]))[-1])(c, hp))
            em.floor()
    if dump:
        dump("xnbB", B["XNB"])
    trc = em.op("vector", lambda v: v.reciprocal(B["DPACK"], B["DPACK"]))
    for c in range(8):
        for q in range(4):
            b = q % 4
            mm_group(em, bk, [b], [(bk.ap(b), B["E16"][:, 128 * c:128 * (c + 1)],
                                    B["DPACK"][:, q * 512:(q + 1) * 512], True, True)], deps=[trc])
            ev(em, "vector", bk, [b], (lambda c, q, b: lambda v: v.tensor_tensor(
                YN[:, c * 2048 + q * 512:c * 2048 + (q + 1) * 512],
                YN[:, c * 2048 + q * 512:c * 2048 + (q + 1) * 512], bk.ap(b), ALU.mult))(c, q, b))
    em.floor()
    if dump:
        dump("xnbC", B["XNB"])
    twov = ldw_fm(em, B["WVT"], W["l0_wovT"], deps=[em.tok("tensor")], sem="wvt")
    for m2 in range(2):
        for c in range(4):
            b = (m2 * 4 + c) % 4
            mm_group(em, bk, [b], [(bk.ap(b), B["WVT"][:, g * 256 + m2 * 128:g * 256 + m2 * 128 + 128],
                                    YN[:, g * 2048 + c * 512:g * 2048 + (c + 1) * 512], g == 0, g == 7)
                                   for g in range(8)], deps=[twov])
            ev(em, "vector", bk, [b], (lambda m2, c, b: lambda v: v.tensor_copy(
                T1B[:, m2 * 2048 + c * 512:m2 * 2048 + (c + 1) * 512], bk.ap(b)))(m2, c, b))
    em.floor()
    twou = ldw_fm(em, B["WOU"], W["l0_wouT"], deps=[em.tok("tensor")], sem="wvt")
    for m in range(8):
        for c in range(4):
            b = (m * 4 + c) % 4
            mm_group(em, bk, [b], [(bk.ap(b), B["WOU"][:, kc * 1024 + m * 128:kc * 1024 + m * 128 + 128],
                                    T1B[:, kc * 2048 + c * 512:kc * 2048 + (c + 1) * 512], kc == 0, kc == 1)
                                   for kc in range(2)], deps=[twou])
            ev(em, "vector", bk, [b], (lambda m, c, b: lambda v: v.tensor_copy(
                S[:, m * 2048 + c * 512:m * 2048 + (c + 1) * 512], bk.ap(b)))(m, c, b))
    em.floor()


def emit_l0_moe_residual(em, bk, B, W, x_dram, xout_dram, dump=None):
    pt = bk.pt
    XNB, S, GH = B["XNB"], B["S"], B["BIG"]
    W1S, W2S = B["W1S"], B["W2S"]
    tb1 = em.dma("gpsimd", B["B1R"][0:1, :], W["l0_b1r"], sem="bia")
    tb2 = em.dma("gpsimd", B["B2"][0:8, :], W["l0_b2"], sem="bia2")
    w1_cons = [None, None]
    w2_cons = [None, None]
    xp_cons = [None, None]
    gev_cons = [None, None]
    for tp in range(4):
        for e in range(8):
            slot = e % 2
            tw1 = em.dma("gpsimd", W1S[slot].rearrange("p (k m) -> p k m", k=8),
                         W["l0_w1"][e].rearrange("(k p) m -> p k m", p=128),
                         deps=[w1_cons[slot], w2_cons[slot]], sem="w1s%d" % slot)
            gbase = (0, 32, 64)[e % 3]
            tgb = mm_group(em, bk, [4 + slot], [(bk.ap(4 + slot), B["ones1b"][gbase:gbase + 1, 0:128],
                                                 B["GROW"][e][0:1, tp * 512:(tp + 1) * 512], True, True)])
            for hc in range(4):
                b = hc % 2
                mms = [(bk.ap(b), W1S[slot][:, kc * 512 + hc * 128:kc * 512 + hc * 128 + 128],
                        XNB[:, kc * T + tp * 512:kc * T + (tp + 1) * 512], kc == 0, False) for kc in range(8)]
                mms.append((bk.ap(b), B["B1R"][0:1, e * 512 + hc * 128:e * 512 + hc * 128 + 128],
                            B["onesrowb"], False, True))
                mm_group(em, bk, [b], mms, deps=[tw1, tb1])
                tg = ev(em, "scalar", bk, [b], (lambda b: lambda s: s.activation(
                    B["GEV"][:, b * 512:(b + 1) * 512], bk.ap(b), AF.Gelu))(b), deps=[gev_cons[b]])
                gev_cons[b] = ev(em, "vector", bk, [4 + slot], (lambda e, hc, b, slot: lambda v: v.tensor_tensor(
                    GH[:, (e * 4 + hc) * 512:(e * 4 + hc + 1) * 512], B["GEV"][:, b * 512:(b + 1) * 512],
                    bk.ap(4 + slot), ALU.mult))(e, hc, b, slot), deps=[tg])
            w1_cons[slot] = em.tok("tensor")
            if getattr(em, "serial_moe", False):
                em.floor()
        em.floor()
        for e in range(8):
            slot = e % 2
            tw2 = em.dma("gpsimd", W2S[slot].rearrange("p (k m) -> p k m", k=4),
                         W["l0_w2"][e].rearrange("(k p) m -> p k m", p=128),
                         deps=[w2_cons[slot], w1_cons[slot]], sem="w2s%d" % slot)
            for dc in range(8):
                mms = [(bk.ap(dc), W2S[slot][:, jc * 1024 + dc * 128:jc * 1024 + dc * 128 + 128],
                        GH[:, (e * 4 + jc) * 512:(e * 4 + jc + 1) * 512], e == 0 and jc == 0, False)
                       for jc in range(4)]
                if e == 7:
                    mms.append((bk.ap(dc), B["B2"][0:8, dc * 128:(dc + 1) * 128],
                                B["GFM"][0:8, tp * 512:(tp + 1) * 512], False, True))
                mm_group(em, bk, [dc], mms, deps=[tw2, tb2])
            w2_cons[slot] = em.tok("tensor")
            if getattr(em, "serial_acc", False) or getattr(em, "serial_moe", False):
                em.floor()
        for dc in range(8):
            slot = dc % 2
            xp = B["XP"][:, slot * 512:(slot + 1) * 512]
            gv = B["GEV"][:, slot * 512:(slot + 1) * 512]
            tx = em.dma("sync", xp, x_dram[dc * 128:(dc + 1) * 128, tp * 512:(tp + 1) * 512], deps=[xp_cons[slot]],
                        sem="xp%d" % slot, nofloor=False)
            tsl = ev(em, "scalar", bk, [dc], (lambda dc, gv: lambda s: s.activation(gv, bk.ap(dc), AF.Silu))(dc, gv),
                     deps=[gev_cons[slot]])
            tv = em.seq("vector", [
                (lambda dc, gv, tp: lambda v: v.tensor_tensor(
                    gv, gv, S[:, dc * 2048 + tp * 512:dc * 2048 + (tp + 1) * 512], ALU.mult))(dc, gv, tp),
                (lambda xp, gv: lambda v: v.tensor_tensor(xp, xp, gv, ALU.add))(xp, gv)], deps=[tx, tsl])
            gev_cons[slot] = tv
            xp_cons[slot] = em.dma("sync", xout_dram[dc * 128:(dc + 1) * 128, tp * 512:(tp + 1) * 512], xp,
                                   deps=[tv], nofloor=False, sem="xo%d" % slot)
            if getattr(em, "serial_moe", False):
                em.floor()
        em.floor()





def emit_l1(em, bk, B, W, x_dram, xout_dram):
    pt = bk.pt
    XNB, S, BIG, SA, SB2 = B["XNB"], B["S"], B["BIG"], B["SMALLA"], B["SMALLB"]
    MR, VR, AR = B["MROW"], B["VROW"], B["AROW"]
    XBB = BIG
    BCR = SB2.bitcast(F32)[:, 0:T]
    WSL = SB2[:, 4096:10240]
    XPRE = SA[:, 0:T]
    ACC = SA[:, T:2 * T]
    RS = SA[:, 0:T]
    IS = SA[:, T:2 * T]
    AS = SA[:, 2 * T:3 * T]
    HYB = [B["W1S"][0][:, 0:2048], B["W1S"][0][:, 2048:4096],
           B["W1S"][1][:, 0:2048], B["W1S"][1][:, 2048:4096]]
    COLS = B["L1C"]
    SPL = B["gsc1"][:, 0:8]
    TM8 = B["gsc1"][:, 8:16]
    n1c = lambda k: COLS[:, k:k + 1]
    n2c = lambda k: COLS[:, 8 + k:9 + k]
    cbc = lambda g: COLS[:, 24 + g:25 + g]
    cwc = lambda g, k: COLS[:, 32 + g * 4 + k:33 + g * 4 + k]

    tcols = em.dma("sync", COLS, W["l1_cols"], sem="cols", nofloor=False)
    lam = COLS[:, 16:24]
    t1 = em.seq("scalar", [
        lambda s: s.activation(TM8, lam, AF.Abs),
        lambda s: s.activation(TM8, TM8, AF.Exp, scale=-1.0),
        lambda s: s.activation(TM8, TM8, AF.Ln, bias=B["CEPS"][:, 2:3])], deps=[tcols])
    tspl = em.seq("vector", [
        lambda v: v.tensor_scalar_max(SPL, lam, 0.0),
        lambda v: v.tensor_tensor(SPL, SPL, TM8, ALU.add),
        lambda v: v.tensor_scalar_mul(SPL, SPL, -8.0)], deps=[t1])
    t2 = em.seq("vector", [
        lambda v: v.tensor_tensor(MR, AR, AR, ALU.mult),
        lambda v: v.tensor_tensor(MR, MR, VR, ALU.mult)])
    t3 = em.seq("scalar", [
        lambda s: s.activation(MR, MR, AF.Ln, bias=B["CEPS"][0:1, 1:2]),
        lambda s: s.activation(MR, MR, AF.Exp, scale=-0.5)], deps=[t2])
    tb = mm_group(em, bk, [0, 1, 2, 3],
                  [(bk.ap(c), B["ones1f"][0:1, :], MR[:, c * 512:(c + 1) * 512], True, True) for c in range(4)],
                  deps=[t3])
    tbc = ev(em, "vector", bk, [0, 1, 2, 3], lambda v: (
        v.tensor_copy(BCR[:, 0:1024], pt[0][:]),
        v.tensor_copy(BCR[:, 1024:2048], pt[1][:]))[-1])
    em.floor()

    wsl_cons = [None, None]
    xpre_cons = None
    acc_cons = None
    for g in range(8):
        slot = g % 2
        wofs = 4096 + slot * 1024
        wx = SB2[:, wofs:wofs + 1024]
        tw = ldw_fm(em, wx, W["l1_wx"][:, 128 * g:128 * (g + 1)], deps=[wsl_cons[slot]], sem="wA%d" % slot)
        tsc = em.seq("vector", [(lambda wx, k: lambda v: v.tensor_scalar_mul(
            wx[:, k * 128:(k + 1) * 128], wx[:, k * 128:(k + 1) * 128], n1c(k)))(wx, k) for k in range(8)],
            deps=[tw, tcols])
        evs = []
        for c in range(4):
            b = c % 4
            mm_group(em, bk, [b], [(bk.ap(b), wx[:, k * 128:(k + 1) * 128],
                                    XNB[:, k * T + c * 512:k * T + (c + 1) * 512], k == 0, k == 7)
                                   for k in range(8)], deps=[tsc])
            evs.append(ev(em, "vector", bk, [b], (lambda c, b: lambda v: v.tensor_tensor(
                XPRE[:, c * 512:(c + 1) * 512], bk.ap(b), BCR[:, c * 512:(c + 1) * 512], ALU.mult))(c, b),
                deps=[tbc, xpre_cons]))
        wsl_cons[slot] = em.tok("tensor")
        tconv = em.seq("vector", [
            (lambda g: lambda v: v.tensor_scalar(ACC, XPRE, cwc(g, 3), cbc(g), ALU.mult, ALU.add))(g),
            (lambda g: lambda v: v.scalar_tensor_tensor(ACC[:, 1:T], XPRE[:, 0:T - 1], cwc(g, 2), ACC[:, 1:T],
                                                        ALU.mult, ALU.add))(g),
            (lambda g: lambda v: v.scalar_tensor_tensor(ACC[:, 2:T], XPRE[:, 0:T - 2], cwc(g, 1), ACC[:, 2:T],
                                                        ALU.mult, ALU.add))(g),
            (lambda g: lambda v: v.scalar_tensor_tensor(ACC[:, 3:T], XPRE[:, 0:T - 3], cwc(g, 0), ACC[:, 3:T],
                                                        ALU.mult, ALU.add))(g),
        ], deps=evs + [acc_cons, tcols])
        xpre_cons = tconv
        acc_cons = em.op("vector", (lambda g: lambda v: v.tensor_copy(BIG[:, g * T:(g + 1) * T], ACC))(g),
                         deps=[tconv])
    em.floor()

    prev_hyb = None
    for half in range(2):
        for gp in range(4 * half, 4 * half + 4):
            twi = ldw_fm(em, WSL[:, 0:1024], W["l1_wi"][:, 128 * gp:128 * (gp + 1)], deps=[em.tok("tensor")], sem="wB0")
            twa = ldw_fm(em, WSL[:, 1024:2048], W["l1_wa"][:, 128 * gp:128 * (gp + 1)], deps=[em.tok("tensor")], sem="wB1")
            twy = ldw_fm(em, WSL[:, 2048:3072], W["l1_wy"][:, 128 * gp:128 * (gp + 1)], deps=[em.tok("tensor")], sem="wB2")
            tscy = em.seq("vector", [(lambda k: lambda v: v.tensor_scalar_mul(
                WSL[:, 2048 + k * 128:2048 + (k + 1) * 128], WSL[:, 2048 + k * 128:2048 + (k + 1) * 128],
                n1c(k)))(k) for k in range(8)], deps=[twy, tcols])
            for c in range(4):
                mm_group(em, bk, [c], [(bk.ap(c), WSL[:, k * 128:(k + 1) * 128],
                                        BIG[:, k * T + c * 512:k * T + (c + 1) * 512], k == 0, k == 7)
                                       for k in range(8)], deps=[twi])
                mm_group(em, bk, [4 + c], [(bk.ap(4 + c), WSL[:, 1024 + k * 128:1024 + (k + 1) * 128],
                                            BIG[:, k * T + c * 512:k * T + (c + 1) * 512], k == 0, k == 7)
                                           for k in range(8)], deps=[twa])
            tr_ = ev(em, "scalar", bk, [0, 1, 2, 3], lambda s: (
                s.activation(RS[:, 0:1024], pt[0][:], AF.Sigmoid),
                s.activation(RS[:, 1024:2048], pt[1][:], AF.Sigmoid))[-1], deps=[prev_hyb])
            ti_ = ev(em, "scalar", bk, [4, 5, 6, 7], lambda s: (
                s.activation(IS[:, 0:1024], pt[2][:], AF.Sigmoid),
                s.activation(IS[:, 1024:2048], pt[3][:], AF.Sigmoid))[-1], deps=[tr_])
            ta_ = em.op("scalar", (lambda gp: lambda s: s.activation(AS, RS, AF.Exp, scale=SPL[:, gp:gp + 1]))(gp),
                        deps=[tr_, ti_, tspl])
            tsq = em.op("scalar", lambda s: s.activation(RS, AS, AF.Square), deps=[ta_])
            tb_ = em.seq("vector", [
                lambda v: v.tensor_scalar(RS, RS, -1.0, 1.0, ALU.mult, ALU.add),
                lambda v: v.tensor_scalar_max(RS, RS, 1e-12)], deps=[tsq])
            tsqr = em.op("scalar", lambda s: s.activation(RS, RS, AF.Sqrt), deps=[tb_])
            tbb2 = em.seq("vector", [
                (lambda gp: lambda v: v.tensor_tensor(IS, IS, BIG[:, gp * T:(gp + 1) * T], ALU.mult))(gp),
                lambda v: v.tensor_tensor(RS, RS, IS, ALU.mult),
                lambda v: v.tensor_tensor_scan(IS, AS, RS, 0.0, ALU.mult, ALU.add)], deps=[tsqr])
            tyb = []
            for c in range(4):
                b = c % 4
                mm_group(em, bk, [b], [(bk.ap(b), WSL[:, 2048 + k * 128:2048 + (k + 1) * 128],
                                        XNB[:, k * T + c * 512:k * T + (c + 1) * 512], k == 0, k == 7)
                                       for k in range(8)], deps=[tscy])
                tyb.append(ev(em, "vector", bk, [b], (lambda c, b: lambda v: v.tensor_tensor(
                    RS[:, c * 512:(c + 1) * 512], bk.ap(b), BCR[:, c * 512:(c + 1) * 512], ALU.mult))(c, b),
                    deps=[tbb2]))
            tg = em.op("scalar", lambda s: s.activation(RS, RS, AF.Gelu_apprx_tanh), deps=tyb)
            prev_hyb = em.op("vector", (lambda gp: lambda v: v.tensor_tensor(HYB[gp % 4], IS, RS, ALU.mult))(gp),
                             deps=[tg])
        em.floor()
        for m in range(8):
            tws = ldw_fm(em, WSL[:, 0:512],
                         W["l1_wo"][512 * half:512 * (half + 1), 128 * m:128 * (m + 1)],
                         deps=[em.tok("tensor")], sem="wB0")
            for c in range(4):
                b = c % 4
                mm_group(em, bk, [b], [(bk.ap(b), WSL[:, kc * 128:(kc + 1) * 128],
                                        HYB[kc][:, c * 512:(c + 1) * 512], kc == 0, kc == 3)
                                       for kc in range(4)], deps=[tws])
                if half == 0:
                    ev(em, "vector", bk, [b], (lambda m, c, b: lambda v: v.tensor_tensor(
                        S[:, m * T + c * 512:m * T + (c + 1) * 512],
                        XNB[:, m * T + c * 512:m * T + (c + 1) * 512], bk.ap(b), ALU.add))(m, c, b))
                else:
                    ev(em, "vector", bk, [b], (lambda m, c, b: lambda v: v.tensor_tensor(
                        S[:, m * T + c * 512:m * T + (c + 1) * 512],
                        S[:, m * T + c * 512:m * T + (c + 1) * 512], bk.ap(b), ALU.add))(m, c, b))
        em.floor()

    SQ2 = SA[:, 0:T]
    BCR2 = SA[:, T:2 * T]
    rowD = SA[0:1, 2 * T:3 * T]
    tq_prev = None
    for g in range(8):
        tq = em.op("scalar", (lambda g: lambda s: s.activation(SQ2, S[:, g * T:(g + 1) * T], AF.Square))(g),
                   deps=[tq_prev])
        tq_prev = mm_group(em, bk, [0, 1, 2, 3],
                           [(bk.ap(c)[0:1, :], B["onescolf"], SQ2[:, c * 512:(c + 1) * 512], g == 0, g == 7)
                            for c in range(4)], deps=[tq])
    trow = ev(em, "vector", bk, [0, 1, 2, 3], lambda v: (
        v.tensor_single_scalar(rowD[:, 0:1024], pt[0][0:1, :], 1.0 / 1024.0, ALU.mult),
        v.tensor_single_scalar(rowD[:, 1024:2048], pt[1][0:1, :], 1.0 / 1024.0, ALU.mult))[-1])
    trow = em.seq("scalar", [
        lambda s: s.activation(rowD, rowD, AF.Ln, bias=B["CEPS"][0:1, 1:2]),
        lambda s: s.activation(rowD, rowD, AF.Exp, scale=-0.5)], deps=[trow])
    tb2 = mm_group(em, bk, [0, 1, 2, 3],
                   [(bk.ap(c), B["ones1f"][0:1, :], rowD[:, c * 512:(c + 1) * 512], True, True) for c in range(4)],
                   deps=[trow])
    ev(em, "vector", bk, [0, 1, 2, 3], lambda v: (
        v.tensor_copy(BCR2[:, 0:1024], pt[0][:]),
        v.tensor_copy(BCR2[:, 1024:2048], pt[1][:]))[-1])
    em.floor()

    UK = B["W1S"][0]
    H2 = BIG
    for tp in range(4):
        wsl_cons = [em.tok("tensor"), em.tok("tensor")]
        tuk = em.seq("vector",
                     [(lambda k, tp: lambda v: v.tensor_tensor(
                         UK[:, k * 512:(k + 1) * 512], S[:, k * T + tp * 512:k * T + tp * 512 + 512],
                         BCR2[:, tp * 512:tp * 512 + 512], ALU.mult))(k, tp) for k in range(8)] +
                     [(lambda k: lambda v: v.tensor_scalar_mul(
                         UK[:, k * 512:(k + 1) * 512], UK[:, k * 512:(k + 1) * 512], n2c(k)))(k) for k in range(8)],
                     deps=[em.tok("tensor"), tcols])
        for j in range(32):
            slot = j % 2
            wofs = 4096 + slot * 2048
            twg = ldw_fm(em, SB2[:, wofs:wofs + 1024], W["l1_wg"][:, 128 * j:128 * (j + 1)],
                         deps=[wsl_cons[slot]], sem="wE%d" % slot)
            twu = ldw_fm(em, SB2[:, wofs + 1024:wofs + 2048], W["l1_wu"][:, 128 * j:128 * (j + 1)],
                         deps=[wsl_cons[slot]], sem="wE%d" % slot)
            mm_group(em, bk, [0], [(bk.ap(0), SB2[:, wofs + k * 128:wofs + (k + 1) * 128],
                                    UK[:, k * 512:(k + 1) * 512], k == 0, k == 7)
                                   for k in range(8)], deps=[twg, twu, tuk])
            mm_group(em, bk, [2], [(bk.ap(2), SB2[:, wofs + 1024 + k * 128:wofs + 1152 + k * 128],
                                    UK[:, k * 512:(k + 1) * 512], k == 0, k == 7)
                                   for k in range(8)], deps=[twg, twu, tuk])
            tgel = ev(em, "scalar", bk, [0], (lambda: lambda s: s.activation(
                B["GEV"][:, 0:512], bk.ap(0), AF.Gelu_apprx_tanh))(), deps=[em.tok("vector")])
            tup = ev(em, "vector", bk, [2], (lambda: lambda v: v.tensor_copy(
                B["GEV"].bitcast(BF16)[:, 1024:1536], bk.ap(2)))(), deps=[tgel])
            wsl_cons[slot] = ev(em, "vector", bk, [0, 1], (lambda j: lambda v: v.tensor_tensor(
                H2[:, j * 512:(j + 1) * 512], B["GEV"][:, 0:512], B["GEV"].bitcast(BF16)[:, 1024:1536],
                ALU.mult))(j), deps=[tgel, tup])
        em.floor()
        wd_cons = [em.tok("tensor"), em.tok("tensor")]
        for j in range(32):
            slot = j % 2
            wofs = 4096 + slot * 1024
            twd = ldw_fm(em, SB2[:, wofs:wofs + 1024], W["l1_wd"][128 * j:128 * (j + 1), :],
                         deps=[wd_cons[slot]], sem="wD%d" % slot)
            for m in range(8):
                mm_group(em, bk, [m], [(bk.ap(m), SB2[:, wofs + m * 128:wofs + (m + 1) * 128],
                                        H2[:, j * 512:(j + 1) * 512], j == 0, j == 31)], deps=[twd])
            wd_cons[slot] = em.tok("tensor")
        for m in range(8):
            ev(em, "vector", bk, [m], (lambda m, tp: lambda v: v.tensor_tensor(
                S[:, m * T + tp * 512:m * T + tp * 512 + 512],
                S[:, m * T + tp * 512:m * T + tp * 512 + 512], bk.ap(m), ALU.add))(m, tp))
        em.floor()

    H3 = BIG
    xp_cons = [None, None]
    gev_cons = None
    for tp in range(4):
        wsl_cons = [em.tok("tensor"), em.tok("tensor")]
        for j in range(24):
            slot = j % 2
            wofs = 4096 + slot * 2048
            twg = em.dma("gpsimd", SB2[:, wofs:wofs + 1024].rearrange("p (k m) -> p k m", k=8),
                         W["l1_wgT"][:, 128 * j:128 * (j + 1)].rearrange("(k p) m -> p k m", p=128),
                         deps=[wsl_cons[slot]], sem="wE%d" % slot)
            twu = em.dma("gpsimd", SB2[:, wofs + 1024:wofs + 2048].rearrange("p (k m) -> p k m", k=8),
                         W["l1_wuT"][:, 128 * j:128 * (j + 1)].rearrange("(k p) m -> p k m", p=128),
                         deps=[wsl_cons[slot]], sem="wE%d" % slot)
            mm_group(em, bk, [0], [(bk.ap(0), SB2[:, wofs + k * 128:wofs + (k + 1) * 128],
                                    XNB[:, k * T + tp * 512:k * T + (tp + 1) * 512], k == 0, k == 7)
                                   for k in range(8)], deps=[twg, twu])
            mm_group(em, bk, [2], [(bk.ap(2), SB2[:, wofs + 1024 + k * 128:wofs + 1152 + k * 128],
                                    XNB[:, k * T + tp * 512:k * T + (tp + 1) * 512], k == 0, k == 7)
                                   for k in range(8)], deps=[twg, twu])
            tsi = ev(em, "scalar", bk, [0], (lambda: lambda s: s.activation(
                B["GEV"][:, 0:512], bk.ap(0), AF.Silu, scale=1.0 / 127.0))(), deps=[em.tok("vector")])
            tup = ev(em, "vector", bk, [2], lambda v: v.tensor_single_scalar(
                B["GEV"].bitcast(BF16)[:, 1024:1536], bk.ap(2), 1.0 / 127.0, ALU.mult), deps=[tsi])
            wsl_cons[slot] = ev(em, "vector", bk, [0, 1], (lambda j: lambda v: v.tensor_tensor(
                H3[:, j * 512:(j + 1) * 512], B["GEV"][:, 0:512], B["GEV"].bitcast(BF16)[:, 1024:1536],
                ALU.mult))(j), deps=[tsi, tup])
        em.floor()
        wd_cons = [em.tok("tensor"), em.tok("tensor")]
        for j in range(24):
            slot = j % 2
            wofs = 4096 + slot * 1024
            twd = em.dma("gpsimd", SB2[:, wofs:wofs + 1024],
                         W["l1_wdT"][128 * j:128 * (j + 1), :],
                         deps=[wd_cons[slot]], sem="wD%d" % slot)
            for m in range(8):
                mm_group(em, bk, [m], [(bk.ap(m), SB2[:, wofs + m * 128:wofs + (m + 1) * 128],
                                        H3[:, j * 512:(j + 1) * 512], j == 0, j == 23)], deps=[twd])
            wd_cons[slot] = em.tok("tensor")
        for m in range(8):
            slot = m % 2
            xp = B["XP"][:, slot * 512:(slot + 1) * 512]
            gv = B["GEV"][:, slot * 512:(slot + 1) * 512]
            tx = em.dma("sync", xp, x_dram[m * 128:(m + 1) * 128, tp * 512:(tp + 1) * 512],
                        deps=[xp_cons[slot]], sem="xp%d" % slot, nofloor=False)
            tsl = ev(em, "scalar", bk, [m], (lambda m, gv: lambda s: s.activation(
                gv, bk.ap(m), AF.Silu, scale=1.0 / 127.0))(m, gv), deps=[gev_cons])
            tv = em.seq("vector", [
                (lambda m, gv, tp: lambda v: v.tensor_tensor(
                    gv, gv, S[:, m * T + tp * 512:m * T + (tp + 1) * 512], ALU.mult))(m, gv, tp),
                (lambda xp, gv: lambda v: v.tensor_tensor(xp, xp, gv, ALU.add))(xp, gv)], deps=[tx, tsl])
            gev_cons = tv
            xp_cons[slot] = em.dma("sync", xout_dram[m * 128:(m + 1) * 128, tp * 512:(tp + 1) * 512], xp,
                                   deps=[tv], nofloor=False, sem="xo%d" % slot)
        em.floor()




def emit_l2(em, bk, B, W, x_dram, xout_dram):
    pt = bk.pt
    XNB, S, BIG, SA, SB2 = B["XNB"], B["S"], B["BIG"], B["SMALLA"], B["SMALLB"]
    RS = SA[:, 0:T]
    IS = SA[:, T:2 * T]
    AS = SA[:, 2 * T:3 * T]
    COLS = B["L1C"]
    SPL = B["gsc1"][:, 0:8]
    TM8 = B["gsc1"][:, 8:16]
    bac = lambda g: COLS[:, g:g + 1]
    bxc = lambda g: COLS[:, 8 + g:9 + g]
    tcols = em.dma("sync", COLS[:, 0:24], W["l2_cols"], sem="cols", nofloor=False)
    lam = COLS[:, 16:24]
    t1 = em.seq("scalar", [
        lambda s: s.activation(TM8, lam, AF.Abs),
        lambda s: s.activation(TM8, TM8, AF.Exp, scale=-1.0),
        lambda s: s.activation(TM8, TM8, AF.Ln, bias=B["CEPS"][:, 2:3])], deps=[tcols])
    tspl = em.seq("vector", [
        lambda v: v.tensor_scalar(SPL, lam, -1.0, 0.0, ALU.mult, ALU.max),
        lambda v: v.tensor_tensor(SPL, SPL, TM8, ALU.add),
        lambda v: v.tensor_scalar_mul(SPL, SPL, -8.0)], deps=[t1])
    em.floor()

    prev_cons = None
    for g in range(8):
        twa = ldw_fm(em, SB2[:, 4096:5120], W["l2_waT"][:, 128 * g:128 * (g + 1)],
                     deps=[em.tok("tensor")], sem="wB0")
        twx = ldw_fm(em, SB2[:, 5120:6144], W["l2_wxT"][:, 128 * g:128 * (g + 1)],
                     deps=[em.tok("tensor")], sem="wB1")
        for c in range(4):
            mm_group(em, bk, [c], [(bk.ap(c), SB2[:, 4096 + k * 128:4096 + (k + 1) * 128],
                                    XNB[:, k * T + c * 512:k * T + (c + 1) * 512], k == 0, k == 7)
                                   for k in range(8)], deps=[twa])
            mm_group(em, bk, [4 + c], [(bk.ap(4 + c), SB2[:, 5120 + k * 128:5120 + (k + 1) * 128],
                                        XNB[:, k * T + c * 512:k * T + (c + 1) * 512], k == 0, k == 7)
                                       for k in range(8)], deps=[twx])
        tr_ = ev(em, "scalar", bk, [0, 1, 2, 3], (lambda g: lambda s: (
            s.activation(RS[:, 0:1024], pt[0][:], AF.Sigmoid, bias=bac(g)),
            s.activation(RS[:, 1024:2048], pt[1][:], AF.Sigmoid, bias=bac(g)))[-1])(g),
            deps=[prev_cons, tcols])
        ti_ = ev(em, "scalar", bk, [4, 5, 6, 7], (lambda g: lambda s: (
            s.activation(IS[:, 0:1024], pt[2][:], AF.Sigmoid, bias=bxc(g)),
            s.activation(IS[:, 1024:2048], pt[3][:], AF.Sigmoid, bias=bxc(g)))[-1])(g), deps=[tr_])
        ta_ = em.op("scalar", (lambda g: lambda s: s.activation(AS, RS, AF.Exp, scale=SPL[:, g:g + 1]))(g),
                    deps=[tr_, ti_, tspl])
        tsq = em.op("scalar", lambda s: s.activation(RS, AS, AF.Square), deps=[ta_])
        tb_ = em.op("vector", lambda v: v.tensor_scalar(RS, RS, -1.0, 1.0, ALU.mult, ALU.add), deps=[tsq])
        tsqr = em.op("scalar", lambda s: s.activation(RS, RS, AF.Sqrt), deps=[tb_])
        prev_cons = em.seq("vector", [
            (lambda g: lambda v: v.tensor_tensor(IS, IS, XNB[:, g * T:(g + 1) * T], ALU.mult))(g),
            lambda v: v.tensor_tensor(RS, RS, IS, ALU.mult),
            (lambda g: lambda v: v.tensor_tensor_scan(S[:, g * T:(g + 1) * T], AS, RS, 0.0, ALU.mult, ALU.add))(g),
        ], deps=[tsqr])
    em.floor()

    LG, LTM, GTM, GFM = B["LG"], B["LTM"], B["GTM"], B["GFM"]
    twg = ldw_fm(em, B["CNB"], W["l2_wg"], sem="wcn")
    tbg = em.dma("gpsimd", B["B1R"][0:1, 0:8], W["l2_bgr"], sem="bia")
    for c in range(4):
        mms = [(bk.ap(c)[0:8, :], B["CNB"][:, k * 8:(k + 1) * 8],
                XNB[:, k * T + c * 512:k * T + (c + 1) * 512], k == 0, False) for k in range(8)]
        mms.append((bk.ap(c)[0:8, :], B["B1R"][0:1, 0:8], B["onesrowb"], False, True))
        mm_group(em, bk, [c], mms, deps=[twg, tbg])
    tlg = ev(em, "vector", bk, [0, 1, 2, 3], lambda v: (
        v.tensor_copy(LG[0:8, 0:1024], pt[0][0:8, :]),
        v.tensor_copy(LG[0:8, 1024:2048], pt[1][0:8, :]))[-1])
    for t in range(16):
        b = t % 4
        tr_group(em, bk, [b], [(bk.ap(b, 0, 8), LG[0:8, t * 128:(t + 1) * 128], B["idf"][0:8, 0:8])], deps=[tlg])
        ev(em, "vector", bk, [b], (lambda t, b: lambda v: v.tensor_copy(LTM[:, t * 8:(t + 1) * 8], bk.ap(b, 0, 8)))(t, b))
    em.floor()
    C1, C2, C3 = B["gcol1"], B["gcol2"], B["gcol3"]
    ta = None
    for t in range(16):
        lt = LTM[:, t * 8:(t + 1) * 8]
        gt = GTM[:, t * 8:(t + 1) * 8]
        sc1 = B["gsc1"][:, t * 8:(t + 1) * 8]
        tv = em.seq("vector", [
            (lambda lt: lambda v: v.reduce_max(C1, lt, AX))(lt),
            (lambda lt, sc1: lambda v: v.tensor_scalar(sc1, lt, C1, None, ALU.is_ge))(lt, sc1),
            (lambda lt, sc1: lambda v: v.scalar_tensor_tensor(sc1, sc1, -1e30, lt, ALU.mult, ALU.add))(lt, sc1),
            (lambda sc1: lambda v: v.reduce_max(C2, sc1, AX))(sc1),
            lambda v: v.tensor_scalar(C1, C1, -1.0, None, ALU.mult)], deps=[ta])
        ta = em.op("scalar", (lambda lt, sc1: lambda s: s.activation(sc1, lt, AF.Exp, bias=C1))(lt, sc1), deps=[tv])
        em.seq("vector", [
            (lambda sc1: lambda v: v.reduce_sum(C3, sc1, AX))(sc1),
            lambda v: v.reciprocal(C3, C3),
            (lambda lt, gt: lambda v: v.tensor_scalar(gt, lt, C2, None, ALU.is_ge))(lt, gt),
            (lambda gt, sc1: lambda v: v.tensor_tensor(sc1, sc1, gt, ALU.mult))(gt, sc1),
            (lambda gt, sc1: lambda v: v.tensor_scalar(gt, sc1, C3, None, ALU.mult))(gt, sc1)], deps=[ta])
    em.floor()
    for t in range(16):
        b = t % 4
        oap = bk.pt[b // 2].bitcast(BF16)[0:8, (b % 2) * 1024:(b % 2) * 1024 + 128]
        tr_group(em, bk, [b], [(oap, GTM[:, t * 8:(t + 1) * 8], B["idb"])])
        ev(em, "vector", bk, [b], (lambda t, oap: lambda v: v.tensor_copy(
            GFM[0:8, t * 128:(t + 1) * 128], oap))(t, oap))
    em.floor()
    for e in range(8):
        em.dma("sync", B["GROW"][e][0:1, :], GFM[e:e + 1, :], nofloor=False, sem="grow")
    tb2 = em.dma("gpsimd", B["B2"][0:8, :], W["l2_b2"], sem="bia2")
    tb1c = em.dma("sync", B["LTM"], W["l2_b1c"][:, 0:128], sem="cols", nofloor=False)
    tb1c2 = em.dma("sync", B["gsc1"], W["l2_b1c"][:, 128:256], sem="cols", nofloor=False)
    em.floor()

    GH = [BIG[:, 0:8192], BIG[:, 8192:16384]]
    w1_cons = [None, None]
    w2_cons = [None, None]
    xp_cons = [None, None]
    gev_cons = [None, None]
    gb_cons = [None, None]
    for tp2 in range(8):
        t0 = tp2 * 256
        for e in range(8):
            eslot = e % 2
            gbase = (0, 32, 64)[e % 3]
            bnk = 6 + eslot
            gbofs = 512 + eslot * 1024
            tgb = mm_group(em, bk, [bnk], [(bk.ap(bnk, 0, 256), B["ones1b"][gbase:gbase + 1, 0:128],
                                            B["GROW"][e][0:1, t0:t0 + 256], True, True)])
            tge = ev(em, "vector", bk, [bnk], (lambda gbofs, bnk: lambda v: v.tensor_copy(
                B["GEV"].bitcast(BF16)[:, gbofs:gbofs + 256], bk.ap(bnk, 0, 256)))(gbofs, bnk),
                deps=[gb_cons[eslot]])
            for jb in range(8):
                slot = jb % 2
                tw1 = em.dma("gpsimd", B["W1S"][slot].rearrange("p (k m) -> p k m", k=8),
                             W["l2_w1"][e][:, 512 * jb:512 * (jb + 1)].rearrange("(k p) m -> p k m", p=128),
                             deps=[w1_cons[slot]], sem="w1s%d" % slot)
                pb = 4 + slot
                for jj in range(4):
                    j = jb * 4 + jj
                    mm_group(em, bk, [pb], [(bk.ap(pb, 0, 256),
                                             B["W1S"][slot][:, k * 512 + jj * 128:k * 512 + jj * 128 + 128],
                                             XNB[:, k * T + t0:k * T + t0 + 256], k == 0, k == 7)
                                            for k in range(8)], deps=[tw1])
                    bcol = B["LTM"] if e < 4 else B["gsc1"]
                    bidx = (e % 4) * 32 + j
                    tge2 = ev(em, "scalar", bk, [pb], (lambda e, j, pb, bcol, bidx: lambda s: s.activation(
                        GH[e % 2][:, j * 256:(j + 1) * 256], bk.ap(pb, 0, 256), AF.Gelu,
                        bias=bcol[:, bidx:bidx + 1]))(e, j, pb, bcol, bidx),
                        deps=[tb1c, tb1c2, em.tok("tensor")])
                gev_cons[eslot] = em.op("vector", (lambda e, jb, gbofs: lambda v: v.tensor_tensor(
                    GH[e % 2].rearrange("p (j m) -> p j m", m=256)[:, jb * 4:(jb + 1) * 4, :],
                    GH[e % 2].rearrange("p (j m) -> p j m", m=256)[:, jb * 4:(jb + 1) * 4, :],
                    B["GEV"].bitcast(BF16)[:, gbofs:gbofs + 256].unsqueeze(1).broadcast_to([128, 4, 256]),
                    ALU.mult))(e, jb, gbofs), deps=[tge, em.tok("scalar")])
                w1_cons[slot] = em.tok("tensor")
            gb_cons[eslot] = gev_cons[eslot]
            for j in range(32):
                slot = j % 2
                wofs = 4096 + slot * 1024
                tw2 = em.dma("gpsimd", SB2[:, wofs:wofs + 1024], W["l2_w2"][e][128 * j:128 * (j + 1), :],
                             deps=[w2_cons[slot]], sem="w2s%d" % slot)
                mms = []
                for dc in range(8):
                    mms.append((bk.ap(dc // 2, (dc % 2) * 256, (dc % 2) * 256 + 256),
                                SB2[:, wofs + dc * 128:wofs + (dc + 1) * 128],
                                GH[e % 2][:, j * 256:(j + 1) * 256],
                                e == 0 and j == 0 and dc % 2 == 0, False))
                mm_group(em, bk, [0, 1, 2, 3], mms, deps=[tw2, gev_cons[e % 2]])
                w2_cons[slot] = em.tok("tensor")
        mmsb = []
        for dc in range(8):
            mmsb.append((bk.ap(dc // 2, (dc % 2) * 256, (dc % 2) * 256 + 256),
                         B["B2"][0:8, dc * 128:(dc + 1) * 128], B["GFM"][0:8, t0:t0 + 256],
                         False, dc % 2 == 1))
        mm_group(em, bk, [0, 1, 2, 3], mmsb, deps=[tb2])
        for dc in range(8):
            slot = dc % 2
            xp = B["XP"][:, slot * 512:slot * 512 + 256]
            gv = B["GEV"][:, slot * 512:slot * 512 + 256]
            tx = em.dma("sync", xp, x_dram[dc * 128:(dc + 1) * 128, t0:t0 + 256], deps=[xp_cons[slot]],
                        sem="xp%d" % slot, nofloor=False)
            tsl = ev(em, "scalar", bk, [dc // 2], (lambda dc, gv: lambda s: s.activation(
                gv, bk.ap(dc // 2, (dc % 2) * 256, (dc % 2) * 256 + 256), AF.Silu))(dc, gv),
                deps=[gev_cons[0], gev_cons[1]])
            tv = em.seq("vector", [
                (lambda dc, gv, t0: lambda v: v.tensor_tensor(
                    gv, gv, S[:, dc * T + t0:dc * T + t0 + 256], ALU.mult))(dc, gv, t0),
                (lambda xp, gv: lambda v: v.tensor_tensor(xp, xp, gv, ALU.add))(xp, gv)], deps=[tx, tsl])
            gev_cons[0] = tv if slot == 0 else gev_cons[0]
            gev_cons[1] = tv if slot == 1 else gev_cons[1]
            xp_cons[slot] = em.dma("sync", xout_dram[dc * 128:(dc + 1) * 128, t0:t0 + 256], xp,
                                   deps=[tv], nofloor=False, sem="xo%d" % slot)
        em.floor()




MSK_ABS = 0x7FFFFFFF
MSK_EXP = 0x7F800000
MSK_RND = -4194304
MSK_SGN = -2147483648


def emit_quant(em, B, H, YA, W2, M, SC, src_tok, dst, nblk, deps=()):
    Wd = 16 * nblk
    Hi, YAi, W2i, Mi = H.bitcast(I32), YA.bitcast(I32), W2.bitcast(I32), M.bitcast(I32)
    SCm = SC[:, 0:nblk]
    RC = SC[:, nblk:2 * nblk]
    bc = lambda a: a.unsqueeze(2).broadcast_to([128, nblk, 16])
    v3 = lambda a: a.rearrange("p (b w) -> p b w", w=16)
    return em.seq("vector", [
        lambda v: v.tensor_single_scalar(YAi, Hi, MSK_ABS, ALU.bitwise_and),
        lambda v: v.tensor_single_scalar(Hi, Hi, MSK_SGN, ALU.bitwise_and),
        lambda v: v.reduce_max(SCm.unsqueeze(2), v3(YA), AX),
        lambda v: v.tensor_scalar(SCm, SCm, 1.0 / 6.0, 1e-12, ALU.mult, ALU.max),
        lambda v: v.reciprocal(RC, SCm),
        lambda v: v.tensor_tensor(v3(YA), v3(YA), bc(RC), ALU.mult),
        lambda v: v.tensor_single_scalar(W2i, YAi, MSK_EXP, ALU.bitwise_and),
        lambda v: v.scalar_tensor_tensor(W2, W2, 0.25, YA, ALU.mult, ALU.add),
        lambda v: v.tensor_single_scalar(W2i, W2i, MSK_RND, ALU.bitwise_and),
        lambda v: v.tensor_scalar_max(W2, W2, 0.5),
        lambda v: v.tensor_single_scalar(M, YA, 0.75, ALU.is_gt),
        lambda v: v.tensor_scalar(M, M, 0.5, 0.5, ALU.mult, ALU.add),
        lambda v: v.tensor_single_scalar(YA, YA, 0.25, ALU.is_gt),
        lambda v: v.tensor_tensor(Hi, Hi, YAi, ALU.bitwise_or),
        lambda v: v.tensor_single_scalar(YA, W2, 0.75, ALU.is_equal),
        lambda v: v.copy_predicated(W2, YAi, M),
        lambda v: v.tensor_tensor(v3(W2), v3(W2), bc(SCm), ALU.mult),
        lambda v: v.tensor_tensor(dst, W2, H, ALU.mult),
    ], deps=list(deps) + [src_tok])


def emit_l3_p0(em, bk, B, W, QW):
    pt = bk.pt
    XNB, SA, SB2 = B["XNB"], B["SMALLA"], B["SMALLB"]
    qwg_d, qwu_d, qwd_d = QW["qwg"], QW["qwu"], QW["qwd"]
    H1 = SA[:, 0:1024]
    YA1 = SA[:, 1024:2048]
    W21 = SA[:, 2 * T:2 * T + 1024]
    M1 = SA[:, 2 * T + 1024:2 * T + 2048]
    SC1 = SA[:, 2048:2176]
    QT1 = SB2[:, 4096:5120]
    STG = SB2[:, 5120:6144]
    ld_cons = None
    stg_cons = None
    for (wdram, qdram) in ((W["l3_wg"], qwg_d), (W["l3_wu"], qwu_d)):
        for r in range(22):
            tl = em.dma("sync", H1, wdram[128 * r:128 * (r + 1), :], deps=[ld_cons], sem="wq", nofloor=False)
            tq = emit_quant(em, B, H1, YA1, W21, M1, SC1, tl, QT1, 64, deps=[stg_cons, em.tok("tensor")])
            ld_cons = tq
            trs = []
            for cc in range(8):
                b = cc % 4
                tr_group(em, bk, [b], [(bk.pt[b // 2].bitcast(BF16)[:, (b % 2) * 1024:(b % 2) * 1024 + 128],
                                        QT1[:, cc * 128:(cc + 1) * 128], B["idb"])], deps=[tq])
                trs.append(ev(em, "vector", bk, [b], (lambda cc, b: lambda v: v.tensor_copy(
                    STG[:, cc * 128:(cc + 1) * 128],
                    bk.pt[b // 2].bitcast(BF16)[:, (b % 2) * 1024:(b % 2) * 1024 + 128]))(cc, b)))
            stg_cons = em.dma("sync", qdram[:, 128 * r:128 * (r + 1)].rearrange("(c p) m -> p c m", p=128),
                              STG.rearrange("p (c m) -> p c m", c=8), deps=trs, sem="wqs", nofloor=False)
    em.floor()
    H2b = SA[:, 0:2816]
    SC2 = SA[:, 2816:2816 + 352]
    YA2 = SA[:, 3328:3328 + 2816]
    XNBF_ = XNB.bitcast(F32)
    W22 = XNBF_[:, 0:2816]
    M2 = XNBF_[:, 2816:5632]
    QT22 = SB2[:, 4096:6912]
    ld_cons = None
    stg_cons = None
    for r in range(8):
        tl = em.dma("sync", H2b, W["l3_wd"][128 * r:128 * (r + 1), :], deps=[ld_cons], sem="wq", nofloor=False)
        tq = emit_quant(em, B, H2b, YA2, W22, M2, SC2, tl, QT22, 176, deps=[stg_cons, em.tok("tensor")])
        ld_cons = tq
        trs = []
        for cc in range(22):
            b = cc % 4
            tr_group(em, bk, [b], [(bk.pt[b // 2].bitcast(BF16)[:, (b % 2) * 1024:(b % 2) * 1024 + 128],
                                    QT22[:, cc * 128:(cc + 1) * 128], B["idb"])], deps=[tq])
            trs.append(ev(em, "vector", bk, [b], (lambda cc, b: lambda v: v.tensor_copy(
                SB2[:, 6912 + cc * 128:6912 + (cc + 1) * 128],
                bk.pt[b // 2].bitcast(BF16)[:, (b % 2) * 1024:(b % 2) * 1024 + 128]))(cc, b)))
        stg_cons = em.dma("sync", qwd_d[:, 128 * r:128 * (r + 1)].rearrange("(c p) m -> p c m", p=128),
                          SB2[:, 6912:6912 + 2816].rearrange("p (c m) -> p c m", c=22), deps=trs,
                          sem="wqs", nofloor=False)
    em.floor()


def emit_l3(em, bk, B, W, QW, x_dram, xout_dram):
    pt = bk.pt
    XNB, S, BIG, SA, SB2 = B["XNB"], B["S"], B["BIG"], B["SMALLA"], B["SMALLB"]
    XNBF = XNB.bitcast(F32)
    YN = BIG
    qwg_d, qwu_d, qwd_d, qh_d, h_d = QW["qwg"], QW["qwu"], QW["qwd"], QW["qh"], QW["hsp"]
    COLS = B["L1C"]
    boc = lambda g: COLS[:, g:g + 1]
    tcols = em.dma("sync", COLS[:, 0:8], W["l3_boc"], sem="cols", nofloor=False)
    H1 = SA[:, 0:1024]
    YA1 = SA[:, 1024:2048]
    W21 = SA[:, 2 * T:2 * T + 1024]
    M1 = SA[:, 2 * T + 1024:2 * T + 2048]
    SC1 = SA[:, 2048:2176]

    QF = BIG
    QA = SA[:, 0:512]
    QB = SA[:, 512:1024]
    tqf = None
    for g in range(8):
        tw = ldw_fm(em, SB2[:, 4096:5120], W["l3_wqT"][:, 128 * g:128 * (g + 1)],
                    deps=[em.tok("tensor")], sem="wB0")
        for c in range(4):
            b = c % 2
            mm_group(em, bk, [b], [(bk.ap(b), SB2[:, 4096 + k * 128:4096 + (k + 1) * 128],
                                    XNB[:, k * T + c * 512:k * T + (c + 1) * 512], k == 0, k == 7)
                                   for k in range(8)], deps=[tw])
            t1 = ev(em, "vector", bk, [b], (lambda b: lambda v: (
                v.tensor_single_scalar(QA, bk.ap(b), 0.0, ALU.min),
                v.tensor_single_scalar(QB, bk.ap(b), 0.0, ALU.max))[-1])(b), deps=[em.tok("scalar"), tqf])
            t2 = em.op("scalar", lambda s: s.activation(QA, QA, AF.Exp), deps=[t1])
            tqf = em.op("vector", (lambda g, c: lambda v: v.tensor_tensor(
                QF[:, g * T + c * 512:g * T + (c + 1) * 512], QA, QB, ALU.add))(g, c), deps=[t2])
    em.floor()
    KF = SB2[:, 8192:8448]
    VA = SB2[:, 8448:8708]
    KVB = SB2[:, 8708:9228]
    VA4 = VA.rearrange("p (h w) -> p h w", w=65)
    tvon = em.op("vector", lambda v: v.memset(VA4[:, :, 64:65], 1.0))
    for q in range(4):
        twk = ldw_fm(em, SB2[:, 4096:6144], W["l3_wkT"][:, 256 * q:256 * (q + 1)],
                     deps=[em.tok("tensor")], sem="wB0")
        twv = ldw_fm(em, SB2[:, 6144:8192], W["l3_wvT"][:, 256 * q:256 * (q + 1)],
                     deps=[em.tok("tensor")], sem="wB1")
        kf_cons = None
        va_cons = tvon
        for tt in range(16):
            mm_group(em, bk, [4], [(bk.ap(4, 0, 256), XNB[:, k * T + tt * 128:k * T + (tt + 1) * 128],
                                    SB2[:, 4096 + k * 256:4096 + (k + 1) * 256], k == 0, k == 7)
                                   for k in range(8)], deps=[twk])
            t1 = ev(em, "vector", bk, [4], (lambda: lambda v: (
                v.tensor_single_scalar(QA[:, 0:256], bk.ap(4, 0, 256), 0.0, ALU.min),
                v.tensor_single_scalar(QB[:, 0:256], bk.ap(4, 0, 256), 0.0, ALU.max))[-1])(),
                deps=[em.tok("scalar"), kf_cons])
            t2 = em.op("scalar", lambda s: s.activation(QA[:, 0:256], QA[:, 0:256], AF.Exp), deps=[t1])
            tkf = em.op("vector", lambda v: v.tensor_tensor(KF, QA[:, 0:256], QB[:, 0:256], ALU.add), deps=[t2])
            mm_group(em, bk, [5], [(bk.ap(5, 0, 256), XNB[:, k * T + tt * 128:k * T + (tt + 1) * 128],
                                    SB2[:, 6144 + k * 256:6144 + (k + 1) * 256], k == 0, k == 7)
                                   for k in range(8)], deps=[twv])
            tva = ev(em, "vector", bk, [5], (lambda: lambda v: v.tensor_copy(
                VA4[:, :, 0:64], bk.ap(5, 0, 256).rearrange("p (h w) -> p h w", w=64)))(), deps=[va_cons])
            mms = []
            for hl in range(4):
                mms.append((bk.ap(q, 128 * hl, 128 * hl + 65)[0:64, :], KF[:, 64 * hl:64 * hl + 64],
                            VA[:, 65 * hl:65 * hl + 65], tt == 0 and hl == 0, tt == 15 and hl == 3))
            tkv = mm_group(em, bk, [q], mms, deps=[tkf, tva])
            kf_cons = tkv
            va_cons = tkv
        for hl in range(4):
            h = 4 * q + hl
            ev(em, "vector", bk, [q], (lambda q, hl, h: lambda v: v.tensor_copy(
                KVB[64 * (h % 2):64 * (h % 2) + 64, 65 * (h // 2):65 * (h // 2) + 65],
                bk.ap(q, 128 * hl, 128 * hl + 65)[0:64, :]))(q, hl, h))
    em.floor()
    tdpk = None
    for h in range(16):
        hp = h % 2
        c = h // 2
        mm_group(em, bk, [4, 5, 6, 7],
                 [(bk.ap(4 + qq)[0:65, :], KVB[64 * hp:64 * hp + 64, 65 * c:65 * c + 65],
                   QF[64 * hp:64 * hp + 64, c * T + qq * 512:c * T + (qq + 1) * 512], True, True)
                  for qq in range(4)])
        tdr = ev(em, "vector", bk, [4, 5, 6, 7], lambda v: (
            v.tensor_copy(B["RROW"][0:1, 0:1024], pt[2][64:65, :]),
            v.tensor_copy(B["RROW"][0:1, 1024:2048], pt[3][64:65, :]))[-1], deps=[tdpk])
        tdpk = em.dma("sync", B["DPACK"][h:h + 1, :], B["RROW"][0:1, :], deps=[tdr], nofloor=False, sem="dpk")
        ev(em, "vector", bk, [4, 5, 6, 7], (lambda c, hp: lambda v: (
            v.tensor_copy(YN[64 * hp:64 * hp + 64, c * 2048:c * 2048 + 1024], pt[2][0:64, :]),
            v.tensor_copy(YN[64 * hp:64 * hp + 64, c * 2048 + 1024:c * 2048 + 2048], pt[3][0:64, :]))[-1])(c, hp))
    em.floor()
    trc = em.seq("vector", [
        lambda v: v.tensor_scalar_add(B["DPACK"], B["DPACK"], 1e-6),
        lambda v: v.reciprocal(B["DPACK"], B["DPACK"])])
    for c in range(8):
        for cq in range(4):
            b = cq % 4
            mm_group(em, bk, [b], [(bk.ap(b), B["E16"][:, 128 * c:128 * (c + 1)],
                                    B["DPACK"][:, cq * 512:(cq + 1) * 512], True, True)], deps=[trc])
            ev(em, "vector", bk, [b], (lambda c, cq, b: lambda v: v.tensor_tensor(
                YN[:, c * 2048 + cq * 512:c * 2048 + (cq + 1) * 512],
                YN[:, c * 2048 + cq * 512:c * 2048 + (cq + 1) * 512], bk.ap(b), ALU.mult))(c, cq, b))
    em.floor()
    for m in range(8):
        tw = ldw_fm(em, SB2[:, 4096:5120], W["l3_woT"][:, 128 * m:128 * (m + 1)],
                    deps=[em.tok("tensor")], sem="wB0")
        for c in range(4):
            b = c % 4
            mm_group(em, bk, [b], [(bk.ap(b), SB2[:, 4096 + k * 128:4096 + (k + 1) * 128],
                                    YN[:, k * 2048 + c * 512:k * 2048 + (c + 1) * 512], k == 0, k == 7)
                                   for k in range(8)], deps=[tw])
            ev(em, "vector", bk, [b], (lambda m, c, b: lambda v: v.tensor_scalar_add(
                S[:, m * T + c * 512:m * T + (c + 1) * 512], bk.ap(b), boc(m)))(m, c, b),
                deps=[tcols])
    em.floor()

    XTMB = SB2[:, 4096:5120]
    QXT = SB2[:, 5120:6144]
    prevq = None
    for tt in range(16):
        trs = []
        for k in range(8):
            b = k % 4
            tr_group(em, bk, [b], [(bk.pt[b // 2].bitcast(BF16)[:, (b % 2) * 1024:(b % 2) * 1024 + 128],
                                    XNB[:, k * T + tt * 128:k * T + (tt + 1) * 128], B["idb"])],
                     deps=[prevq])
            trs.append(ev(em, "vector", bk, [b], (lambda k, b: lambda v: v.tensor_copy(
                XTMB[:, k * 128:(k + 1) * 128],
                bk.pt[b // 2].bitcast(BF16)[:, (b % 2) * 1024:(b % 2) * 1024 + 128]))(k, b)))
        tcp = em.op("vector", lambda v: v.tensor_copy(H1, XTMB), deps=trs)
        tq = emit_quant(em, B, H1, YA1, W21, M1, SC1, tcp, QXT, 64, deps=[em.tok("tensor")])
        trs2 = []
        for k in range(8):
            b = k % 4
            tr_group(em, bk, [b], [(bk.pt[b // 2].bitcast(BF16)[:, (b % 2) * 1024:(b % 2) * 1024 + 128],
                                    QXT[:, k * 128:(k + 1) * 128], B["idb"])], deps=[tq])
            trs2.append(ev(em, "vector", bk, [b], (lambda k, tt, b: lambda v: v.tensor_copy(
                BIG[:, k * T + tt * 128:k * T + (tt + 1) * 128],
                bk.pt[b // 2].bitcast(BF16)[:, (b % 2) * 1024:(b % 2) * 1024 + 128]))(k, tt, b)))
        prevq = trs2[-1]
    em.floor()

    hg = SA[:, 0:256]
    hu = SA[:, 256:512]
    hh = SA[:, 512:768]
    hsp_cons = None
    for nb in range(11):
        twg = em.dma("sync", SB2[:, 4096:6144].rearrange("p (c m) -> p c m", c=8),
                     qwg_d[:, 256 * nb:256 * (nb + 1)].rearrange("(c p) m -> p c m", p=128),
                     deps=[em.tok("tensor")], sem="sqg", nofloor=False)
        twu = em.dma("sync", SB2[:, 6144:8192].rearrange("p (c m) -> p c m", c=8),
                     qwu_d[:, 256 * nb:256 * (nb + 1)].rearrange("(c p) m -> p c m", p=128),
                     deps=[em.tok("tensor")], sem="squ", nofloor=False)
        for tt in range(16):
            bg_ = (tt % 2)
            bu_ = 2 + (tt % 2)
            mm_group(em, bk, [bg_], [(bk.ap(bg_, 0, 256), BIG[:, k * T + tt * 128:k * T + (tt + 1) * 128],
                                      SB2[:, 4096 + k * 256:4096 + (k + 1) * 256], k == 0, k == 7)
                                     for k in range(8)], deps=[twg])
            mm_group(em, bk, [bu_], [(bk.ap(bu_, 0, 256), BIG[:, k * T + tt * 128:k * T + (tt + 1) * 128],
                                      SB2[:, 6144 + k * 256:6144 + (k + 1) * 256], k == 0, k == 7)
                                     for k in range(8)], deps=[twu])
            ts1 = ev(em, "scalar", bk, [bg_], (lambda bg_: lambda s: s.activation(
                hg, bk.ap(bg_, 0, 256), AF.Silu))(bg_), deps=[hsp_cons])
            ts2 = ev(em, "vector", bk, [bu_], (lambda bu_: lambda v: v.tensor_copy(
                hu, bk.ap(bu_, 0, 256)))(bu_), deps=[hsp_cons])
            ts3 = em.op("vector", lambda v: v.tensor_tensor(hh, hg, hu, ALU.mult), deps=[ts1, ts2])
            hsp_cons = em.dma("sync", h_d[tt, :, 256 * nb:256 * (nb + 1)], hh, deps=[ts3],
                              sem="hsp", nofloor=False)
    em.floor()
    H2b = SA[:, 0:2816]
    SC2 = SA[:, 2816:2816 + 352]
    YA2 = SA[:, 3328:3328 + 2816]
    W22 = XNBF[:, 0:2816]
    M2 = XNBF[:, 2816:5632]
    QHT = SB2[:, 4096:6912]
    STG2 = SB2[:, 6912:7168]
    prevq = None
    stg_cons = None
    for tt in range(16):
        tl = em.dma("sync", H2b, h_d[tt], deps=[prevq], sem="wq", nofloor=False)
        tq = emit_quant(em, B, H2b, YA2, W22, M2, SC2, tl, QHT, 176, deps=[stg_cons, em.tok("tensor")])
        prevq = tq
        for nb in range(11):
            trs = []
            for cc in range(2):
                b = (nb * 2 + cc) % 4
                tr_group(em, bk, [b], [(bk.pt[b // 2].bitcast(BF16)[:, (b % 2) * 1024:(b % 2) * 1024 + 128],
                                        QHT[:, nb * 256 + cc * 128:nb * 256 + (cc + 1) * 128], B["idb"])],
                         deps=[tq])
                trs.append(ev(em, "vector", bk, [b], (lambda cc, b: lambda v: v.tensor_copy(
                    STG2[:, cc * 128:(cc + 1) * 128],
                    bk.pt[b // 2].bitcast(BF16)[:, (b % 2) * 1024:(b % 2) * 1024 + 128]))(cc, b),
                    deps=[stg_cons]))
            stg_cons = em.dma("sync", qh_d[256 * nb:256 * (nb + 1), 128 * tt:128 * (tt + 1)]
                              .rearrange("(c p) m -> p c m", p=128),
                              STG2.rearrange("p (c m) -> p c m", c=2), deps=trs, sem="wqs", nofloor=False)
    em.floor()
    xp_cons = [None, None]
    gev_cons = None
    for ttb in range(4):
        tqh = em.dma("sync", BIG[:, 0:22 * 512].rearrange("p (c m) -> p c m", c=22),
                     qh_d[:, 512 * ttb:512 * (ttb + 1)].rearrange("(c p) m -> p c m", p=128),
                     deps=[em.tok("tensor")], sem="sqh", nofloor=False)
        wd_cons = [None, None]
        for nb2 in range(22):
            slot = nb2 % 2
            wofs = 7168 + slot * 1024
            twd = em.dma("sync", SB2[:, wofs:wofs + 1024].rearrange("p (c m) -> p c m", c=8),
                         qwd_d[128 * nb2:128 * (nb2 + 1), :].rearrange("(c p) m -> p c m", p=128),
                         deps=[wd_cons[slot]], sem="sqd%d" % slot, nofloor=False)
            for m in range(8):
                mm_group(em, bk, [m], [(bk.ap(m), SB2[:, wofs + m * 128:wofs + (m + 1) * 128],
                                        BIG[:, nb2 * 512:(nb2 + 1) * 512], nb2 == 0, nb2 == 21)],
                         deps=[twd, tqh])
            wd_cons[slot] = em.tok("tensor")
        for m in range(8):
            slot = m % 2
            xp = B["XP"][:, slot * 512:(slot + 1) * 512]
            gv = B["GEV"][:, slot * 512:(slot + 1) * 512]
            tx = em.dma("sync", xp, x_dram[m * 128:(m + 1) * 128, ttb * 512:(ttb + 1) * 512],
                        deps=[xp_cons[slot]], sem="xp%d" % slot, nofloor=False)
            tsl = ev(em, "scalar", bk, [m], (lambda m, gv: lambda s: s.activation(
                gv, bk.ap(m), AF.Silu))(m, gv), deps=[gev_cons])
            tv = em.seq("vector", [
                (lambda m, gv, ttb: lambda v: v.tensor_tensor(
                    gv, gv, S[:, m * T + ttb * 512:m * T + (ttb + 1) * 512], ALU.mult))(m, gv, ttb),
                (lambda xp, gv: lambda v: v.tensor_tensor(xp, xp, gv, ALU.add))(xp, gv)], deps=[tx, tsl])
            gev_cons = tv
            xp_cons[slot] = em.dma("sync", xout_dram[m * 128:(m + 1) * 128, ttb * 512:(ttb + 1) * 512], xp,
                                   deps=[tv], nofloor=False, sem="xo%d" % slot)
        em.floor()





def build_program(layers, debug=None):
    nc = bass.Bass("TRN2", target_bir_lowering=False, debug=False, dynamic_dma_scratch_size=16384)
    ctx = ExitStack()
    em = Emitter(nc, ctx)

    xin = nc.dram_tensor("xin", [D, T], F32, kind="ExternalInput").ap()
    xout = nc.dram_tensor("xout", [D, T], F32, kind="ExternalOutput").ap()
    W = {}

    def win(name, shape, dt=F32):
        W[name] = nc.dram_tensor(name, shape, dt, kind="ExternalInput").ap()

    win("c_idb", [128, 128], BF16)
    win("c_idf", [8, 8], F32)
    win("c_ones1f", [65, 128], F32)
    win("c_onescolf", [128, 1], F32)
    win("c_ones1b", [65, 512], BF16)
    win("c_E16", [16, 1024], F32)
    if 0 in layers:
        win("l0_wvT", [1024, 256]); win("l0_wuT", [256, 3072])
        win("l0_wovT", [1024, 256]); win("l0_wouT", [256, 1024])
        win("l0_w1", [8, 1024, 512]); win("l0_b1r", [1, 4096])
        win("l0_w2", [8, 512, 1024]); win("l0_b2", [8, 1024])
        win("l0_centT", [1024, 8]); win("l0_routeT", [1024, 8])

    if 1 in layers:
        win("l1_wx", [1024, 1024]); win("l1_wy", [1024, 1024]); win("l1_wi", [1024, 1024])
        win("l1_wa", [1024, 1024]); win("l1_wo", [1024, 1024])
        win("l1_wg", [1024, 4096]); win("l1_wu", [1024, 4096]); win("l1_wd", [4096, 1024])
        win("l1_wgT", [1024, 3072], I32); win("l1_wuT", [1024, 3072], I32); win("l1_wdT", [3072, 1024], I32)
        win("l1_cols", [128, 64])

    if 2 in layers:
        win("l2_waT", [1024, 1024]); win("l2_wxT", [1024, 1024])
        win("l2_cols", [128, 24]); win("l2_wg", [1024, 8]); win("l2_bgr", [1, 8])
        win("l2_w1", [8, 1024, 4096]); win("l2_b1c", [128, 256])
        win("l2_w2", [8, 4096, 1024]); win("l2_b2", [8, 1024])

    if 3 in layers:
        win("l3_wqT", [1024, 1024]); win("l3_wkT", [1024, 1024]); win("l3_wvT", [1024, 1024])
        win("l3_woT", [1024, 1024]); win("l3_boc", [128, 8])
        win("l3_wg", [2816, 1024]); win("l3_wu", [2816, 1024]); win("l3_wd", [1024, 2816])
    QW3 = {}
    if 3 in layers:
        QW3["qwg"] = nc.dram_tensor("qwg_d", [1024, 2816], BF16, kind="Internal").ap()
        QW3["qwu"] = nc.dram_tensor("qwu_d", [1024, 2816], BF16, kind="Internal").ap()
        QW3["qwd"] = nc.dram_tensor("qwd_d", [2816, 1024], BF16, kind="Internal").ap()
        QW3["qh"] = nc.dram_tensor("qh_d", [2816, 2048], BF16, kind="Internal").ap()
        QW3["hsp"] = nc.dram_tensor("h_d", [16, 128, 2816], F32, kind="Internal").ap()

    xmid = [nc.dram_tensor(f"xmid{i}", [D, T], F32, kind="Internal").ap()
            for i in range(max(0, len(layers) - 1))]

    B = {}

    def sb(name, shape, dt):
        B[name] = ctx.enter_context(nc.sbuf_tensor(name, shape, dt))[:]

    sb("XNB", [128, 8 * T], BF16)
    sb("S", [128, 8 * T], F32)
    sb("BIG", [128, 8 * T], BF16)
    sb("W1Sa", [128, 4096], BF16)
    sb("W1Sb", [128, 4096], BF16)
    B["W1S"] = [B["W1Sa"], B["W1Sb"]]
    B["W2S"] = B["W1S"]
    sb("SMALLA", [128, 3 * T], F32)
    sb("SMALLB", [128, 5 * T], BF16)
    sb("WVT", [128, 2048], BF16)
    sb("CNB", [128, 64], BF16)
    sb("RTB", [128, 64], BF16)
    sb("LTM", [128, 16 * 8], F32)
    sb("GTM", [128, 16 * 8], BF16)
    sb("gsc1", [128, 16 * 8], F32)
    B["SQ8"] = B["gsc1"][:, 0:8]
    sb("gcol1", [128, 1], F32)
    sb("gcol2", [128, 1], F32)
    sb("gcol3", [128, 1], F32)
    sb("CEPS", [128, 3], F32)
    sb("L1C", [128, 64], F32)
    sb("idb", [128, 128], BF16)
    sb("E16", [16, 1024], F32)
    sb("idf", [8, 8], F32)
    sb("ones1f", [65, 128], F32)
    sb("onescolf", [128, 1], F32)
    sb("ones1b", [65, 512], BF16)
    B["SQ"] = B["S"][:, 0:T]
    B["BCA"] = B["BIG"].bitcast(F32)[:, 0:T]
    B["BCB"] = B["BIG"].bitcast(F32)[:, T:2 * T]
    B["XF2"] = B["BIG"].bitcast(F32)[:, 2 * T:4 * T]
    B["T1B"] = B["BIG"][:, 8192:12288]
    B["WUS"] = B["WVT"][:, 0:1536]
    B["WOU"] = B["WVT"][:, 0:2048]
    B["GEV"] = B["SMALLA"][:, 0:1024]
    B["XP"] = B["SMALLA"][:, 1024:2048]
    SA, SB2 = B["SMALLA"], B["SMALLB"]
    B["MROW"] = SA[0:1, 0:T]
    B["VROW"] = SA[0:1, T:2 * T]
    B["AROW"] = SA[0:1, 2 * T:3 * T]
    B["RROW"] = SA[0:1, 0:T]
    B["LG"] = SA[0:8, 2 * T:3 * T]
    B["DPACK"] = SA[0:16, T:2 * T]
    B["GROW"] = [SB2[0:1, 0:T], SB2[32:33, 0:T], SB2[64:65, 0:T],
                 SB2[0:1, T:2 * T], SB2[32:33, T:2 * T], SB2[64:65, T:2 * T],
                 SB2[0:1, 2 * T:3 * T], SB2[32:33, 2 * T:3 * T]]
    B["GFM"] = SB2[64:72, 2 * T:3 * T]
    B["B1R"] = SB2[0:1, 3 * T:3 * T + 4096]
    B["B2"] = SB2[64:72, 3 * T:3 * T + 1024]
    B["onesrowb"] = B["ones1b"][0:1, 0:512]

    pts = [ctx.enter_context(nc.psum_tensor(f"pt{i}", [128, 1024], F32))[:] for i in range(4)]
    bk = Banks(em, pts)

    for nm in ("idb", "idf", "ones1f", "onescolf", "ones1b"):
        em.dma("sync", B[nm], W["c_" + nm], sem="const")
    em.dma("sync", B["E16"], W["c_E16"], sem="const")
    em.op("vector", lambda v: (v.memset(B["CEPS"][:, 0:1], 1e-5), v.memset(B["CEPS"][:, 1:2], 1e-6),
                               v.memset(B["CEPS"][:, 2:3], 1.0))[-1])
    em.floor()

    def dump(name, ap):
        if debug is None or name not in debug:
            return
        o = nc.dram_tensor("dbg_" + name, list(ap.shape), ap.dtype, kind="ExternalOutput").ap()
        em.floor()
        em.dma("sync", o, ap, nofloor=False, sem="dump")
        em.floor()

    xcur = xin
    for i, li in enumerate(layers):
        xnext = xout if i == len(layers) - 1 else xmid[i]
        if li == 3:
            emit_l3_p0(em, bk, B, W, QW3)
            emit_ln(em, bk, B, xcur, want_rxn=False)
            emit_l3(em, bk, B, W, QW3, xcur, xnext)
        if li == 2:
            emit_ln(em, bk, B, xcur, want_rxn=False)
            emit_l2(em, bk, B, W, xcur, xnext)
        if li == 1:
            emit_ln(em, bk, B, xcur, want_rxn=False)
            emit_l1(em, bk, B, W, xcur, xnext)
        if li == 0:
            trxn = emit_ln(em, bk, B, xcur, want_rxn=True)
            dump("xnb", B["XNB"])
            emit_l0_routing(em, bk, B, W, trxn)
            dump("gfm", B["GFM"])
            dump("ltm", B["LTM"])
            dump("gtm", B["GTM"])
            dump("xnb3", B["XNB"])
            dump("gsc1", B["gsc1"])
            emit_l0_attn(em, bk, B, W, dump=(dump if debug else None))
            dump("s", B["S"])
            dump("xnb2", B["XNB"])
            emit_l0_moe_residual(em, bk, B, W, xcur, xnext, dump=(dump if debug else None))
            dump("ghlast", B["BIG"])
            dump("smallb", B["SMALLB"])
        xcur = xnext

    taildeps = [(k, em.cnt[k]) for k in list(em.cnt.keys()) if k.startswith("D:")]
    em.op("vector", lambda v: v.engine_nop(), deps=taildeps, nofloor=True)

    blk = ctx.enter_context(nc.Block())
    em.run(blk)
    ctx.close()
    return nc



def _consts():
    c = {}
    c["c_idb"] = np.eye(128, dtype=BF)
    c["c_idf"] = np.eye(8, dtype=np.float32)
    of = np.zeros((65, 128), np.float32); of[0] = 1; of[32] = 1; of[64] = 1
    c["c_ones1f"] = of
    c["c_onescolf"] = np.ones((128, 1), np.float32)
    ob = np.zeros((65, 512), BF); ob[0] = 1; ob[32] = 1; ob[64] = 1
    c["c_ones1b"] = ob
    E = np.zeros((16, 1024), np.float32)
    for p in range(1024):
        E[p // 64, p] = 1.0
    c["c_E16"] = E
    return c


def _prep_weights(inp, layers):
    w = {}
    A = np.ascontiguousarray
    if 0 in layers:
        w["l0_wvT"] = A(np.asarray(inp["L0_attn_W_qkv_v"]).T)
        w["l0_wuT"] = A(np.asarray(inp["L0_attn_W_qkv_u"]).T)
        w["l0_wovT"] = A(np.asarray(inp["L0_attn_W_o_v"]).T)
        w["l0_wouT"] = A(np.asarray(inp["L0_attn_W_o_u"]).T)
        w["l0_w1"] = A(np.asarray(inp["L0_ffn_w1"]))
        w["l0_b1r"] = A(np.asarray(inp["L0_ffn_b1"]).reshape(1, 4096))
        w["l0_w2"] = A(np.asarray(inp["L0_ffn_w2"]))
        w["l0_b2"] = A(np.asarray(inp["L0_ffn_b2"]))
        w["l0_centT"] = A(np.asarray(inp["L0_ffn_centroids"]).T)
        w["l0_routeT"] = A(np.asarray(inp["L0_ffn_w_route"]).T)
    if 1 in layers:
        w["l1_wx"] = A(np.asarray(inp["L1_attn_Wx"]))
        w["l1_wy"] = A(np.asarray(inp["L1_attn_Wy"]))
        w["l1_wi"] = A(np.asarray(inp["L1_attn_Wi"]))
        w["l1_wa"] = A(np.asarray(inp["L1_attn_Wa"]))
        w["l1_wo"] = A(np.asarray(inp["L1_attn_Wo"]))
        w["l1_wg"] = A(np.asarray(inp["L1_attn_Wg"]))
        w["l1_wu"] = A(np.asarray(inp["L1_attn_Wu"]))
        w["l1_wd"] = A(np.asarray(inp["L1_attn_Wd"]))
        w["l1_wgT"] = A(np.asarray(inp["L1_ffn_w_gate"]).T)
        w["l1_wuT"] = A(np.asarray(inp["L1_ffn_w_up"]).T)
        w["l1_wdT"] = A(np.asarray(inp["L1_ffn_w_down"]).T)
        cols = np.zeros((128, 64), np.float32)
        cols[:, 0:8] = np.asarray(inp["L1_attn_norm1_w"]).reshape(8, 128).T
        cols[:, 8:16] = np.asarray(inp["L1_attn_norm2_w"]).reshape(8, 128).T
        cols[:, 16:24] = np.asarray(inp["L1_attn_lam"]).reshape(8, 128).T
        cols[:, 24:32] = np.asarray(inp["L1_attn_conv_b"]).reshape(8, 128).T
        cw = np.asarray(inp["L1_attn_conv_w"])
        for g in range(8):
            for k in range(4):
                cols[:, 32 + g * 4 + k] = cw[k, g * 128:(g + 1) * 128]
        w["l1_cols"] = cols
    if 3 in layers:
        w["l3_wqT"] = A(np.asarray(inp["L3_attn_wq"]).T)
        w["l3_wkT"] = A(np.asarray(inp["L3_attn_wk"]).T)
        w["l3_wvT"] = A(np.asarray(inp["L3_attn_wv"]).T)
        w["l3_woT"] = A(np.asarray(inp["L3_attn_wo"]).T)
        w["l3_boc"] = A(np.asarray(inp["L3_attn_bo"]).reshape(8, 128).T)
        w["l3_wg"] = A(np.asarray(inp["L3_ffn_w_gate"]))
        w["l3_wu"] = A(np.asarray(inp["L3_ffn_w_up"]))
        w["l3_wd"] = A(np.asarray(inp["L3_ffn_w_down"]))
    if 2 in layers:
        w["l2_waT"] = A(np.asarray(inp["L2_attn_Wa"]).T)
        w["l2_wxT"] = A(np.asarray(inp["L2_attn_Wx"]).T)
        cols = np.zeros((128, 24), np.float32)
        cols[:, 0:8] = np.asarray(inp["L2_attn_ba"]).reshape(8, 128).T
        cols[:, 8:16] = np.asarray(inp["L2_attn_bx"]).reshape(8, 128).T
        cols[:, 16:24] = np.asarray(inp["L2_attn_Lam"]).reshape(8, 128).T
        w["l2_cols"] = cols
        w["l2_wg"] = A(np.asarray(inp["L2_ffn_Wg"]))
        w["l2_bgr"] = A(np.asarray(inp["L2_ffn_bg"]).reshape(1, 8))
        w["l2_w1"] = A(np.asarray(inp["L2_ffn_W1"]))
        b1 = np.asarray(inp["L2_ffn_b1"])
        b1c = np.zeros((128, 256), np.float32)
        for e in range(8):
            for j in range(32):
                b1c[:, e * 32 + j] = b1[e, 128 * j:128 * (j + 1)]
        w["l2_b1c"] = b1c
        w["l2_w2"] = A(np.asarray(inp["L2_ffn_W2"]))
        w["l2_b2"] = A(np.asarray(inp["L2_ffn_b2"]))
    return w


def run_layers(x, inputs, layers, debug=None):
    consts = _consts()
    w = _prep_weights(inputs, layers)
    nc = build_program(layers, debug=debug)
    in_maps = []
    for b in range(NCORES):
        m = dict(consts)
        m.update(w)
        m["xin"] = np.ascontiguousarray(np.asarray(x[b]).T)
        in_maps.append(m)
    res = run_bass_kernel_spmd(nc, in_maps, core_ids=list(range(NCORES)))
    xn = np.stack([np.ascontiguousarray(r["xout"].T) for r in res.results])
    return xn, res


def kernel(**inputs):
    x = np.asarray(inputs["L0_attn_x"])
    for li in range(4):
        x, _ = run_layers(x, inputs, [li])
    return x


# version 02 (final)
# speedup vs baseline: 1.0126x; 1.0126x over previous
import numpy as np
import ml_dtypes
from contextlib import ExitStack
import concourse.bass as bass
import concourse.mybir as mybir
from concourse.bass_utils import run_bass_kernel_spmd

F32 = mybir.dt.float32
BF16 = mybir.dt.bfloat16
I32 = mybir.dt.int32
AF = mybir.ActivationFunctionType
ALU = mybir.AluOpType
AX = mybir.AxisListType.X
BF = ml_dtypes.bfloat16

NCORES = 8
T = 2048
D = 1024

ENGINES = ("sync", "gpsimd", "tensor", "vector", "scalar")


class Emitter:

    def __init__(self, nc, ctx):
        self.nc = nc
        self.ctx = ctx
        self.ops = {e: [] for e in ENGINES}
        self.sem = {e: ctx.enter_context(nc.semaphore("S" + e)) for e in ENGINES}
        self.cnt = {e: 0 for e in ENGINES}
        self.lastw = {e: {} for e in ENGINES}
        self.floor_ = {e: 0 for e in ENGINES}
        self.dsems = {}

    def dsem(self, name):
        key = "D:" + name
        if key not in self.dsems:
            self.dsems[key] = self.ctx.enter_context(self.nc.semaphore(key))
            self.cnt[key] = 0
            self.lastw[key] = {}
        return key

    def tok(self, e):
        return (e, self.cnt[e])

    def floor(self):
        for k in list(self.cnt.keys()):
            self.floor_[k] = self.cnt[k]

    def _sem_of(self, key):
        return self.dsems[key] if key.startswith("D:") else self.sem[key]

    def op(self, e, fn, deps=(), inc=1, nofloor=False, semkey=None):
        alld = [d for d in deps if d is not None]
        if not nofloor:
            for de, fv in self.floor_.items():
                if fv > 0:
                    alld.append((de, fv))
        agg = {}
        for (de, dv) in alld:
            if dv <= 0:
                continue
            if de == e and e == "tensor":
                continue
            agg[de] = max(agg.get(de, 0), dv)
        waits = []
        lw = self.lastw[e]
        for de, dv in agg.items():
            if lw.get(de, 0) < dv:
                lw[de] = dv
                waits.append((self._sem_of(de), dv))
        key = semkey if semkey is not None else e
        self.cnt[key] += inc
        self.ops[e].append((waits, fn, self._sem_of(key), inc))
        return (key, self.cnt[key])

    def seq(self, e, fns, deps=()):
        t = None
        for i, fn in enumerate(fns):
            t = self.op(e, fn, deps=(list(deps) if i == 0 else [t]))
        return t

    def dma(self, q, out, in_, deps=(), nofloor=True, sem="misc"):
        return self.op(q, (lambda o, i: (lambda eng: eng.dma_start(out=o, in_=i)))(out, in_),
                       deps=deps, inc=16, nofloor=nofloor, semkey=self.dsem(sem))

    def run(self, blk):
        def mk(e):
            def body(eng):
                for (waits, fn, mysem, inc) in self.ops[e]:
                    for (s, v) in waits:
                        eng.wait_ge(s, v)
                    ins = fn(eng)
                    if inc > 0:
                        ins.then_inc(mysem, inc)
            return body
        blk.sync(mk("sync"))
        blk.gpsimd(mk("gpsimd"))
        blk.tensor(mk("tensor"))
        blk.vector(mk("vector"))
        blk.scalar(mk("scalar"))


class Banks:

    def __init__(self, em, pts):
        self.em = em
        self.pt = pts
        self.w = [None] * 8
        self.r = [{} for _ in range(8)]

    def ap(self, b, c0=0, c1=512):
        return self.pt[b // 2][:, (b % 2) * 512 + c0:(b % 2) * 512 + c1]

    def pair(self, i):
        return self.pt[i][:]

    def wdeps(self, banks):
        d = []
        for b in banks:
            d += list(self.r[b].items())
            if self.w[b] is not None:
                d.append(self.w[b])
        return d

    def rdeps(self, banks):
        return [self.w[b] for b in banks if self.w[b] is not None]

    def setw(self, banks, tokv):
        for b in banks:
            self.w[b] = tokv
            self.r[b] = {}

    def setr(self, banks, tokv):
        for b in banks:
            self.r[b][tokv[0]] = max(self.r[b].get(tokv[0], 0), tokv[1])


def mm_group(em, bk, banks, mms, deps=()):
    alld = list(deps) + bk.wdeps(banks)

    def fn(eng, mms=mms):
        ins = None
        for (o, l, r, st, sp) in mms:
            ins = eng.matmul(o, l, r, start=st, stop=sp)
        return ins
    t = em.op("tensor", fn, deps=alld, inc=1)
    bk.setw(banks, t)
    return t


def tr_group(em, bk, banks, trs, deps=()):
    alld = list(deps) + bk.wdeps(banks)

    def fn(eng, trs=trs):
        ins = None
        for (o, i, idn) in trs:
            ins = eng.transpose(o, i, idn)
        return ins
    t = em.op("tensor", fn, deps=alld, inc=1)
    bk.setw(banks, t)
    return t


def ev(em, e, bk, banks, fn, deps=()):
    t = em.op(e, fn, deps=list(deps) + bk.rdeps(banks), inc=1)
    bk.setr(banks, t)
    return t


def ldw_fm(em, sb_ap, dram_ap, deps=(), sem="wmisc"):
    K_, M_ = dram_ap.shape
    c = K_ // 128
    src = dram_ap.rearrange("(c p) m -> p c m", p=128)
    dst = sb_ap.rearrange("p (c m) -> p c m", c=c)
    return em.dma("gpsimd", dst, src, deps=deps, sem=sem)



def emit_ln(em, bk, B, x_dram, want_rxn):
    XF2, SQ = B["XF2"], B["SQ"]
    pt = bk.pt
    xf2_cons = [[], []]
    sq_cons = None
    for g in range(8):
        slot = g % 2
        xa = XF2[:, slot * T:(slot + 1) * T]
        sqa = SQ[:, 0:T]
        tl = em.dma("sync", xa, x_dram[g * 128:(g + 1) * 128, :], deps=xf2_cons[slot], sem="xf%d" % slot)
        tq = em.op("scalar", (lambda sqa, xa: lambda s: s.activation(sqa, xa, AF.Square))(sqa, xa),
                   deps=[tl, sq_cons])
        t1 = mm_group(em, bk, [0, 1, 2, 3],
                      [(bk.ap(c)[0:1, :], B["onescolf"], xa[:, c * 512:(c + 1) * 512], g == 0, g == 7)
                       for c in range(4)], deps=[tl])
        t2 = mm_group(em, bk, [4, 5, 6, 7],
                      [(bk.ap(4 + c)[0:1, :], B["onescolf"], sqa[:, c * 512:(c + 1) * 512], g == 0, g == 7)
                       for c in range(4)], deps=[tq])
        xf2_cons[slot] = [t1, tq]
        sq_cons = t2
    MR, VR, AR = B["MROW"], B["VROW"], B["AROW"]
    tmean = ev(em, "vector", bk, [0, 1, 2, 3], lambda v: (
        v.tensor_single_scalar(MR[:, 0:1024], pt[0][0:1, :], 1.0 / 1024.0, ALU.mult),
        v.tensor_single_scalar(MR[:, 1024:2048], pt[1][0:1, :], 1.0 / 1024.0, ALU.mult))[-1])
    tmsq = ev(em, "vector", bk, [4, 5, 6, 7], lambda v: (
        v.tensor_single_scalar(VR[:, 0:1024], pt[2][0:1, :], 1.0 / 1024.0, ALU.mult),
        v.tensor_single_scalar(VR[:, 1024:2048], pt[3][0:1, :], 1.0 / 1024.0, ALU.mult))[-1])
    tvar = em.seq("vector", [
        lambda v: v.tensor_tensor(AR, MR, MR, ALU.mult),
        lambda v: v.tensor_tensor(VR, VR, AR, ALU.subtract)], deps=[tmean, tmsq])
    talpha = em.seq("scalar", [
        lambda s: s.activation(AR, VR, AF.Ln, bias=B["CEPS"][0:1, 0:1]),
        lambda s: s.activation(AR, AR, AF.Exp, scale=-0.5)], deps=[tvar])
    tbeta = em.op("vector", lambda v: v.scalar_tensor_tensor(MR, MR, -1.0, AR, ALU.mult, ALU.mult),
                  deps=[talpha])
    trxn = None
    if want_rxn:
        t1 = em.seq("scalar", [
            lambda s: s.activation(VR, VR, AF.Ln),
            lambda s: s.activation(VR, VR, AF.Exp, scale=0.5)], deps=[tvar, talpha])
        trxn = em.seq("vector", [
            lambda v: v.tensor_tensor(VR, VR, AR, ALU.mult),
            lambda v: v.tensor_scalar(VR, VR, 32.0, 1e-12, ALU.mult, ALU.max),
            lambda v: v.reciprocal(VR, VR)], deps=[t1])
    ta = mm_group(em, bk, [0, 1, 2, 3],
                  [(bk.ap(c), B["ones1f"][0:1, :], AR[:, c * 512:(c + 1) * 512], True, True) for c in range(4)],
                  deps=[talpha])
    tb = mm_group(em, bk, [4, 5, 6, 7],
                  [(bk.ap(4 + c), B["ones1f"][0:1, :], MR[:, c * 512:(c + 1) * 512], True, True) for c in range(4)],
                  deps=[tbeta])
    tba = ev(em, "vector", bk, [0, 1, 2, 3], lambda v: (
        v.tensor_copy(B["BCA"][:, 0:1024], pt[0][:]),
        v.tensor_copy(B["BCA"][:, 1024:2048], pt[1][:]))[-1])
    tbb = ev(em, "vector", bk, [4, 5, 6, 7], lambda v: (
        v.tensor_copy(B["BCB"][:, 0:1024], pt[2][:]),
        v.tensor_copy(B["BCB"][:, 1024:2048], pt[3][:]))[-1])
    for g in range(8):
        slot = g % 2
        xa = XF2[:, slot * T:(slot + 1) * T]
        tl = em.dma("sync", xa, x_dram[g * 128:(g + 1) * 128, :],
                    deps=(xf2_cons[slot] or []) + [sq_cons], sem="xf%d" % slot)
        tv = em.seq("vector", [
            (lambda xa: lambda v: v.tensor_tensor(xa, xa, B["BCA"], ALU.mult))(xa),
            (lambda xa, g: lambda v: v.tensor_tensor(B["XNB"][:, g * T:(g + 1) * T], xa, B["BCB"], ALU.add))(xa, g)],
            deps=[tl, tba, tbb])
        xf2_cons[slot] = [tv]
    em.floor()
    return trxn



def emit_l0_routing(em, bk, B, W, rxn_tok):
    pt = bk.pt
    R6 = B["gsc1"][0:1, 8:16]
    RXN = B["VROW"]
    LG, LTM, GTM, GFM = B["LG"], B["LTM"], B["GTM"], B["GFM"]
    tcn = ldw_fm(em, B["CNB"], W["l0_centT"], sem="wcn")
    trt = ldw_fm(em, B["RTB"], W["l0_routeT"], sem="wrt")
    tprev = None
    for g in range(8):
        tq = em.op("scalar", (lambda g: lambda s: s.activation(B["SQ8"], B["CNB"][:, g * 8:(g + 1) * 8], AF.Square))(g),
                   deps=[tcn, tprev])
        tprev = mm_group(em, bk, [0], [(bk.ap(0, 0, 8)[0:1, :], B["onescolf"], B["SQ8"], g == 0, g == 7)],
                         deps=[tq])
    trn = ev(em, "vector", bk, [0], lambda v: v.tensor_copy(R6, pt[0][0:1, 0:8]))
    trn = em.seq("scalar", [
        lambda s: s.activation(R6, R6, AF.Ln),
        lambda s: s.activation(R6, R6, AF.Exp, scale=-0.5)], deps=[trn])
    trn = em.op("vector", lambda v: v.tensor_scalar_min(R6, R6, 1e12), deps=[trn])
    tbc = mm_group(em, bk, [0], [(bk.ap(0, 0, 8), B["ones1f"][0:1, :], R6, True, True)], deps=[trn])
    tnrm = None
    for g in range(8):
        tnrm = ev(em, "vector", bk, [0], (lambda g: lambda v: v.tensor_tensor(
            B["CNB"][:, g * 8:(g + 1) * 8], B["CNB"][:, g * 8:(g + 1) * 8], bk.ap(0, 0, 8), ALU.mult))(g),
            deps=[tcn])
    for c in range(4):
        mm_group(em, bk, [c], [(bk.ap(c)[0:8, :], B["CNB"][:, g * 8:(g + 1) * 8],
                                B["XNB"][:, g * T + c * 512:g * T + (c + 1) * 512], g == 0, g == 7)
                               for g in range(8)], deps=[tnrm])
        mm_group(em, bk, [4 + c], [(bk.ap(4 + c)[0:8, :], B["RTB"][:, g * 8:(g + 1) * 8],
                                    B["XNB"][:, g * T + c * 512:g * T + (c + 1) * 512], g == 0, g == 7)
                                   for g in range(8)], deps=[trt])
    tcos = ev(em, "vector", bk, [0, 1, 2, 3], lambda v: (
        v.tensor_copy(LG[0:8, 0:1024], pt[0][0:8, :]),
        v.tensor_copy(LG[0:8, 1024:2048], pt[1][0:8, :]))[-1])
    trb = mm_group(em, bk, [0, 1, 2, 3],
                   [(bk.ap(c)[0:8, :], B["ones1f"][0:1, 0:8], RXN[0:1, c * 512:(c + 1) * 512], True, True)
                    for c in range(4)], deps=[rxn_tok, tcos])
    t1 = ev(em, "vector", bk, [0, 1, 2, 3], lambda v: (
        v.tensor_tensor(LG[0:8, 0:1024], LG[0:8, 0:1024], pt[0][0:8, :], ALU.mult),
        v.tensor_tensor(LG[0:8, 1024:2048], LG[0:8, 1024:2048], pt[1][0:8, :], ALU.mult))[-1], deps=[tcos])
    t2 = ev(em, "vector", bk, [4, 5, 6, 7], lambda v: (
        v.tensor_tensor(LG[0:8, 0:1024], LG[0:8, 0:1024], pt[2][0:8, :], ALU.add),
        v.tensor_tensor(LG[0:8, 1024:2048], LG[0:8, 1024:2048], pt[3][0:8, :], ALU.add))[-1], deps=[t1])
    for t in range(16):
        b = t % 4
        tr_group(em, bk, [b], [(bk.ap(b, 0, 8), LG[0:8, t * 128:(t + 1) * 128], B["idf"][0:8, 0:8])], deps=[t2])
        ev(em, "vector", bk, [b], (lambda t, b: lambda v: v.tensor_copy(LTM[:, t * 8:(t + 1) * 8], bk.ap(b, 0, 8)))(t, b))
    em.floor()
    C1, C2, C3 = B["gcol1"], B["gcol2"], B["gcol3"]
    ta = None
    for t in range(16):
        lt = LTM[:, t * 8:(t + 1) * 8]
        gt = GTM[:, t * 8:(t + 1) * 8]
        sc1 = B["gsc1"][:, t * 8:(t + 1) * 8]
        tv = em.seq("vector", [
            (lambda lt: lambda v: v.reduce_max(C1, lt, AX))(lt),
            (lambda lt, sc1: lambda v: v.tensor_scalar(sc1, lt, C1, None, ALU.is_ge))(lt, sc1),
            (lambda lt, sc1: lambda v: v.scalar_tensor_tensor(sc1, sc1, -1e30, lt, ALU.mult, ALU.add))(lt, sc1),
            (lambda sc1: lambda v: v.reduce_max(C2, sc1, AX))(sc1),
            lambda v: v.tensor_scalar(C1, C1, -1.0, None, ALU.mult)], deps=[ta])
        ta = em.op("scalar", (lambda lt, sc1: lambda s: s.activation(sc1, lt, AF.Exp, bias=C1))(lt, sc1), deps=[tv])
        em.seq("vector", [
            (lambda lt, gt: lambda v: v.tensor_scalar(gt, lt, C2, None, ALU.is_ge))(lt, gt),
            (lambda gt, sc1: lambda v: v.tensor_tensor(sc1, sc1, gt, ALU.mult))(gt, sc1),
            (lambda sc1: lambda v: v.reduce_sum(C3, sc1, AX))(sc1),
            lambda v: v.reciprocal(C3, C3),
            (lambda gt, sc1: lambda v: v.tensor_scalar(gt, sc1, C3, None, ALU.mult))(gt, sc1)], deps=[ta])
    em.floor()
    for t in range(16):
        b = t % 4
        pb = bk.pair(b // 2).bitcast(BF16)
        out = pb[0:128, (b % 2) * 1024:(b % 2) * 1024 + 8][0:8, :] if False else None
        oap = bk.pt[b // 2].bitcast(BF16)[0:8, (b % 2) * 1024:(b % 2) * 1024 + 128]
        tr_group(em, bk, [b], [(oap, GTM[:, t * 8:(t + 1) * 8], B["idb"])])
        ev(em, "vector", bk, [b], (lambda t, oap: lambda v: v.tensor_copy(
            GFM[0:8, t * 128:(t + 1) * 128], oap))(t, oap))
    em.floor()
    for e in range(8):
        em.dma("sync", B["GROW"][e][0:1, :], GFM[e:e + 1, :], nofloor=False, sem="grow")
    em.floor()


def emit_l0_attn(em, bk, B, W, dump=None):
    pt = bk.pt
    XNB, S, T1B, BIG = B["XNB"], B["S"], B["T1B"], B["BIG"]
    QT = BIG[:, 0:2048]
    KT = BIG[:, 2048:4096]
    ATT = [BIG[:, 4096:6144], BIG[:, 6144:8192]]
    VTM = BIG[:, 12288:12288 + 16 * 130]
    VTM4 = VTM.rearrange("p (t c w) -> p t c w", c=2, w=65)
    YN = S.bitcast(BF16)[:, 0:16384]
    twv = ldw_fm(em, B["WVT"], W["l0_wvT"], sem="wvt")
    for m2 in range(2):
        for c in range(4):
            b = (m2 * 4 + c) % 4
            mm_group(em, bk, [b], [(bk.ap(b), B["WVT"][:, g * 256 + m2 * 128:g * 256 + m2 * 128 + 128],
                                    XNB[:, g * T + c * 512:g * T + (c + 1) * 512], g == 0, g == 7)
                                   for g in range(8)], deps=[twv])
            ev(em, "vector", bk, [b], (lambda m2, c, b: lambda v: v.tensor_copy(
                T1B[:, m2 * 2048 + c * 512:m2 * 2048 + (c + 1) * 512], bk.ap(b)))(m2, c, b))
    em.floor()
    if dump:
        dump("xnbA", B["XNB"])
    tvones = em.op("vector", lambda v: v.memset(VTM4[:, :, :, 64:65], 1.0))
    wus_cons = [None, None]
    tdpk = None
    for c in range(8):
        slot = c % 2
        wq = B["WUS"][:, slot * 768:slot * 768 + 256]
        wk = B["WUS"][:, slot * 768 + 256:slot * 768 + 512]
        wv = B["WUS"][:, slot * 768 + 512:slot * 768 + 768]
        dep = wus_cons[slot] if wus_cons[slot] is not None else em.tok("tensor")
        tq = em.dma("gpsimd", wq.rearrange("p (k m) -> p k m", k=2),
                    W["l0_wuT"][:, 128 * c:128 * (c + 1)].rearrange("(k p) m -> p k m", p=128), deps=[dep],
                    sem="wus%d" % slot)
        tk = em.dma("gpsimd", wk.rearrange("p (k m) -> p k m", k=2),
                    W["l0_wuT"][:, 1024 + 128 * c:1024 + 128 * (c + 1)].rearrange("(k p) m -> p k m", p=128), deps=[dep],
                    sem="wus%d" % slot)
        tv_ = em.dma("gpsimd", wv.rearrange("p (k m) -> p k m", k=2),
                     W["l0_wuT"][:, 2048 + 128 * c:2048 + 128 * (c + 1)].rearrange("(k p) m -> p k m", p=128), deps=[dep],
                     sem="wus%d" % slot)
        for (wqk, dst, tl) in ((wq, QT, tv_), (wk, KT, tv_)):
            for q in range(4):
                b = q % 2
                mm_group(em, bk, [b], [(bk.ap(b), wqk[:, kc * 128:(kc + 1) * 128],
                                        T1B[:, kc * 2048 + q * 512:kc * 2048 + (q + 1) * 512], kc == 0, kc == 1)
                                       for kc in range(2)], deps=[tl])
                ev(em, "vector", bk, [b], (lambda dst, q, b: lambda v: v.tensor_copy(
                    dst[:, q * 512:(q + 1) * 512], bk.ap(b)))(dst, q, b))
        for tt in range(16):
            b = 2 + tt % 2
            mm_group(em, bk, [b], [(bk.ap(b, 0, 128), T1B[:, kc * 2048 + tt * 128:kc * 2048 + (tt + 1) * 128],
                                    wv[:, kc * 128:(kc + 1) * 128], kc == 0, kc == 1)
                                   for kc in range(2)], deps=[tv_, tvones])
            ev(em, "vector", bk, [b], (lambda tt, b: lambda v: v.tensor_copy(
                VTM4[:, tt, :, 0:64], bk.ap(b, 0, 128).rearrange("p (c w) -> p c w", c=2)))(tt, b))
        wus_cons[slot] = em.tok("tensor")
        em.floor()
        for hp in range(2):
            h = 2 * c + hp
            att_cons = [None, None]
            for kt in range(16):
                aslot = kt % 2
                texp = []
                for half in range(2):
                    mm_group(em, bk, [0, 1],
                             [(bk.ap(qq), KT[64 * hp:64 * hp + 64, kt * 128:(kt + 1) * 128],
                               QT[64 * hp:64 * hp + 64, (half * 2 + qq) * 512:(half * 2 + qq + 1) * 512], True, True)
                              for qq in range(2)])
                    texp.append(ev(em, "scalar", bk, [0, 1], (lambda aslot, half: lambda s: s.activation(
                        ATT[aslot][:, half * 1024:(half + 1) * 1024], pt[0][:], AF.Exp, scale=0.125))(aslot, half),
                        deps=[att_cons[aslot]]))
                mm_group(em, bk, [4, 5, 6, 7],
                         [(bk.ap(4 + qq)[0:65, :], VTM[:, kt * 130 + hp * 65:kt * 130 + hp * 65 + 65],
                           ATT[aslot][:, qq * 512:(qq + 1) * 512], kt == 0, kt == 15) for qq in range(4)],
                         deps=texp)
                att_cons[aslot] = em.tok("tensor")
            tdr = ev(em, "vector", bk, [4, 5, 6, 7], lambda v: (
                v.tensor_copy(B["RROW"][0:1, 0:1024], pt[2][64:65, :]),
                v.tensor_copy(B["RROW"][0:1, 1024:2048], pt[3][64:65, :]))[-1], deps=[tdpk])
            tdpk = em.dma("sync", B["DPACK"][h:h + 1, :], B["RROW"][0:1, :], deps=[tdr], nofloor=False, sem="dpk")
            ev(em, "vector", bk, [4, 5, 6, 7], (lambda c, hp: lambda v: (
                v.tensor_copy(YN[64 * hp:64 * hp + 64, c * 2048:c * 2048 + 1024], pt[2][0:64, :]),
                v.tensor_copy(YN[64 * hp:64 * hp + 64, c * 2048 + 1024:c * 2048 + 2048], pt[3][0:64, :]))[-1])(c, hp))
            em.floor()
    if dump:
        dump("xnbB", B["XNB"])
    trc = em.op("vector", lambda v: v.reciprocal(B["DPACK"], B["DPACK"]))
    for c in range(8):
        for q in range(4):
            b = q % 4
            mm_group(em, bk, [b], [(bk.ap(b), B["E16"][:, 128 * c:128 * (c + 1)],
                                    B["DPACK"][:, q * 512:(q + 1) * 512], True, True)], deps=[trc])
            ev(em, "vector", bk, [b], (lambda c, q, b: lambda v: v.tensor_tensor(
                YN[:, c * 2048 + q * 512:c * 2048 + (q + 1) * 512],
                YN[:, c * 2048 + q * 512:c * 2048 + (q + 1) * 512], bk.ap(b), ALU.mult))(c, q, b))
    em.floor()
    if dump:
        dump("xnbC", B["XNB"])
    twov = ldw_fm(em, B["WVT"], W["l0_wovT"], deps=[em.tok("tensor")], sem="wvt")
    for m2 in range(2):
        for c in range(4):
            b = (m2 * 4 + c) % 4
            mm_group(em, bk, [b], [(bk.ap(b), B["WVT"][:, g * 256 + m2 * 128:g * 256 + m2 * 128 + 128],
                                    YN[:, g * 2048 + c * 512:g * 2048 + (c + 1) * 512], g == 0, g == 7)
                                   for g in range(8)], deps=[twov])
            ev(em, "vector", bk, [b], (lambda m2, c, b: lambda v: v.tensor_copy(
                T1B[:, m2 * 2048 + c * 512:m2 * 2048 + (c + 1) * 512], bk.ap(b)))(m2, c, b))
    em.floor()
    twou = ldw_fm(em, B["WOU"], W["l0_wouT"], deps=[em.tok("tensor")], sem="wvt")
    for m in range(8):
        for c in range(4):
            b = (m * 4 + c) % 4
            mm_group(em, bk, [b], [(bk.ap(b), B["WOU"][:, kc * 1024 + m * 128:kc * 1024 + m * 128 + 128],
                                    T1B[:, kc * 2048 + c * 512:kc * 2048 + (c + 1) * 512], kc == 0, kc == 1)
                                   for kc in range(2)], deps=[twou])
            ev(em, "vector", bk, [b], (lambda m, c, b: lambda v: v.tensor_copy(
                S[:, m * 2048 + c * 512:m * 2048 + (c + 1) * 512], bk.ap(b)))(m, c, b))
    em.floor()


def emit_l0_moe_residual(em, bk, B, W, x_dram, xout_dram, dump=None):
    pt = bk.pt
    XNB, S, GH = B["XNB"], B["S"], B["BIG"]
    W1S, W2S = B["W1S"], B["W2S"]
    tb1 = em.dma("gpsimd", B["B1R"][0:1, :], W["l0_b1r"], sem="bia")
    tb2 = em.dma("gpsimd", B["B2"][0:8, :], W["l0_b2"], sem="bia2")
    w1_cons = [None, None]
    w2_cons = [None, None]
    xp_cons = [None, None]
    gev_cons = [None, None]
    for tp in range(4):
        for e in range(8):
            slot = e % 2
            tw1 = em.dma("gpsimd", W1S[slot].rearrange("p (k m) -> p k m", k=8),
                         W["l0_w1"][e].rearrange("(k p) m -> p k m", p=128),
                         deps=[w1_cons[slot], w2_cons[slot]], sem="w1s%d" % slot)
            gbase = (0, 32, 64)[e % 3]
            tgb = mm_group(em, bk, [4 + slot], [(bk.ap(4 + slot), B["ones1b"][gbase:gbase + 1, 0:128],
                                                 B["GROW"][e][0:1, tp * 512:(tp + 1) * 512], True, True)])
            for hc in range(4):
                b = hc % 2
                mms = [(bk.ap(b), W1S[slot][:, kc * 512 + hc * 128:kc * 512 + hc * 128 + 128],
                        XNB[:, kc * T + tp * 512:kc * T + (tp + 1) * 512], kc == 0, False) for kc in range(8)]
                mms.append((bk.ap(b), B["B1R"][0:1, e * 512 + hc * 128:e * 512 + hc * 128 + 128],
                            B["onesrowb"], False, True))
                mm_group(em, bk, [b], mms, deps=[tw1, tb1])
                tg = ev(em, "scalar", bk, [b], (lambda b: lambda s: s.activation(
                    B["GEV"][:, b * 512:(b + 1) * 512], bk.ap(b), AF.Gelu))(b), deps=[gev_cons[b]])
                gev_cons[b] = ev(em, "vector", bk, [4 + slot], (lambda e, hc, b, slot: lambda v: v.tensor_tensor(
                    GH[:, (e * 4 + hc) * 512:(e * 4 + hc + 1) * 512], B["GEV"][:, b * 512:(b + 1) * 512],
                    bk.ap(4 + slot), ALU.mult))(e, hc, b, slot), deps=[tg])
            w1_cons[slot] = em.tok("tensor")
            if getattr(em, "serial_moe", False):
                em.floor()
        em.floor()
        for e in range(8):
            slot = e % 2
            tw2 = em.dma("gpsimd", W2S[slot].rearrange("p (k m) -> p k m", k=4),
                         W["l0_w2"][e].rearrange("(k p) m -> p k m", p=128),
                         deps=[w2_cons[slot], w1_cons[slot]], sem="w2s%d" % slot)
            for dc in range(8):
                mms = [(bk.ap(dc), W2S[slot][:, jc * 1024 + dc * 128:jc * 1024 + dc * 128 + 128],
                        GH[:, (e * 4 + jc) * 512:(e * 4 + jc + 1) * 512], e == 0 and jc == 0, False)
                       for jc in range(4)]
                if e == 7:
                    mms.append((bk.ap(dc), B["B2"][0:8, dc * 128:(dc + 1) * 128],
                                B["GFM"][0:8, tp * 512:(tp + 1) * 512], False, True))
                mm_group(em, bk, [dc], mms, deps=[tw2, tb2])
            w2_cons[slot] = em.tok("tensor")
            if getattr(em, "serial_acc", False) or getattr(em, "serial_moe", False):
                em.floor()
        for dc in range(8):
            slot = dc % 2
            xp = B["XP"][:, slot * 512:(slot + 1) * 512]
            gv = B["GEV"][:, slot * 512:(slot + 1) * 512]
            tx = em.dma("sync", xp, x_dram[dc * 128:(dc + 1) * 128, tp * 512:(tp + 1) * 512], deps=[xp_cons[slot]],
                        sem="xp%d" % slot, nofloor=False)
            tsl = ev(em, "scalar", bk, [dc], (lambda dc, gv: lambda s: s.activation(gv, bk.ap(dc), AF.Silu))(dc, gv),
                     deps=[gev_cons[slot]])
            tv = em.seq("vector", [
                (lambda dc, gv, tp: lambda v: v.tensor_tensor(
                    gv, gv, S[:, dc * 2048 + tp * 512:dc * 2048 + (tp + 1) * 512], ALU.mult))(dc, gv, tp),
                (lambda xp, gv: lambda v: v.tensor_tensor(xp, xp, gv, ALU.add))(xp, gv)], deps=[tx, tsl])
            gev_cons[slot] = tv
            xp_cons[slot] = em.dma("sync", xout_dram[dc * 128:(dc + 1) * 128, tp * 512:(tp + 1) * 512], xp,
                                   deps=[tv], nofloor=False, sem="xo%d" % slot)
            if getattr(em, "serial_moe", False):
                em.floor()
        em.floor()





def emit_l1(em, bk, B, W, x_dram, xout_dram):
    pt = bk.pt
    XNB, S, BIG, SA, SB2 = B["XNB"], B["S"], B["BIG"], B["SMALLA"], B["SMALLB"]
    MR, VR, AR = B["MROW"], B["VROW"], B["AROW"]
    XBB = BIG
    BCR = SB2.bitcast(F32)[:, 0:T]
    WSL = SB2[:, 4096:10240]
    XPRE = SA[:, 0:T]
    ACC = SA[:, T:2 * T]
    RS = SA[:, 0:T]
    IS = SA[:, T:2 * T]
    AS = SA[:, 2 * T:3 * T]
    HYB = [B["W1S"][0][:, 0:2048], B["W1S"][0][:, 2048:4096],
           B["W1S"][1][:, 0:2048], B["W1S"][1][:, 2048:4096]]
    COLS = B["L1C"]
    SPL = B["gsc1"][:, 0:8]
    TM8 = B["gsc1"][:, 8:16]
    n1c = lambda k: COLS[:, k:k + 1]
    n2c = lambda k: COLS[:, 8 + k:9 + k]
    cbc = lambda g: COLS[:, 24 + g:25 + g]
    cwc = lambda g, k: COLS[:, 32 + g * 4 + k:33 + g * 4 + k]

    tcols = em.dma("sync", COLS, W["l1_cols"], sem="cols", nofloor=False)
    lam = COLS[:, 16:24]
    t1 = em.seq("scalar", [
        lambda s: s.activation(TM8, lam, AF.Abs),
        lambda s: s.activation(TM8, TM8, AF.Exp, scale=-1.0),
        lambda s: s.activation(TM8, TM8, AF.Ln, bias=B["CEPS"][:, 2:3])], deps=[tcols])
    tspl = em.seq("vector", [
        lambda v: v.tensor_scalar_max(SPL, lam, 0.0),
        lambda v: v.tensor_tensor(SPL, SPL, TM8, ALU.add),
        lambda v: v.tensor_scalar_mul(SPL, SPL, -8.0)], deps=[t1])
    t2 = em.seq("vector", [
        lambda v: v.tensor_tensor(MR, AR, AR, ALU.mult),
        lambda v: v.tensor_tensor(MR, MR, VR, ALU.mult)])
    t3 = em.seq("scalar", [
        lambda s: s.activation(MR, MR, AF.Ln, bias=B["CEPS"][0:1, 1:2]),
        lambda s: s.activation(MR, MR, AF.Exp, scale=-0.5)], deps=[t2])
    tb = mm_group(em, bk, [0, 1, 2, 3],
                  [(bk.ap(c), B["ones1f"][0:1, :], MR[:, c * 512:(c + 1) * 512], True, True) for c in range(4)],
                  deps=[t3])
    tbc = ev(em, "vector", bk, [0, 1, 2, 3], lambda v: (
        v.tensor_copy(BCR[:, 0:1024], pt[0][:]),
        v.tensor_copy(BCR[:, 1024:2048], pt[1][:]))[-1])
    em.floor()

    wsl_cons = [None, None]
    xpre_cons = None
    acc_cons = None
    for g in range(8):
        slot = g % 2
        wofs = 4096 + slot * 1024
        wx = SB2[:, wofs:wofs + 1024]
        tw = ldw_fm(em, wx, W["l1_wx"][:, 128 * g:128 * (g + 1)], deps=[wsl_cons[slot]], sem="wA%d" % slot)
        tsc = em.seq("vector", [(lambda wx, k: lambda v: v.tensor_scalar_mul(
            wx[:, k * 128:(k + 1) * 128], wx[:, k * 128:(k + 1) * 128], n1c(k)))(wx, k) for k in range(8)],
            deps=[tw, tcols])
        evs = []
        for c in range(4):
            b = c % 4
            mm_group(em, bk, [b], [(bk.ap(b), wx[:, k * 128:(k + 1) * 128],
                                    XNB[:, k * T + c * 512:k * T + (c + 1) * 512], k == 0, k == 7)
                                   for k in range(8)], deps=[tsc])
            evs.append(ev(em, "vector", bk, [b], (lambda c, b: lambda v: v.tensor_tensor(
                XPRE[:, c * 512:(c + 1) * 512], bk.ap(b), BCR[:, c * 512:(c + 1) * 512], ALU.mult))(c, b),
                deps=[tbc, xpre_cons]))
        wsl_cons[slot] = em.tok("tensor")
        tconv = em.seq("vector", [
            (lambda g: lambda v: v.tensor_scalar(ACC, XPRE, cwc(g, 3), cbc(g), ALU.mult, ALU.add))(g),
            (lambda g: lambda v: v.scalar_tensor_tensor(ACC[:, 1:T], XPRE[:, 0:T - 1], cwc(g, 2), ACC[:, 1:T],
                                                        ALU.mult, ALU.add))(g),
            (lambda g: lambda v: v.scalar_tensor_tensor(ACC[:, 2:T], XPRE[:, 0:T - 2], cwc(g, 1), ACC[:, 2:T],
                                                        ALU.mult, ALU.add))(g),
            (lambda g: lambda v: v.scalar_tensor_tensor(ACC[:, 3:T], XPRE[:, 0:T - 3], cwc(g, 0), ACC[:, 3:T],
                                                        ALU.mult, ALU.add))(g),
        ], deps=evs + [acc_cons, tcols])
        xpre_cons = tconv
        acc_cons = em.op("vector", (lambda g: lambda v: v.tensor_copy(BIG[:, g * T:(g + 1) * T], ACC))(g),
                         deps=[tconv])
    em.floor()

    prev_hyb = None
    for half in range(2):
        for gp in range(4 * half, 4 * half + 4):
            twi = ldw_fm(em, WSL[:, 0:1024], W["l1_wi"][:, 128 * gp:128 * (gp + 1)], deps=[em.tok("tensor")], sem="wB0")
            twa = ldw_fm(em, WSL[:, 1024:2048], W["l1_wa"][:, 128 * gp:128 * (gp + 1)], deps=[em.tok("tensor")], sem="wB1")
            twy = ldw_fm(em, WSL[:, 2048:3072], W["l1_wy"][:, 128 * gp:128 * (gp + 1)], deps=[em.tok("tensor")], sem="wB2")
            tscy = em.seq("vector", [(lambda k: lambda v: v.tensor_scalar_mul(
                WSL[:, 2048 + k * 128:2048 + (k + 1) * 128], WSL[:, 2048 + k * 128:2048 + (k + 1) * 128],
                n1c(k)))(k) for k in range(8)], deps=[twy, tcols])
            for c in range(4):
                mm_group(em, bk, [c], [(bk.ap(c), WSL[:, k * 128:(k + 1) * 128],
                                        BIG[:, k * T + c * 512:k * T + (c + 1) * 512], k == 0, k == 7)
                                       for k in range(8)], deps=[twi])
                mm_group(em, bk, [4 + c], [(bk.ap(4 + c), WSL[:, 1024 + k * 128:1024 + (k + 1) * 128],
                                            BIG[:, k * T + c * 512:k * T + (c + 1) * 512], k == 0, k == 7)
                                           for k in range(8)], deps=[twa])
            tr_ = ev(em, "scalar", bk, [0, 1, 2, 3], lambda s: (
                s.activation(RS[:, 0:1024], pt[0][:], AF.Sigmoid),
                s.activation(RS[:, 1024:2048], pt[1][:], AF.Sigmoid))[-1], deps=[prev_hyb])
            ti_ = ev(em, "scalar", bk, [4, 5, 6, 7], lambda s: (
                s.activation(IS[:, 0:1024], pt[2][:], AF.Sigmoid),
                s.activation(IS[:, 1024:2048], pt[3][:], AF.Sigmoid))[-1], deps=[tr_])
            ta_ = em.op("scalar", (lambda gp: lambda s: s.activation(AS, RS, AF.Exp, scale=SPL[:, gp:gp + 1]))(gp),
                        deps=[tr_, ti_, tspl])
            tsq = em.op("scalar", lambda s: s.activation(RS, AS, AF.Square), deps=[ta_])
            tb_ = em.seq("vector", [
                lambda v: v.tensor_scalar(RS, RS, -1.0, 1.0, ALU.mult, ALU.add),
                lambda v: v.tensor_scalar_max(RS, RS, 1e-12)], deps=[tsq])
            tsqr = em.op("scalar", lambda s: s.activation(RS, RS, AF.Sqrt), deps=[tb_])
            tbb2 = em.seq("vector", [
                (lambda gp: lambda v: v.tensor_tensor(IS, IS, BIG[:, gp * T:(gp + 1) * T], ALU.mult))(gp),
                lambda v: v.tensor_tensor(RS, RS, IS, ALU.mult),
                lambda v: v.tensor_tensor_scan(IS, AS, RS, 0.0, ALU.mult, ALU.add)], deps=[tsqr])
            tyb = []
            for c in range(4):
                b = c % 4
                mm_group(em, bk, [b], [(bk.ap(b), WSL[:, 2048 + k * 128:2048 + (k + 1) * 128],
                                        XNB[:, k * T + c * 512:k * T + (c + 1) * 512], k == 0, k == 7)
                                       for k in range(8)], deps=[tscy])
                tyb.append(ev(em, "vector", bk, [b], (lambda c, b: lambda v: v.tensor_tensor(
                    RS[:, c * 512:(c + 1) * 512], bk.ap(b), BCR[:, c * 512:(c + 1) * 512], ALU.mult))(c, b),
                    deps=[tbb2]))
            tg = em.op("scalar", lambda s: s.activation(RS, RS, AF.Gelu_apprx_tanh), deps=tyb)
            prev_hyb = em.op("vector", (lambda gp: lambda v: v.tensor_tensor(HYB[gp % 4], IS, RS, ALU.mult))(gp),
                             deps=[tg])
        em.floor()
        for m in range(8):
            tws = ldw_fm(em, WSL[:, 0:512],
                         W["l1_wo"][512 * half:512 * (half + 1), 128 * m:128 * (m + 1)],
                         deps=[em.tok("tensor")], sem="wB0")
            for c in range(4):
                b = c % 4
                mm_group(em, bk, [b], [(bk.ap(b), WSL[:, kc * 128:(kc + 1) * 128],
                                        HYB[kc][:, c * 512:(c + 1) * 512], kc == 0, kc == 3)
                                       for kc in range(4)], deps=[tws])
                if half == 0:
                    ev(em, "vector", bk, [b], (lambda m, c, b: lambda v: v.tensor_tensor(
                        S[:, m * T + c * 512:m * T + (c + 1) * 512],
                        XNB[:, m * T + c * 512:m * T + (c + 1) * 512], bk.ap(b), ALU.add))(m, c, b))
                else:
                    ev(em, "vector", bk, [b], (lambda m, c, b: lambda v: v.tensor_tensor(
                        S[:, m * T + c * 512:m * T + (c + 1) * 512],
                        S[:, m * T + c * 512:m * T + (c + 1) * 512], bk.ap(b), ALU.add))(m, c, b))
        em.floor()

    SQ2 = SA[:, 0:T]
    BCR2 = SA[:, T:2 * T]
    rowD = SA[0:1, 2 * T:3 * T]
    tq_prev = None
    for g in range(8):
        tq = em.op("scalar", (lambda g: lambda s: s.activation(SQ2, S[:, g * T:(g + 1) * T], AF.Square))(g),
                   deps=[tq_prev])
        tq_prev = mm_group(em, bk, [0, 1, 2, 3],
                           [(bk.ap(c)[0:1, :], B["onescolf"], SQ2[:, c * 512:(c + 1) * 512], g == 0, g == 7)
                            for c in range(4)], deps=[tq])
    trow = ev(em, "vector", bk, [0, 1, 2, 3], lambda v: (
        v.tensor_single_scalar(rowD[:, 0:1024], pt[0][0:1, :], 1.0 / 1024.0, ALU.mult),
        v.tensor_single_scalar(rowD[:, 1024:2048], pt[1][0:1, :], 1.0 / 1024.0, ALU.mult))[-1])
    trow = em.seq("scalar", [
        lambda s: s.activation(rowD, rowD, AF.Ln, bias=B["CEPS"][0:1, 1:2]),
        lambda s: s.activation(rowD, rowD, AF.Exp, scale=-0.5)], deps=[trow])
    tb2 = mm_group(em, bk, [0, 1, 2, 3],
                   [(bk.ap(c), B["ones1f"][0:1, :], rowD[:, c * 512:(c + 1) * 512], True, True) for c in range(4)],
                   deps=[trow])
    ev(em, "vector", bk, [0, 1, 2, 3], lambda v: (
        v.tensor_copy(BCR2[:, 0:1024], pt[0][:]),
        v.tensor_copy(BCR2[:, 1024:2048], pt[1][:]))[-1])
    em.floor()

    UK = B["W1S"][0]
    H2 = BIG
    for tp in range(4):
        wsl_cons = [em.tok("tensor"), em.tok("tensor")]
        tuk = em.seq("vector",
                     [(lambda k, tp: lambda v: v.tensor_tensor(
                         UK[:, k * 512:(k + 1) * 512], S[:, k * T + tp * 512:k * T + tp * 512 + 512],
                         BCR2[:, tp * 512:tp * 512 + 512], ALU.mult))(k, tp) for k in range(8)] +
                     [(lambda k: lambda v: v.tensor_scalar_mul(
                         UK[:, k * 512:(k + 1) * 512], UK[:, k * 512:(k + 1) * 512], n2c(k)))(k) for k in range(8)],
                     deps=[em.tok("tensor"), tcols])
        for j in range(32):
            slot = j % 2
            wofs = 4096 + slot * 2048
            twg = ldw_fm(em, SB2[:, wofs:wofs + 1024], W["l1_wg"][:, 128 * j:128 * (j + 1)],
                         deps=[wsl_cons[slot]], sem="wE%d" % slot)
            twu = ldw_fm(em, SB2[:, wofs + 1024:wofs + 2048], W["l1_wu"][:, 128 * j:128 * (j + 1)],
                         deps=[wsl_cons[slot]], sem="wE%d" % slot)
            mm_group(em, bk, [0], [(bk.ap(0), SB2[:, wofs + k * 128:wofs + (k + 1) * 128],
                                    UK[:, k * 512:(k + 1) * 512], k == 0, k == 7)
                                   for k in range(8)], deps=[twg, twu, tuk])
            mm_group(em, bk, [2], [(bk.ap(2), SB2[:, wofs + 1024 + k * 128:wofs + 1152 + k * 128],
                                    UK[:, k * 512:(k + 1) * 512], k == 0, k == 7)
                                   for k in range(8)], deps=[twg, twu, tuk])
            tgel = ev(em, "scalar", bk, [0], (lambda: lambda s: s.activation(
                B["GEV"][:, 0:512], bk.ap(0), AF.Gelu_apprx_tanh))(), deps=[em.tok("vector")])
            tup = ev(em, "vector", bk, [2], (lambda: lambda v: v.tensor_copy(
                B["GEV"].bitcast(BF16)[:, 1024:1536], bk.ap(2)))(), deps=[tgel])
            wsl_cons[slot] = ev(em, "vector", bk, [0, 1], (lambda j: lambda v: v.tensor_tensor(
                H2[:, j * 512:(j + 1) * 512], B["GEV"][:, 0:512], B["GEV"].bitcast(BF16)[:, 1024:1536],
                ALU.mult))(j), deps=[tgel, tup])
        em.floor()
        wd_cons = [em.tok("tensor"), em.tok("tensor")]
        for j in range(32):
            slot = j % 2
            wofs = 4096 + slot * 1024
            twd = ldw_fm(em, SB2[:, wofs:wofs + 1024], W["l1_wd"][128 * j:128 * (j + 1), :],
                         deps=[wd_cons[slot]], sem="wD%d" % slot)
            for m in range(8):
                mm_group(em, bk, [m], [(bk.ap(m), SB2[:, wofs + m * 128:wofs + (m + 1) * 128],
                                        H2[:, j * 512:(j + 1) * 512], j == 0, j == 31)], deps=[twd])
            wd_cons[slot] = em.tok("tensor")
        for m in range(8):
            ev(em, "vector", bk, [m], (lambda m, tp: lambda v: v.tensor_tensor(
                S[:, m * T + tp * 512:m * T + tp * 512 + 512],
                S[:, m * T + tp * 512:m * T + tp * 512 + 512], bk.ap(m), ALU.add))(m, tp))
        em.floor()

    H3 = BIG
    xp_cons = [None, None]
    gev_cons = None
    for tp in range(4):
        wsl_cons = [em.tok("tensor"), em.tok("tensor")]
        for j in range(24):
            slot = j % 2
            wofs = 4096 + slot * 2048
            twg = em.dma("gpsimd", SB2[:, wofs:wofs + 1024].rearrange("p (k m) -> p k m", k=8),
                         W["l1_wgT"][:, 128 * j:128 * (j + 1)].rearrange("(k p) m -> p k m", p=128),
                         deps=[wsl_cons[slot]], sem="wE%d" % slot)
            twu = em.dma("gpsimd", SB2[:, wofs + 1024:wofs + 2048].rearrange("p (k m) -> p k m", k=8),
                         W["l1_wuT"][:, 128 * j:128 * (j + 1)].rearrange("(k p) m -> p k m", p=128),
                         deps=[wsl_cons[slot]], sem="wE%d" % slot)
            mm_group(em, bk, [0], [(bk.ap(0), SB2[:, wofs + k * 128:wofs + (k + 1) * 128],
                                    XNB[:, k * T + tp * 512:k * T + (tp + 1) * 512], k == 0, k == 7)
                                   for k in range(8)], deps=[twg, twu])
            mm_group(em, bk, [2], [(bk.ap(2), SB2[:, wofs + 1024 + k * 128:wofs + 1152 + k * 128],
                                    XNB[:, k * T + tp * 512:k * T + (tp + 1) * 512], k == 0, k == 7)
                                   for k in range(8)], deps=[twg, twu])
            tsi = ev(em, "scalar", bk, [0], (lambda: lambda s: s.activation(
                B["GEV"][:, 0:512], bk.ap(0), AF.Silu, scale=1.0 / 127.0))(), deps=[em.tok("vector")])
            tup = ev(em, "vector", bk, [2], lambda v: v.tensor_single_scalar(
                B["GEV"].bitcast(BF16)[:, 1024:1536], bk.ap(2), 1.0 / 127.0, ALU.mult), deps=[tsi])
            wsl_cons[slot] = ev(em, "vector", bk, [0, 1], (lambda j: lambda v: v.tensor_tensor(
                H3[:, j * 512:(j + 1) * 512], B["GEV"][:, 0:512], B["GEV"].bitcast(BF16)[:, 1024:1536],
                ALU.mult))(j), deps=[tsi, tup])
        em.floor()
        wd_cons = [em.tok("tensor"), em.tok("tensor")]
        for j in range(24):
            slot = j % 2
            wofs = 4096 + slot * 1024
            twd = em.dma("gpsimd", SB2[:, wofs:wofs + 1024],
                         W["l1_wdT"][128 * j:128 * (j + 1), :],
                         deps=[wd_cons[slot]], sem="wD%d" % slot)
            for m in range(8):
                mm_group(em, bk, [m], [(bk.ap(m), SB2[:, wofs + m * 128:wofs + (m + 1) * 128],
                                        H3[:, j * 512:(j + 1) * 512], j == 0, j == 23)], deps=[twd])
            wd_cons[slot] = em.tok("tensor")
        for m in range(8):
            slot = m % 2
            xp = B["XP"][:, slot * 512:(slot + 1) * 512]
            gv = B["GEV"][:, slot * 512:(slot + 1) * 512]
            tx = em.dma("sync", xp, x_dram[m * 128:(m + 1) * 128, tp * 512:(tp + 1) * 512],
                        deps=[xp_cons[slot]], sem="xp%d" % slot, nofloor=False)
            tsl = ev(em, "scalar", bk, [m], (lambda m, gv: lambda s: s.activation(
                gv, bk.ap(m), AF.Silu, scale=1.0 / 127.0))(m, gv), deps=[gev_cons])
            tv = em.seq("vector", [
                (lambda m, gv, tp: lambda v: v.tensor_tensor(
                    gv, gv, S[:, m * T + tp * 512:m * T + (tp + 1) * 512], ALU.mult))(m, gv, tp),
                (lambda xp, gv: lambda v: v.tensor_tensor(xp, xp, gv, ALU.add))(xp, gv)], deps=[tx, tsl])
            gev_cons = tv
            xp_cons[slot] = em.dma("sync", xout_dram[m * 128:(m + 1) * 128, tp * 512:(tp + 1) * 512], xp,
                                   deps=[tv], nofloor=False, sem="xo%d" % slot)
        em.floor()




def emit_l2(em, bk, B, W, x_dram, xout_dram):
    pt = bk.pt
    XNB, S, BIG, SA, SB2 = B["XNB"], B["S"], B["BIG"], B["SMALLA"], B["SMALLB"]
    RS = SA[:, 0:T]
    IS = SA[:, T:2 * T]
    AS = SA[:, 2 * T:3 * T]
    COLS = B["L1C"]
    SPL = B["gsc1"][:, 0:8]
    TM8 = B["gsc1"][:, 8:16]
    bac = lambda g: COLS[:, g:g + 1]
    bxc = lambda g: COLS[:, 8 + g:9 + g]
    tcols = em.dma("sync", COLS[:, 0:24], W["l2_cols"], sem="cols", nofloor=False)
    lam = COLS[:, 16:24]
    t1 = em.seq("scalar", [
        lambda s: s.activation(TM8, lam, AF.Abs),
        lambda s: s.activation(TM8, TM8, AF.Exp, scale=-1.0),
        lambda s: s.activation(TM8, TM8, AF.Ln, bias=B["CEPS"][:, 2:3])], deps=[tcols])
    tspl = em.seq("vector", [
        lambda v: v.tensor_scalar(SPL, lam, -1.0, 0.0, ALU.mult, ALU.max),
        lambda v: v.tensor_tensor(SPL, SPL, TM8, ALU.add),
        lambda v: v.tensor_scalar_mul(SPL, SPL, -8.0)], deps=[t1])
    em.floor()

    prev_cons = None
    for g in range(8):
        twa = ldw_fm(em, SB2[:, 4096:5120], W["l2_waT"][:, 128 * g:128 * (g + 1)],
                     deps=[em.tok("tensor")], sem="wB0")
        twx = ldw_fm(em, SB2[:, 5120:6144], W["l2_wxT"][:, 128 * g:128 * (g + 1)],
                     deps=[em.tok("tensor")], sem="wB1")
        for c in range(4):
            mm_group(em, bk, [c], [(bk.ap(c), SB2[:, 4096 + k * 128:4096 + (k + 1) * 128],
                                    XNB[:, k * T + c * 512:k * T + (c + 1) * 512], k == 0, k == 7)
                                   for k in range(8)], deps=[twa])
            mm_group(em, bk, [4 + c], [(bk.ap(4 + c), SB2[:, 5120 + k * 128:5120 + (k + 1) * 128],
                                        XNB[:, k * T + c * 512:k * T + (c + 1) * 512], k == 0, k == 7)
                                       for k in range(8)], deps=[twx])
        tr_ = ev(em, "scalar", bk, [0, 1, 2, 3], (lambda g: lambda s: (
            s.activation(RS[:, 0:1024], pt[0][:], AF.Sigmoid, bias=bac(g)),
            s.activation(RS[:, 1024:2048], pt[1][:], AF.Sigmoid, bias=bac(g)))[-1])(g),
            deps=[prev_cons, tcols])
        ti_ = ev(em, "scalar", bk, [4, 5, 6, 7], (lambda g: lambda s: (
            s.activation(IS[:, 0:1024], pt[2][:], AF.Sigmoid, bias=bxc(g)),
            s.activation(IS[:, 1024:2048], pt[3][:], AF.Sigmoid, bias=bxc(g)))[-1])(g), deps=[tr_])
        ta_ = em.op("scalar", (lambda g: lambda s: s.activation(AS, RS, AF.Exp, scale=SPL[:, g:g + 1]))(g),
                    deps=[tr_, ti_, tspl])
        tsq = em.op("scalar", lambda s: s.activation(RS, AS, AF.Square), deps=[ta_])
        tb_ = em.op("vector", lambda v: v.tensor_scalar(RS, RS, -1.0, 1.0, ALU.mult, ALU.add), deps=[tsq])
        tsqr = em.op("scalar", lambda s: s.activation(RS, RS, AF.Sqrt), deps=[tb_])
        prev_cons = em.seq("vector", [
            (lambda g: lambda v: v.tensor_tensor(IS, IS, XNB[:, g * T:(g + 1) * T], ALU.mult))(g),
            lambda v: v.tensor_tensor(RS, RS, IS, ALU.mult),
            (lambda g: lambda v: v.tensor_tensor_scan(S[:, g * T:(g + 1) * T], AS, RS, 0.0, ALU.mult, ALU.add))(g),
        ], deps=[tsqr])
    em.floor()

    LG, LTM, GTM, GFM = B["LG"], B["LTM"], B["GTM"], B["GFM"]
    twg = ldw_fm(em, B["CNB"], W["l2_wg"], sem="wcn")
    tbg = em.dma("gpsimd", B["B1R"][0:1, 0:8], W["l2_bgr"], sem="bia")
    for c in range(4):
        mms = [(bk.ap(c)[0:8, :], B["CNB"][:, k * 8:(k + 1) * 8],
                XNB[:, k * T + c * 512:k * T + (c + 1) * 512], k == 0, False) for k in range(8)]
        mms.append((bk.ap(c)[0:8, :], B["B1R"][0:1, 0:8], B["onesrowb"], False, True))
        mm_group(em, bk, [c], mms, deps=[twg, tbg])
    tlg = ev(em, "vector", bk, [0, 1, 2, 3], lambda v: (
        v.tensor_copy(LG[0:8, 0:1024], pt[0][0:8, :]),
        v.tensor_copy(LG[0:8, 1024:2048], pt[1][0:8, :]))[-1])
    for t in range(16):
        b = t % 4
        tr_group(em, bk, [b], [(bk.ap(b, 0, 8), LG[0:8, t * 128:(t + 1) * 128], B["idf"][0:8, 0:8])], deps=[tlg])
        ev(em, "vector", bk, [b], (lambda t, b: lambda v: v.tensor_copy(LTM[:, t * 8:(t + 1) * 8], bk.ap(b, 0, 8)))(t, b))
    em.floor()
    C1, C2, C3 = B["gcol1"], B["gcol2"], B["gcol3"]
    ta = None
    for t in range(16):
        lt = LTM[:, t * 8:(t + 1) * 8]
        gt = GTM[:, t * 8:(t + 1) * 8]
        sc1 = B["gsc1"][:, t * 8:(t + 1) * 8]
        tv = em.seq("vector", [
            (lambda lt: lambda v: v.reduce_max(C1, lt, AX))(lt),
            (lambda lt, sc1: lambda v: v.tensor_scalar(sc1, lt, C1, None, ALU.is_ge))(lt, sc1),
            (lambda lt, sc1: lambda v: v.scalar_tensor_tensor(sc1, sc1, -1e30, lt, ALU.mult, ALU.add))(lt, sc1),
            (lambda sc1: lambda v: v.reduce_max(C2, sc1, AX))(sc1),
            lambda v: v.tensor_scalar(C1, C1, -1.0, None, ALU.mult)], deps=[ta])
        ta = em.op("scalar", (lambda lt, sc1: lambda s: s.activation(sc1, lt, AF.Exp, bias=C1))(lt, sc1), deps=[tv])
        em.seq("vector", [
            (lambda sc1: lambda v: v.reduce_sum(C3, sc1, AX))(sc1),
            lambda v: v.reciprocal(C3, C3),
            (lambda lt, gt: lambda v: v.tensor_scalar(gt, lt, C2, None, ALU.is_ge))(lt, gt),
            (lambda gt, sc1: lambda v: v.tensor_tensor(sc1, sc1, gt, ALU.mult))(gt, sc1),
            (lambda gt, sc1: lambda v: v.tensor_scalar(gt, sc1, C3, None, ALU.mult))(gt, sc1)], deps=[ta])
    em.floor()
    for t in range(16):
        b = t % 4
        oap = bk.pt[b // 2].bitcast(BF16)[0:8, (b % 2) * 1024:(b % 2) * 1024 + 128]
        tr_group(em, bk, [b], [(oap, GTM[:, t * 8:(t + 1) * 8], B["idb"])])
        ev(em, "vector", bk, [b], (lambda t, oap: lambda v: v.tensor_copy(
            GFM[0:8, t * 128:(t + 1) * 128], oap))(t, oap))
    em.floor()
    for e in range(8):
        em.dma("sync", B["GROW"][e][0:1, :], GFM[e:e + 1, :], nofloor=False, sem="grow")
    tb2 = em.dma("gpsimd", B["B2"][0:8, :], W["l2_b2"], sem="bia2")
    tb1c = em.dma("sync", B["LTM"], W["l2_b1c"][:, 0:128], sem="cols", nofloor=False)
    tb1c2 = em.dma("sync", B["gsc1"], W["l2_b1c"][:, 128:256], sem="cols", nofloor=False)
    em.floor()

    GH = [BIG[:, 0:8192], BIG[:, 8192:16384]]
    w1_cons = [None, None]
    w2_cons = [None, None]
    xp_cons = [None, None]
    gev_cons = [None, None]
    gb_cons = [None, None]
    for tp2 in range(8):
        t0 = tp2 * 256
        for e in range(8):
            eslot = e % 2
            gbase = (0, 32, 64)[e % 3]
            bnk = 6 + eslot
            gbofs = 512 + eslot * 1024
            tgb = mm_group(em, bk, [bnk], [(bk.ap(bnk, 0, 256), B["ones1b"][gbase:gbase + 1, 0:128],
                                            B["GROW"][e][0:1, t0:t0 + 256], True, True)])
            tge = ev(em, "vector", bk, [bnk], (lambda gbofs, bnk: lambda v: v.tensor_copy(
                B["GEV"].bitcast(BF16)[:, gbofs:gbofs + 256], bk.ap(bnk, 0, 256)))(gbofs, bnk),
                deps=[gb_cons[eslot]])
            for jb in range(8):
                slot = jb % 2
                tw1 = em.dma("gpsimd", B["W1S"][slot].rearrange("p (k m) -> p k m", k=8),
                             W["l2_w1"][e][:, 512 * jb:512 * (jb + 1)].rearrange("(k p) m -> p k m", p=128),
                             deps=[w1_cons[slot]], sem="w1s%d" % slot)
                pb = 4 + slot
                for jj in range(4):
                    j = jb * 4 + jj
                    mm_group(em, bk, [pb], [(bk.ap(pb, 0, 256),
                                             B["W1S"][slot][:, k * 512 + jj * 128:k * 512 + jj * 128 + 128],
                                             XNB[:, k * T + t0:k * T + t0 + 256], k == 0, k == 7)
                                            for k in range(8)], deps=[tw1])
                    bcol = B["LTM"] if e < 4 else B["gsc1"]
                    bidx = (e % 4) * 32 + j
                    tge2 = ev(em, "scalar", bk, [pb], (lambda e, j, pb, bcol, bidx: lambda s: s.activation(
                        GH[e % 2][:, j * 256:(j + 1) * 256], bk.ap(pb, 0, 256), AF.Gelu,
                        bias=bcol[:, bidx:bidx + 1]))(e, j, pb, bcol, bidx),
                        deps=[tb1c, tb1c2, em.tok("tensor")])
                gev_cons[eslot] = em.op("vector", (lambda e, jb, gbofs: lambda v: v.tensor_tensor(
                    GH[e % 2].rearrange("p (j m) -> p j m", m=256)[:, jb * 4:(jb + 1) * 4, :],
                    GH[e % 2].rearrange("p (j m) -> p j m", m=256)[:, jb * 4:(jb + 1) * 4, :],
                    B["GEV"].bitcast(BF16)[:, gbofs:gbofs + 256].unsqueeze(1).broadcast_to([128, 4, 256]),
                    ALU.mult))(e, jb, gbofs), deps=[tge, em.tok("scalar")])
                w1_cons[slot] = em.tok("tensor")
            gb_cons[eslot] = gev_cons[eslot]
            for j in range(32):
                slot = j % 2
                wofs = 4096 + slot * 1024
                tw2 = em.dma("gpsimd", SB2[:, wofs:wofs + 1024], W["l2_w2"][e][128 * j:128 * (j + 1), :],
                             deps=[w2_cons[slot]], sem="w2s%d" % slot)
                mms = []
                for dc in range(8):
                    mms.append((bk.ap(dc // 2, (dc % 2) * 256, (dc % 2) * 256 + 256),
                                SB2[:, wofs + dc * 128:wofs + (dc + 1) * 128],
                                GH[e % 2][:, j * 256:(j + 1) * 256],
                                e == 0 and j == 0 and dc % 2 == 0, False))
                mm_group(em, bk, [0, 1, 2, 3], mms, deps=[tw2, gev_cons[e % 2]])
                w2_cons[slot] = em.tok("tensor")
        mmsb = []
        for dc in range(8):
            mmsb.append((bk.ap(dc // 2, (dc % 2) * 256, (dc % 2) * 256 + 256),
                         B["B2"][0:8, dc * 128:(dc + 1) * 128], B["GFM"][0:8, t0:t0 + 256],
                         False, dc % 2 == 1))
        mm_group(em, bk, [0, 1, 2, 3], mmsb, deps=[tb2])
        for dc in range(8):
            slot = dc % 2
            xp = B["XP"][:, slot * 512:slot * 512 + 256]
            gv = B["GEV"][:, slot * 512:slot * 512 + 256]
            tx = em.dma("sync", xp, x_dram[dc * 128:(dc + 1) * 128, t0:t0 + 256], deps=[xp_cons[slot]],
                        sem="xp%d" % slot, nofloor=False)
            tsl = ev(em, "scalar", bk, [dc // 2], (lambda dc, gv: lambda s: s.activation(
                gv, bk.ap(dc // 2, (dc % 2) * 256, (dc % 2) * 256 + 256), AF.Silu))(dc, gv),
                deps=[gev_cons[0], gev_cons[1]])
            tv = em.seq("vector", [
                (lambda dc, gv, t0: lambda v: v.tensor_tensor(
                    gv, gv, S[:, dc * T + t0:dc * T + t0 + 256], ALU.mult))(dc, gv, t0),
                (lambda xp, gv: lambda v: v.tensor_tensor(xp, xp, gv, ALU.add))(xp, gv)], deps=[tx, tsl])
            gev_cons[0] = tv if slot == 0 else gev_cons[0]
            gev_cons[1] = tv if slot == 1 else gev_cons[1]
            xp_cons[slot] = em.dma("sync", xout_dram[dc * 128:(dc + 1) * 128, t0:t0 + 256], xp,
                                   deps=[tv], nofloor=False, sem="xo%d" % slot)
        em.floor()




MSK_ABS = 0x7FFFFFFF
MSK_EXP = 0x7F800000
MSK_RND = -4194304
MSK_SGN = -2147483648


def emit_quant(em, B, H, YA, W2, M, SC, src_tok, dst, nblk, deps=()):
    Wd = 16 * nblk
    Hi, YAi, W2i, Mi = H.bitcast(I32), YA.bitcast(I32), W2.bitcast(I32), M.bitcast(I32)
    SCm = SC[:, 0:nblk]
    RC = SC[:, nblk:2 * nblk]
    bc = lambda a: a.unsqueeze(2).broadcast_to([128, nblk, 16])
    v3 = lambda a: a.rearrange("p (b w) -> p b w", w=16)
    return em.seq("vector", [
        lambda v: v.tensor_single_scalar(YAi, Hi, MSK_ABS, ALU.bitwise_and),
        lambda v: v.tensor_single_scalar(Hi, Hi, MSK_SGN, ALU.bitwise_and),
        lambda v: v.reduce_max(SCm.unsqueeze(2), v3(YA), AX),
        lambda v: v.tensor_scalar(SCm, SCm, 1.0 / 6.0, 1e-12, ALU.mult, ALU.max),
        lambda v: v.reciprocal(RC, SCm),
        lambda v: v.tensor_tensor(v3(YA), v3(YA), bc(RC), ALU.mult),
        lambda v: v.tensor_single_scalar(W2i, YAi, MSK_EXP, ALU.bitwise_and),
        lambda v: v.scalar_tensor_tensor(W2, W2, 0.25, YA, ALU.mult, ALU.add),
        lambda v: v.tensor_single_scalar(W2i, W2i, MSK_RND, ALU.bitwise_and),
        lambda v: v.tensor_scalar_max(W2, W2, 0.5),
        lambda v: v.tensor_single_scalar(M, YA, 0.75, ALU.is_gt),
        lambda v: v.tensor_scalar(M, M, 0.5, 0.5, ALU.mult, ALU.add),
        lambda v: v.tensor_single_scalar(YA, YA, 0.25, ALU.is_gt),
        lambda v: v.tensor_tensor(Hi, Hi, YAi, ALU.bitwise_or),
        lambda v: v.tensor_single_scalar(YA, W2, 0.75, ALU.is_equal),
        lambda v: v.copy_predicated(W2, YAi, M),
        lambda v: v.tensor_tensor(v3(W2), v3(W2), bc(SCm), ALU.mult),
        lambda v: v.tensor_tensor(dst, W2, H, ALU.mult),
    ], deps=list(deps) + [src_tok])


def emit_l3_p0(em, bk, B, W, QW):
    pt = bk.pt
    XNB, SA, SB2 = B["XNB"], B["SMALLA"], B["SMALLB"]
    qwg_d, qwu_d, qwd_d = QW["qwg"], QW["qwu"], QW["qwd"]
    H1 = SA[:, 0:1024]
    YA1 = SA[:, 1024:2048]
    W21 = SA[:, 2 * T:2 * T + 1024]
    M1 = SA[:, 2 * T + 1024:2 * T + 2048]
    SC1 = SA[:, 2048:2176]
    QT1 = SB2[:, 4096:5120]
    STG = SB2[:, 5120:6144]
    ld_cons = None
    stg_cons = None
    for (wdram, qdram) in ((W["l3_wg"], qwg_d), (W["l3_wu"], qwu_d)):
        for r in range(22):
            tl = em.dma("sync", H1, wdram[128 * r:128 * (r + 1), :], deps=[ld_cons], sem="wq", nofloor=False)
            tq = emit_quant(em, B, H1, YA1, W21, M1, SC1, tl, QT1, 64, deps=[stg_cons, em.tok("tensor")])
            ld_cons = tq
            trs = []
            for cc in range(8):
                b = cc % 4
                tr_group(em, bk, [b], [(bk.pt[b // 2].bitcast(BF16)[:, (b % 2) * 1024:(b % 2) * 1024 + 128],
                                        QT1[:, cc * 128:(cc + 1) * 128], B["idb"])], deps=[tq])
                trs.append(ev(em, "vector", bk, [b], (lambda cc, b: lambda v: v.tensor_copy(
                    STG[:, cc * 128:(cc + 1) * 128],
                    bk.pt[b // 2].bitcast(BF16)[:, (b % 2) * 1024:(b % 2) * 1024 + 128]))(cc, b)))
            stg_cons = em.dma("sync", qdram[:, 128 * r:128 * (r + 1)].rearrange("(c p) m -> p c m", p=128),
                              STG.rearrange("p (c m) -> p c m", c=8), deps=trs, sem="wqs", nofloor=False)
    em.floor()
    H2b = SA[:, 0:2816]
    SC2 = SA[:, 2816:2816 + 352]
    YA2 = SA[:, 3328:3328 + 2816]
    XNBF_ = XNB.bitcast(F32)
    W22 = XNBF_[:, 0:2816]
    M2 = XNBF_[:, 2816:5632]
    QT22 = SB2[:, 4096:6912]
    ld_cons = None
    stg_cons = None
    for r in range(8):
        tl = em.dma("sync", H2b, W["l3_wd"][128 * r:128 * (r + 1), :], deps=[ld_cons], sem="wq", nofloor=False)
        tq = emit_quant(em, B, H2b, YA2, W22, M2, SC2, tl, QT22, 176, deps=[stg_cons, em.tok("tensor")])
        ld_cons = tq
        trs = []
        for cc in range(22):
            b = cc % 4
            tr_group(em, bk, [b], [(bk.pt[b // 2].bitcast(BF16)[:, (b % 2) * 1024:(b % 2) * 1024 + 128],
                                    QT22[:, cc * 128:(cc + 1) * 128], B["idb"])], deps=[tq])
            trs.append(ev(em, "vector", bk, [b], (lambda cc, b: lambda v: v.tensor_copy(
                SB2[:, 6912 + cc * 128:6912 + (cc + 1) * 128],
                bk.pt[b // 2].bitcast(BF16)[:, (b % 2) * 1024:(b % 2) * 1024 + 128]))(cc, b)))
        stg_cons = em.dma("sync", qwd_d[:, 128 * r:128 * (r + 1)].rearrange("(c p) m -> p c m", p=128),
                          SB2[:, 6912:6912 + 2816].rearrange("p (c m) -> p c m", c=22), deps=trs,
                          sem="wqs", nofloor=False)
    em.floor()


def emit_l3(em, bk, B, W, QW, x_dram, xout_dram):
    pt = bk.pt
    XNB, S, BIG, SA, SB2 = B["XNB"], B["S"], B["BIG"], B["SMALLA"], B["SMALLB"]
    XNBF = XNB.bitcast(F32)
    YN = BIG
    qwg_d, qwu_d, qwd_d, qh_d, h_d = QW["qwg"], QW["qwu"], QW["qwd"], QW["qh"], QW["hsp"]
    COLS = B["L1C"]
    boc = lambda g: COLS[:, g:g + 1]
    tcols = em.dma("sync", COLS[:, 0:8], W["l3_boc"], sem="cols", nofloor=False)
    H1 = SA[:, 0:1024]
    YA1 = SA[:, 1024:2048]
    W21 = SA[:, 2 * T:2 * T + 1024]
    M1 = SA[:, 2 * T + 1024:2 * T + 2048]
    SC1 = SA[:, 2048:2176]

    QF = BIG
    QA = SA[:, 0:512]
    QB = SA[:, 512:1024]
    tqf = None
    for g in range(8):
        tw = ldw_fm(em, SB2[:, 4096:5120], W["l3_wqT"][:, 128 * g:128 * (g + 1)],
                    deps=[em.tok("tensor")], sem="wB0")
        for c in range(4):
            b = c % 2
            mm_group(em, bk, [b], [(bk.ap(b), SB2[:, 4096 + k * 128:4096 + (k + 1) * 128],
                                    XNB[:, k * T + c * 512:k * T + (c + 1) * 512], k == 0, k == 7)
                                   for k in range(8)], deps=[tw])
            t1 = ev(em, "vector", bk, [b], (lambda b: lambda v: (
                v.tensor_single_scalar(QA, bk.ap(b), 0.0, ALU.min),
                v.tensor_single_scalar(QB, bk.ap(b), 0.0, ALU.max))[-1])(b), deps=[em.tok("scalar"), tqf])
            t2 = em.op("scalar", lambda s: s.activation(QA, QA, AF.Exp), deps=[t1])
            tqf = em.op("vector", (lambda g, c: lambda v: v.tensor_tensor(
                QF[:, g * T + c * 512:g * T + (c + 1) * 512], QA, QB, ALU.add))(g, c), deps=[t2])
    em.floor()
    KF = SB2[:, 8192:8448]
    VA = SB2[:, 8448:8708]
    KVB = SB2[:, 8708:9228]
    VA4 = VA.rearrange("p (h w) -> p h w", w=65)
    tvon = em.op("vector", lambda v: v.memset(VA4[:, :, 64:65], 1.0))
    for q in range(4):
        twk = ldw_fm(em, SB2[:, 4096:6144], W["l3_wkT"][:, 256 * q:256 * (q + 1)],
                     deps=[em.tok("tensor")], sem="wB0")
        twv = ldw_fm(em, SB2[:, 6144:8192], W["l3_wvT"][:, 256 * q:256 * (q + 1)],
                     deps=[em.tok("tensor")], sem="wB1")
        kf_cons = None
        va_cons = tvon
        for tt in range(16):
            mm_group(em, bk, [4], [(bk.ap(4, 0, 256), XNB[:, k * T + tt * 128:k * T + (tt + 1) * 128],
                                    SB2[:, 4096 + k * 256:4096 + (k + 1) * 256], k == 0, k == 7)
                                   for k in range(8)], deps=[twk])
            t1 = ev(em, "vector", bk, [4], (lambda: lambda v: (
                v.tensor_single_scalar(QA[:, 0:256], bk.ap(4, 0, 256), 0.0, ALU.min),
                v.tensor_single_scalar(QB[:, 0:256], bk.ap(4, 0, 256), 0.0, ALU.max))[-1])(),
                deps=[em.tok("scalar"), kf_cons])
            t2 = em.op("scalar", lambda s: s.activation(QA[:, 0:256], QA[:, 0:256], AF.Exp), deps=[t1])
            tkf = em.op("vector", lambda v: v.tensor_tensor(KF, QA[:, 0:256], QB[:, 0:256], ALU.add), deps=[t2])
            mm_group(em, bk, [5], [(bk.ap(5, 0, 256), XNB[:, k * T + tt * 128:k * T + (tt + 1) * 128],
                                    SB2[:, 6144 + k * 256:6144 + (k + 1) * 256], k == 0, k == 7)
                                   for k in range(8)], deps=[twv])
            tva = ev(em, "vector", bk, [5], (lambda: lambda v: v.tensor_copy(
                VA4[:, :, 0:64], bk.ap(5, 0, 256).rearrange("p (h w) -> p h w", w=64)))(), deps=[va_cons])
            mms = []
            for hl in range(4):
                mms.append((bk.ap(q, 128 * hl, 128 * hl + 65)[0:64, :], KF[:, 64 * hl:64 * hl + 64],
                            VA[:, 65 * hl:65 * hl + 65], tt == 0 and hl == 0, tt == 15 and hl == 3))
            tkv = mm_group(em, bk, [q], mms, deps=[tkf, tva])
            kf_cons = tkv
            va_cons = tkv
        for hl in range(4):
            h = 4 * q + hl
            ev(em, "vector", bk, [q], (lambda q, hl, h: lambda v: v.tensor_copy(
                KVB[64 * (h % 2):64 * (h % 2) + 64, 65 * (h // 2):65 * (h // 2) + 65],
                bk.ap(q, 128 * hl, 128 * hl + 65)[0:64, :]))(q, hl, h))
    em.floor()
    tdpk = None
    for h in range(16):
        hp = h % 2
        c = h // 2
        mm_group(em, bk, [4, 5, 6, 7],
                 [(bk.ap(4 + qq)[0:65, :], KVB[64 * hp:64 * hp + 64, 65 * c:65 * c + 65],
                   QF[64 * hp:64 * hp + 64, c * T + qq * 512:c * T + (qq + 1) * 512], True, True)
                  for qq in range(4)])
        tdr = ev(em, "vector", bk, [4, 5, 6, 7], lambda v: (
            v.tensor_copy(B["RROW"][0:1, 0:1024], pt[2][64:65, :]),
            v.tensor_copy(B["RROW"][0:1, 1024:2048], pt[3][64:65, :]))[-1], deps=[tdpk])
        tdpk = em.dma("sync", B["DPACK"][h:h + 1, :], B["RROW"][0:1, :], deps=[tdr], nofloor=False, sem="dpk")
        ev(em, "vector", bk, [4, 5, 6, 7], (lambda c, hp: lambda v: (
            v.tensor_copy(YN[64 * hp:64 * hp + 64, c * 2048:c * 2048 + 1024], pt[2][0:64, :]),
            v.tensor_copy(YN[64 * hp:64 * hp + 64, c * 2048 + 1024:c * 2048 + 2048], pt[3][0:64, :]))[-1])(c, hp))
    em.floor()
    trc = em.seq("vector", [
        lambda v: v.tensor_scalar_add(B["DPACK"], B["DPACK"], 1e-6),
        lambda v: v.reciprocal(B["DPACK"], B["DPACK"])])
    for c in range(8):
        for cq in range(4):
            b = cq % 4
            mm_group(em, bk, [b], [(bk.ap(b), B["E16"][:, 128 * c:128 * (c + 1)],
                                    B["DPACK"][:, cq * 512:(cq + 1) * 512], True, True)], deps=[trc])
            ev(em, "vector", bk, [b], (lambda c, cq, b: lambda v: v.tensor_tensor(
                YN[:, c * 2048 + cq * 512:c * 2048 + (cq + 1) * 512],
                YN[:, c * 2048 + cq * 512:c * 2048 + (cq + 1) * 512], bk.ap(b), ALU.mult))(c, cq, b))
    em.floor()
    for m in range(8):
        tw = ldw_fm(em, SB2[:, 4096:5120], W["l3_woT"][:, 128 * m:128 * (m + 1)],
                    deps=[em.tok("tensor")], sem="wB0")
        for c in range(4):
            b = c % 4
            mm_group(em, bk, [b], [(bk.ap(b), SB2[:, 4096 + k * 128:4096 + (k + 1) * 128],
                                    YN[:, k * 2048 + c * 512:k * 2048 + (c + 1) * 512], k == 0, k == 7)
                                   for k in range(8)], deps=[tw])
            ev(em, "vector", bk, [b], (lambda m, c, b: lambda v: v.tensor_scalar_add(
                S[:, m * T + c * 512:m * T + (c + 1) * 512], bk.ap(b), boc(m)))(m, c, b),
                deps=[tcols])
    em.floor()

    XTMB = SB2[:, 4096:5120]
    QXT = SB2[:, 5120:6144]
    prevq = None
    for tt in range(16):
        trs = []
        for k in range(8):
            b = k % 4
            tr_group(em, bk, [b], [(bk.pt[b // 2].bitcast(BF16)[:, (b % 2) * 1024:(b % 2) * 1024 + 128],
                                    XNB[:, k * T + tt * 128:k * T + (tt + 1) * 128], B["idb"])],
                     deps=[prevq])
            trs.append(ev(em, "vector", bk, [b], (lambda k, b: lambda v: v.tensor_copy(
                XTMB[:, k * 128:(k + 1) * 128],
                bk.pt[b // 2].bitcast(BF16)[:, (b % 2) * 1024:(b % 2) * 1024 + 128]))(k, b)))
        tcp = em.op("vector", lambda v: v.tensor_copy(H1, XTMB), deps=trs)
        tq = emit_quant(em, B, H1, YA1, W21, M1, SC1, tcp, QXT, 64, deps=[em.tok("tensor")])
        trs2 = []
        for k in range(8):
            b = k % 4
            tr_group(em, bk, [b], [(bk.pt[b // 2].bitcast(BF16)[:, (b % 2) * 1024:(b % 2) * 1024 + 128],
                                    QXT[:, k * 128:(k + 1) * 128], B["idb"])], deps=[tq])
            trs2.append(ev(em, "vector", bk, [b], (lambda k, tt, b: lambda v: v.tensor_copy(
                BIG[:, k * T + tt * 128:k * T + (tt + 1) * 128],
                bk.pt[b // 2].bitcast(BF16)[:, (b % 2) * 1024:(b % 2) * 1024 + 128]))(k, tt, b)))
        prevq = trs2[-1]
    em.floor()

    hg = SA[:, 0:256]
    hu = SA[:, 256:512]
    hh = SA[:, 512:768]
    hsp_cons = None
    for nb in range(11):
        twg = em.dma("sync", SB2[:, 4096:6144].rearrange("p (c m) -> p c m", c=8),
                     qwg_d[:, 256 * nb:256 * (nb + 1)].rearrange("(c p) m -> p c m", p=128),
                     deps=[em.tok("tensor")], sem="sqg", nofloor=False)
        twu = em.dma("sync", SB2[:, 6144:8192].rearrange("p (c m) -> p c m", c=8),
                     qwu_d[:, 256 * nb:256 * (nb + 1)].rearrange("(c p) m -> p c m", p=128),
                     deps=[em.tok("tensor")], sem="squ", nofloor=False)
        for tt in range(16):
            bg_ = (tt % 2)
            bu_ = 2 + (tt % 2)
            mm_group(em, bk, [bg_], [(bk.ap(bg_, 0, 256), BIG[:, k * T + tt * 128:k * T + (tt + 1) * 128],
                                      SB2[:, 4096 + k * 256:4096 + (k + 1) * 256], k == 0, k == 7)
                                     for k in range(8)], deps=[twg])
            mm_group(em, bk, [bu_], [(bk.ap(bu_, 0, 256), BIG[:, k * T + tt * 128:k * T + (tt + 1) * 128],
                                      SB2[:, 6144 + k * 256:6144 + (k + 1) * 256], k == 0, k == 7)
                                     for k in range(8)], deps=[twu])
            ts1 = ev(em, "scalar", bk, [bg_], (lambda bg_: lambda s: s.activation(
                hg, bk.ap(bg_, 0, 256), AF.Silu))(bg_), deps=[hsp_cons])
            ts2 = ev(em, "vector", bk, [bu_], (lambda bu_: lambda v: v.tensor_copy(
                hu, bk.ap(bu_, 0, 256)))(bu_), deps=[hsp_cons])
            ts3 = em.op("vector", lambda v: v.tensor_tensor(hh, hg, hu, ALU.mult), deps=[ts1, ts2])
            hsp_cons = em.dma("sync", h_d[tt, :, 256 * nb:256 * (nb + 1)], hh, deps=[ts3],
                              sem="hsp", nofloor=False)
    em.floor()
    H2b = SA[:, 0:2816]
    SC2 = SA[:, 2816:2816 + 352]
    YA2 = SA[:, 3328:3328 + 2816]
    W22 = XNBF[:, 0:2816]
    M2 = XNBF[:, 2816:5632]
    QHT = SB2[:, 4096:6912]
    STG2 = SB2[:, 6912:7168]
    prevq = None
    stg_cons = None
    for tt in range(16):
        tl = em.dma("sync", H2b, h_d[tt], deps=[prevq], sem="wq", nofloor=False)
        tq = emit_quant(em, B, H2b, YA2, W22, M2, SC2, tl, QHT, 176, deps=[stg_cons, em.tok("tensor")])
        prevq = tq
        for nb in range(11):
            trs = []
            for cc in range(2):
                b = (nb * 2 + cc) % 4
                tr_group(em, bk, [b], [(bk.pt[b // 2].bitcast(BF16)[:, (b % 2) * 1024:(b % 2) * 1024 + 128],
                                        QHT[:, nb * 256 + cc * 128:nb * 256 + (cc + 1) * 128], B["idb"])],
                         deps=[tq])
                trs.append(ev(em, "vector", bk, [b], (lambda cc, b: lambda v: v.tensor_copy(
                    STG2[:, cc * 128:(cc + 1) * 128],
                    bk.pt[b // 2].bitcast(BF16)[:, (b % 2) * 1024:(b % 2) * 1024 + 128]))(cc, b),
                    deps=[stg_cons]))
            stg_cons = em.dma("sync", qh_d[256 * nb:256 * (nb + 1), 128 * tt:128 * (tt + 1)]
                              .rearrange("(c p) m -> p c m", p=128),
                              STG2.rearrange("p (c m) -> p c m", c=2), deps=trs, sem="wqs", nofloor=False)
    em.floor()
    xp_cons = [None, None]
    gev_cons = None
    for ttb in range(4):
        tqh = em.dma("sync", BIG[:, 0:22 * 512].rearrange("p (c m) -> p c m", c=22),
                     qh_d[:, 512 * ttb:512 * (ttb + 1)].rearrange("(c p) m -> p c m", p=128),
                     deps=[em.tok("tensor")], sem="sqh", nofloor=False)
        wd_cons = [None, None]
        for nb2 in range(22):
            slot = nb2 % 2
            wofs = 7168 + slot * 1024
            twd = em.dma("sync", SB2[:, wofs:wofs + 1024].rearrange("p (c m) -> p c m", c=8),
                         qwd_d[128 * nb2:128 * (nb2 + 1), :].rearrange("(c p) m -> p c m", p=128),
                         deps=[wd_cons[slot]], sem="sqd%d" % slot, nofloor=False)
            for m in range(8):
                mm_group(em, bk, [m], [(bk.ap(m), SB2[:, wofs + m * 128:wofs + (m + 1) * 128],
                                        BIG[:, nb2 * 512:(nb2 + 1) * 512], nb2 == 0, nb2 == 21)],
                         deps=[twd, tqh])
            wd_cons[slot] = em.tok("tensor")
        for m in range(8):
            slot = m % 2
            xp = B["XP"][:, slot * 512:(slot + 1) * 512]
            gv = B["GEV"][:, slot * 512:(slot + 1) * 512]
            tx = em.dma("sync", xp, x_dram[m * 128:(m + 1) * 128, ttb * 512:(ttb + 1) * 512],
                        deps=[xp_cons[slot]], sem="xp%d" % slot, nofloor=False)
            tsl = ev(em, "scalar", bk, [m], (lambda m, gv: lambda s: s.activation(
                gv, bk.ap(m), AF.Silu))(m, gv), deps=[gev_cons])
            tv = em.seq("vector", [
                (lambda m, gv, ttb: lambda v: v.tensor_tensor(
                    gv, gv, S[:, m * T + ttb * 512:m * T + (ttb + 1) * 512], ALU.mult))(m, gv, ttb),
                (lambda xp, gv: lambda v: v.tensor_tensor(xp, xp, gv, ALU.add))(xp, gv)], deps=[tx, tsl])
            gev_cons = tv
            xp_cons[slot] = em.dma("sync", xout_dram[m * 128:(m + 1) * 128, ttb * 512:(ttb + 1) * 512], xp,
                                   deps=[tv], nofloor=False, sem="xo%d" % slot)
        em.floor()





def build_program(layers, debug=None):
    nc = bass.Bass("TRN2", target_bir_lowering=False, debug=False, dynamic_dma_scratch_size=16384)
    ctx = ExitStack()
    em = Emitter(nc, ctx)

    xin = nc.dram_tensor("xin", [D, T], F32, kind="ExternalInput").ap()
    xout = nc.dram_tensor("xout", [D, T], F32, kind="ExternalOutput").ap()
    W = {}

    def win(name, shape, dt=F32):
        W[name] = nc.dram_tensor(name, shape, dt, kind="ExternalInput").ap()

    win("c_idb", [128, 128], BF16)
    win("c_idf", [8, 8], F32)
    win("c_ones1f", [65, 128], F32)
    win("c_onescolf", [128, 1], F32)
    win("c_ones1b", [65, 512], BF16)
    win("c_E16", [16, 1024], F32)
    if 0 in layers:
        win("l0_wvT", [1024, 256]); win("l0_wuT", [256, 3072])
        win("l0_wovT", [1024, 256]); win("l0_wouT", [256, 1024])
        win("l0_w1", [8, 1024, 512]); win("l0_b1r", [1, 4096])
        win("l0_w2", [8, 512, 1024]); win("l0_b2", [8, 1024])
        win("l0_centT", [1024, 8]); win("l0_routeT", [1024, 8])

    if 1 in layers:
        win("l1_wx", [1024, 1024]); win("l1_wy", [1024, 1024]); win("l1_wi", [1024, 1024])
        win("l1_wa", [1024, 1024]); win("l1_wo", [1024, 1024])
        win("l1_wg", [1024, 4096]); win("l1_wu", [1024, 4096]); win("l1_wd", [4096, 1024])
        win("l1_wgT", [1024, 3072], I32); win("l1_wuT", [1024, 3072], I32); win("l1_wdT", [3072, 1024], I32)
        win("l1_cols", [128, 64])

    if 2 in layers:
        win("l2_waT", [1024, 1024]); win("l2_wxT", [1024, 1024])
        win("l2_cols", [128, 24]); win("l2_wg", [1024, 8]); win("l2_bgr", [1, 8])
        win("l2_w1", [8, 1024, 4096]); win("l2_b1c", [128, 256])
        win("l2_w2", [8, 4096, 1024]); win("l2_b2", [8, 1024])

    if 3 in layers:
        win("l3_wqT", [1024, 1024]); win("l3_wkT", [1024, 1024]); win("l3_wvT", [1024, 1024])
        win("l3_woT", [1024, 1024]); win("l3_boc", [128, 8])
        win("l3_wg", [2816, 1024]); win("l3_wu", [2816, 1024]); win("l3_wd", [1024, 2816])
    QW3 = {}
    if 3 in layers:
        QW3["qwg"] = nc.dram_tensor("qwg_d", [1024, 2816], BF16, kind="Internal").ap()
        QW3["qwu"] = nc.dram_tensor("qwu_d", [1024, 2816], BF16, kind="Internal").ap()
        QW3["qwd"] = nc.dram_tensor("qwd_d", [2816, 1024], BF16, kind="Internal").ap()
        QW3["qh"] = nc.dram_tensor("qh_d", [2816, 2048], BF16, kind="Internal").ap()
        QW3["hsp"] = nc.dram_tensor("h_d", [16, 128, 2816], F32, kind="Internal").ap()

    xmid = [nc.dram_tensor(f"xmid{i}", [D, T], F32, kind="Internal").ap()
            for i in range(max(0, len(layers) - 1))]

    B = {}

    def sb(name, shape, dt):
        B[name] = ctx.enter_context(nc.sbuf_tensor(name, shape, dt))[:]

    sb("XNB", [128, 8 * T], BF16)
    sb("S", [128, 8 * T], F32)
    sb("BIG", [128, 8 * T], BF16)
    sb("W1Sa", [128, 4096], BF16)
    sb("W1Sb", [128, 4096], BF16)
    B["W1S"] = [B["W1Sa"], B["W1Sb"]]
    B["W2S"] = B["W1S"]
    sb("SMALLA", [128, 3 * T], F32)
    sb("SMALLB", [128, 5 * T], BF16)
    sb("WVT", [128, 2048], BF16)
    sb("CNB", [128, 64], BF16)
    sb("RTB", [128, 64], BF16)
    sb("LTM", [128, 16 * 8], F32)
    sb("GTM", [128, 16 * 8], BF16)
    sb("gsc1", [128, 16 * 8], F32)
    B["SQ8"] = B["gsc1"][:, 0:8]
    sb("gcol1", [128, 1], F32)
    sb("gcol2", [128, 1], F32)
    sb("gcol3", [128, 1], F32)
    sb("CEPS", [128, 3], F32)
    sb("L1C", [128, 64], F32)
    sb("idb", [128, 128], BF16)
    sb("E16", [16, 1024], F32)
    sb("idf", [8, 8], F32)
    sb("ones1f", [65, 128], F32)
    sb("onescolf", [128, 1], F32)
    sb("ones1b", [65, 512], BF16)
    B["SQ"] = B["S"][:, 0:T]
    B["BCA"] = B["BIG"].bitcast(F32)[:, 0:T]
    B["BCB"] = B["BIG"].bitcast(F32)[:, T:2 * T]
    B["XF2"] = B["BIG"].bitcast(F32)[:, 2 * T:4 * T]
    B["T1B"] = B["BIG"][:, 8192:12288]
    B["WUS"] = B["WVT"][:, 0:1536]
    B["WOU"] = B["WVT"][:, 0:2048]
    B["GEV"] = B["SMALLA"][:, 0:1024]
    B["XP"] = B["SMALLA"][:, 1024:2048]
    SA, SB2 = B["SMALLA"], B["SMALLB"]
    B["MROW"] = SA[0:1, 0:T]
    B["VROW"] = SA[0:1, T:2 * T]
    B["AROW"] = SA[0:1, 2 * T:3 * T]
    B["RROW"] = SA[0:1, 0:T]
    B["LG"] = SA[0:8, 2 * T:3 * T]
    B["DPACK"] = SA[0:16, T:2 * T]
    B["GROW"] = [SB2[0:1, 0:T], SB2[32:33, 0:T], SB2[64:65, 0:T],
                 SB2[0:1, T:2 * T], SB2[32:33, T:2 * T], SB2[64:65, T:2 * T],
                 SB2[0:1, 2 * T:3 * T], SB2[32:33, 2 * T:3 * T]]
    B["GFM"] = SB2[64:72, 2 * T:3 * T]
    B["B1R"] = SB2[0:1, 3 * T:3 * T + 4096]
    B["B2"] = SB2[64:72, 3 * T:3 * T + 1024]
    B["onesrowb"] = B["ones1b"][0:1, 0:512]

    pts = [ctx.enter_context(nc.psum_tensor(f"pt{i}", [128, 1024], F32))[:] for i in range(4)]
    bk = Banks(em, pts)

    for nm in ("idb", "idf", "ones1f", "onescolf", "ones1b"):
        em.dma("sync", B[nm], W["c_" + nm], sem="const")
    em.dma("sync", B["E16"], W["c_E16"], sem="const")
    em.op("vector", lambda v: (v.memset(B["CEPS"][:, 0:1], 1e-5), v.memset(B["CEPS"][:, 1:2], 1e-6),
                               v.memset(B["CEPS"][:, 2:3], 1.0))[-1])
    em.floor()

    def dump(name, ap):
        if debug is None or name not in debug:
            return
        o = nc.dram_tensor("dbg_" + name, list(ap.shape), ap.dtype, kind="ExternalOutput").ap()
        em.floor()
        em.dma("sync", o, ap, nofloor=False, sem="dump")
        em.floor()

    xcur = xin
    for i, li in enumerate(layers):
        xnext = xout if i == len(layers) - 1 else xmid[i]
        if li == 3:
            emit_l3_p0(em, bk, B, W, QW3)
            emit_ln(em, bk, B, xcur, want_rxn=False)
            emit_l3(em, bk, B, W, QW3, xcur, xnext)
        if li == 2:
            emit_ln(em, bk, B, xcur, want_rxn=False)
            emit_l2(em, bk, B, W, xcur, xnext)
        if li == 1:
            emit_ln(em, bk, B, xcur, want_rxn=False)
            emit_l1(em, bk, B, W, xcur, xnext)
        if li == 0:
            trxn = emit_ln(em, bk, B, xcur, want_rxn=True)
            dump("xnb", B["XNB"])
            emit_l0_routing(em, bk, B, W, trxn)
            dump("gfm", B["GFM"])
            dump("ltm", B["LTM"])
            dump("gtm", B["GTM"])
            dump("xnb3", B["XNB"])
            dump("gsc1", B["gsc1"])
            emit_l0_attn(em, bk, B, W, dump=(dump if debug else None))
            dump("s", B["S"])
            dump("xnb2", B["XNB"])
            emit_l0_moe_residual(em, bk, B, W, xcur, xnext, dump=(dump if debug else None))
            dump("ghlast", B["BIG"])
            dump("smallb", B["SMALLB"])
        xcur = xnext

    taildeps = [(k, em.cnt[k]) for k in list(em.cnt.keys()) if k.startswith("D:")]
    em.op("vector", lambda v: v.engine_nop(), deps=taildeps, nofloor=True)

    blk = ctx.enter_context(nc.Block())
    em.run(blk)
    ctx.close()
    return nc



def _consts():
    c = {}
    c["c_idb"] = np.eye(128, dtype=BF)
    c["c_idf"] = np.eye(8, dtype=np.float32)
    of = np.zeros((65, 128), np.float32); of[0] = 1; of[32] = 1; of[64] = 1
    c["c_ones1f"] = of
    c["c_onescolf"] = np.ones((128, 1), np.float32)
    ob = np.zeros((65, 512), BF); ob[0] = 1; ob[32] = 1; ob[64] = 1
    c["c_ones1b"] = ob
    E = np.zeros((16, 1024), np.float32)
    for p in range(1024):
        E[p // 64, p] = 1.0
    c["c_E16"] = E
    return c


def _prep_weights(inp, layers):
    w = {}
    A = np.ascontiguousarray
    if 0 in layers:
        w["l0_wvT"] = A(np.asarray(inp["L0_attn_W_qkv_v"]).T)
        w["l0_wuT"] = A(np.asarray(inp["L0_attn_W_qkv_u"]).T)
        w["l0_wovT"] = A(np.asarray(inp["L0_attn_W_o_v"]).T)
        w["l0_wouT"] = A(np.asarray(inp["L0_attn_W_o_u"]).T)
        w["l0_w1"] = A(np.asarray(inp["L0_ffn_w1"]))
        w["l0_b1r"] = A(np.asarray(inp["L0_ffn_b1"]).reshape(1, 4096))
        w["l0_w2"] = A(np.asarray(inp["L0_ffn_w2"]))
        w["l0_b2"] = A(np.asarray(inp["L0_ffn_b2"]))
        w["l0_centT"] = A(np.asarray(inp["L0_ffn_centroids"]).T)
        w["l0_routeT"] = A(np.asarray(inp["L0_ffn_w_route"]).T)
    if 1 in layers:
        w["l1_wx"] = A(np.asarray(inp["L1_attn_Wx"]))
        w["l1_wy"] = A(np.asarray(inp["L1_attn_Wy"]))
        w["l1_wi"] = A(np.asarray(inp["L1_attn_Wi"]))
        w["l1_wa"] = A(np.asarray(inp["L1_attn_Wa"]))
        w["l1_wo"] = A(np.asarray(inp["L1_attn_Wo"]))
        w["l1_wg"] = A(np.asarray(inp["L1_attn_Wg"]))
        w["l1_wu"] = A(np.asarray(inp["L1_attn_Wu"]))
        w["l1_wd"] = A(np.asarray(inp["L1_attn_Wd"]))
        w["l1_wgT"] = A(np.asarray(inp["L1_ffn_w_gate"]).T)
        w["l1_wuT"] = A(np.asarray(inp["L1_ffn_w_up"]).T)
        w["l1_wdT"] = A(np.asarray(inp["L1_ffn_w_down"]).T)
        cols = np.zeros((128, 64), np.float32)
        cols[:, 0:8] = np.asarray(inp["L1_attn_norm1_w"]).reshape(8, 128).T
        cols[:, 8:16] = np.asarray(inp["L1_attn_norm2_w"]).reshape(8, 128).T
        cols[:, 16:24] = np.asarray(inp["L1_attn_lam"]).reshape(8, 128).T
        cols[:, 24:32] = np.asarray(inp["L1_attn_conv_b"]).reshape(8, 128).T
        cw = np.asarray(inp["L1_attn_conv_w"])
        for g in range(8):
            for k in range(4):
                cols[:, 32 + g * 4 + k] = cw[k, g * 128:(g + 1) * 128]
        w["l1_cols"] = cols
    if 3 in layers:
        w["l3_wqT"] = A(np.asarray(inp["L3_attn_wq"]).T)
        w["l3_wkT"] = A(np.asarray(inp["L3_attn_wk"]).T)
        w["l3_wvT"] = A(np.asarray(inp["L3_attn_wv"]).T)
        w["l3_woT"] = A(np.asarray(inp["L3_attn_wo"]).T)
        w["l3_boc"] = A(np.asarray(inp["L3_attn_bo"]).reshape(8, 128).T)
        w["l3_wg"] = A(np.asarray(inp["L3_ffn_w_gate"]))
        w["l3_wu"] = A(np.asarray(inp["L3_ffn_w_up"]))
        w["l3_wd"] = A(np.asarray(inp["L3_ffn_w_down"]))
    if 2 in layers:
        w["l2_waT"] = A(np.asarray(inp["L2_attn_Wa"]).T)
        w["l2_wxT"] = A(np.asarray(inp["L2_attn_Wx"]).T)
        cols = np.zeros((128, 24), np.float32)
        cols[:, 0:8] = np.asarray(inp["L2_attn_ba"]).reshape(8, 128).T
        cols[:, 8:16] = np.asarray(inp["L2_attn_bx"]).reshape(8, 128).T
        cols[:, 16:24] = np.asarray(inp["L2_attn_Lam"]).reshape(8, 128).T
        w["l2_cols"] = cols
        w["l2_wg"] = A(np.asarray(inp["L2_ffn_Wg"]))
        w["l2_bgr"] = A(np.asarray(inp["L2_ffn_bg"]).reshape(1, 8))
        w["l2_w1"] = A(np.asarray(inp["L2_ffn_W1"]))
        b1 = np.asarray(inp["L2_ffn_b1"])
        b1c = np.zeros((128, 256), np.float32)
        for e in range(8):
            for j in range(32):
                b1c[:, e * 32 + j] = b1[e, 128 * j:128 * (j + 1)]
        w["l2_b1c"] = b1c
        w["l2_w2"] = A(np.asarray(inp["L2_ffn_W2"]))
        w["l2_b2"] = A(np.asarray(inp["L2_ffn_b2"]))
    return w


def run_layers(x, inputs, layers, debug=None):
    consts = _consts()
    w = _prep_weights(inputs, layers)
    nc = build_program(layers, debug=debug)
    in_maps = []
    for b in range(NCORES):
        m = dict(consts)
        m.update(w)
        m["xin"] = np.ascontiguousarray(np.asarray(x[b]).T)
        in_maps.append(m)
    res = run_bass_kernel_spmd(nc, in_maps, core_ids=list(range(NCORES)))
    xn = np.stack([np.ascontiguousarray(r["xout"].T) for r in res.results])
    return xn, res


def kernel(**inputs):
    x = np.asarray(inputs["L0_attn_x"])
    x, _ = run_layers(x, inputs, [0, 1, 2, 3])
    return x
